# version 18
# baseline (speedup 1.0000x reference)
import contextlib
import numpy as np
import ml_dtypes
import concourse.bass as bass
import concourse.mybir as mybir
from concourse.bass_utils import run_bass_kernel_spmd
from concourse.alu_op_type import AluOpType as Op

F32 = mybir.dt.float32
BF16 = mybir.dt.bfloat16
AF = mybir.ActivationFunctionType

NCORE = 8
D = 1024
KC = 8
T = 2048
TTA = 512
NTA = T // TTA
TTB = 256
NTB = T // TTB
FF = 4096
FFC = 32
NB = 8
NEXP = 18
EG = [list(range(0, 4)), list(range(4, 8)), list(range(8, 12)),
      list(range(12, 16)), [16, 17]]
NHEAD = 16
QKVB_O = 3088
EPS_LN_A = 1e-6
EPS_MOD = 1e-5
EPS_B = 1e-5
LNEPS_B = 1e-5
NEG_LN_EPS_ATTN = float(-np.log(1e-6))
HOIST_V = False
SCALE_B = 1.0 / 8.0
REC1 = float(1.0 / (1.0 + EPS_B))
GELU_T = AF.Gelu_apprx_tanh
GELU_E = AF.Gelu



class DmaStream:
    def __init__(self, nc, es, name, engine="sync", step=16):
        self.name = name
        self.engine = engine
        self.step = step
        self.sem = es.enter_context(nc.semaphore(f"dma_{name}"))
        self.count = 0


class Trk:
    ENGINES = ("tensor", "vector", "scalar", "gpsimd", "sync")

    def __init__(self, nc, es):
        self.nc = nc
        self.sem = {e: es.enter_context(nc.semaphore(f"prog_{e}"))
                    for e in self.ENGINES}
        self.cnt = {e: 0 for e in self.ENGINES}
        self.known = {}
        self.buf = {}
        self.eng = {e: getattr(nc, e) for e in self.ENGINES}

    def _wait(self, engine, dep):
        if dep[0] == "eng":
            _, e, c = dep
            if e == engine and c > self.cnt[e]:
                return
            k = (engine, "e", e)
            if self.known.get(k, 0) >= c:
                return
            self.eng[engine].wait_ge(self.sem[e], c)
            self.known[k] = c
        else:
            _, stream, c = dep
            c = max(c, stream.step * stream.count)
            k = (engine, "d", stream.name)
            if self.known.get(k, 0) >= c:
                return
            self.eng[engine].wait_ge(stream.sem, c)
            self.known[k] = c

    def depend(self, engine, reads=(), writes=()):
        deps = []
        for key in reads:
            b = self.buf.get(key)
            if b and b["w"] is not None:
                deps.append(b["w"])
        for key in writes:
            b = self.buf.get(key)
            if b:
                if b["w"] is not None:
                    deps.append(b["w"])
                deps.extend(b["r"])
        best = {}
        for d in deps:
            k = (d[0], d[1] if d[0] == "eng" else d[1].name)
            if k not in best or best[k][2] < d[2]:
                best[k] = d
        for d in best.values():
            self._wait(engine, d)

    def _record(self, dep, reads, writes):
        for key in writes:
            self.buf[key] = {"w": dep, "r": []}
        for key in reads:
            b = self.buf.setdefault(key, {"w": None, "r": []})
            b["r"].append(dep)

    def op(self, engine, emit, reads=(), writes=(), inc=True):
        self.depend(engine, reads, writes)
        inst = emit(self.eng[engine])
        if inc:
            self.cnt[engine] += 1
            inst.then_inc(self.sem[engine], 1)
            dep = ("eng", engine, self.cnt[engine])
        else:
            dep = ("eng", engine, self.cnt[engine] + 1)
        self._record(dep, reads, writes)
        return inst

    def dma(self, stream, out, in_, reads=(), writes=()):
        self.depend(stream.engine, reads, writes)
        inst = self.eng[stream.engine].dma_start(out, in_)
        stream.count += 1
        inst.then_inc(stream.sem, stream.step)
        self._record(("dma", stream, stream.step * stream.count), reads, writes)
        return inst

    def cc(self, stream, kind, op, replica_groups, ins, outs, reads=(), writes=()):
        self.depend(stream.engine, reads, writes)
        inst = self.eng[stream.engine].collective_compute(
            kind, op, replica_groups=replica_groups, ins=ins, outs=outs)
        stream.count += 1
        inst.then_inc(stream.sem, stream.step)
        self._record(("dma", stream, stream.step * stream.count), reads, writes)
        return inst

    def barrier(self, streams=()):
        for e in self.ENGINES:
            for f in self.ENGINES:
                if f != e and self.cnt[f] > 0:
                    self._wait(e, ("eng", f, self.cnt[f]))
            for s in streams:
                if s.count > 0:
                    self._wait(e, ("dma", s, s.step * s.count))



class ConstPack:
    def __init__(self):
        self.cols = []
        self.index = {}

    def add(self, name, vec):
        vec = np.asarray(vec, dtype=np.float32).reshape(-1)
        assert vec.size % 128 == 0
        n = vec.size // 128
        self.index[name] = (len(self.cols), n)
        arr = vec.reshape(n, 128).T
        for j in range(n):
            self.cols.append(np.ascontiguousarray(arr[:, j]))

    def array(self):
        return np.ascontiguousarray(np.stack(self.cols, axis=1))


def make_const_index():
    idx = {}
    pos = 0

    def add(name, n):
        nonlocal pos
        idx[name] = (pos, n)
        pos += n
    for i in range(NB):
        for nm in ("bq", "bk", "bv", "bo", "bff2", "ln1s", "ln1b", "ln2s", "ln2b"):
            add((nm, i), KC)
        add(("bff1", i), FFC)
    add(("bg",), KC)
    add(("bb",), KC)
    return idx, pos


CINDEX, NCONST = make_const_index()



def build_program(dump=(), upto=None, nblocks=NB, no_cc=False):
    nc = bass.Bass("TRN2", target_bir_lowering=False)
    es = contextlib.ExitStack()

    def din(name, shape):
        return nc.dram_tensor(name, list(shape), F32, kind="ExternalInput")

    lean = (upto == "j0")
    x0 = din("x0", (D, T))
    if not lean:
        wq_cb = din("wq_cb", (NB, KC, D, 128))
        wk_cb = din("wk_cb", (NB, KC, D, 128))
        wv_cb = din("wv_cb", (NB, KC, D, 128))
        wo_w = din("wo_w", (NB, D, D))
        w1_w = din("w1_w", (NB, D, FF))
        w2_w = din("w2_w", (NB, FF, D))
        router_t = din("router_t", (D, 16))
        up_t = din("up_t", (NEXP, D, 512))
        down_t = din("down_t", (NEXP, 512, D))
        wqkvb_t = din("wqkvb_t", (D, QKVB_O))
        wob_t = din("wob_t", (D, D))
        consts_in = din("consts", (128, NCONST))
    out_t = nc.dram_tensor("out", [D, T], F32, kind="ExternalOutput")

    def scrt(name, shape, dt):
        kind = "ExternalOutput" if name in dump else "Internal"
        return nc.dram_tensor(name, list(shape), dt, kind=kind)

    scr_stream = scrt("scr_stream", (D, T), BF16)
    scr_af32 = scrt("scr_af32", (D, T), F32)
    scr_q = scrt("scr_q", (D, T), BF16)
    scr_x1 = scrt("scr_x1", (D, T), F32)
    scr_x2 = scrt("scr_x2", (D, T), F32)
    scr_x3 = scrt("scr_x3", (D, T), F32)
    scr_n1 = scrt("scr_n1", (D, T), BF16)
    scr_n2 = scrt("scr_n2", (D, T), BF16)
    scr_n3 = scrt("scr_n3", (D, T), BF16)
    scr_w1c = nc.dram_tensor("scr_w1c", [2, 128, KC * FF], BF16)
    scr_w2c = nc.dram_tensor("scr_w2c", [2, 128, FFC * D], BF16)
    scr_woc = nc.dram_tensor("scr_woc", [2, 128, KC * D], BF16)
    scr_qkvc = nc.dram_tensor("scr_qkvc", [NB, 3, KC, 128, KC * 128], BF16)
    scr_wupc = nc.dram_tensor("scr_wupc", [NEXP, 128, KC * 512], BF16)
    scr_wdnc = nc.dram_tensor("scr_wdnc", [NEXP, 128, 4 * D], BF16)
    scr_qkvbc = nc.dram_tensor("scr_qkvbc", [128, KC * QKVB_O], BF16)
    scr_wobc = nc.dram_tensor("scr_wobc", [128, KC * D], BF16)
    cc_in = nc.dram_tensor("cc_in", [128, 32], F32)
    cc_out = nc.dram_tensor("cc_out", [128, 32], F32)

    def fmview(t_):
        return t_.ap().rearrange("(kc p) t -> p kc t", p=128)

    x0_v = fmview(x0)
    strm_v = fmview(scr_stream)
    af32_v = fmview(scr_af32)
    q_v = fmview(scr_q)
    x1_v = fmview(scr_x1)
    x2_v = fmview(scr_x2)
    x3_v = fmview(scr_x3)
    n1_v = fmview(scr_n1)
    n2_v = fmview(scr_n2)
    n3_v = fmview(scr_n3)
    out_v = fmview(out_t)

    sb = lambda name, shape, dt: es.enter_context(
        nc.sbuf_tensor(name, list(shape), dt))
    ones_col = sb("ones_col", (128, 1), F32)
    ones_row = sb("ones_row", (1, 128), F32)
    ones_col_bf = sb("ones_col_bf", (128, 1), BF16)
    wst2 = [sb(f"wst2_{i}", (128, 256), F32) for i in range(4)]
    wbst = [sb(f"wbst{i}", (128, 256), BF16) for i in range(4)]
    fconst = sb("fconst", (128, 4), F32)

    t = Trk(nc, es)
    ws1 = DmaStream(nc, es, "ws1", "sync")
    ws2 = DmaStream(nc, es, "ws2", "sync")
    wso = DmaStream(nc, es, "wso", "sync")
    wcb = DmaStream(nc, es, "wcb", "scalar")
    wmo = DmaStream(nc, es, "wmo", "sync")
    wci = DmaStream(nc, es, "wci", "sync")
    wco = DmaStream(nc, es, "wco", "gpsimd")
    lx = DmaStream(nc, es, "lx", "scalar")
    sx = DmaStream(nc, es, "sx", "scalar")
    ccs = DmaStream(nc, es, "ccs", "gpsimd", step=1)
    STREAMS = [ws1, ws2, wso, wcb, wmo, wci, wco, lx, sx, ccs]

    with nc.Block():
        t.op("vector", lambda v: v.memset(ones_col[:], 1.0), writes=[("ones_col",)])
        t.op("vector", lambda v: v.memset(ones_row[:], 1.0), writes=[("ones_row",)])
        t.op("vector", lambda v: v.memset(ones_col_bf[:], 1.0), writes=[("ones_cb",)])
        for i, val in enumerate((EPS_MOD, EPS_LN_A, NEG_LN_EPS_ATTN)):
            t.op("vector", lambda v, i=i, val=val: v.memset(
                fconst[:, i:i + 1], val), writes=[("fconst",)])
            nc.const_aps.register(F32, val, fconst[:, i:i + 1])

        def cast(dst_ap, src_ap, reads, writes, engine="gpsimd"):
            if engine == "scalar":
                t.op(engine, lambda e: e.activation(dst_ap, src_ap, AF.Copy),
                     reads=reads, writes=writes)
            else:
                t.op(engine, lambda e: e.tensor_copy(dst_ap, src_ap),
                     reads=reads, writes=writes)

        cast_rr = [0]
        CAST_ENGS = ("gpsimd",)

        def load_cast2(stream, dram_ap, dst_ap, dkey, slot, width):
            st = wst2[slot % 4]
            skey = ("wst2", slot % 4)
            t.dma(stream, st[:, 0:width], dram_ap, writes=[skey])
            eng = CAST_ENGS[cast_rr[0] % len(CAST_ENGS)]
            cast_rr[0] += 1
            cast(dst_ap, st[:, 0:width], reads=[skey], writes=[dkey], engine=eng)

        conv_queue = []
        conv_rr = [0]
        refilled_w = [-1]

        def conv_pump(n):
            done = 0
            i = 0
            while done < n and i < len(conv_queue):
                gate, src_ap, dst_ap, dkey, width = conv_queue[i]
                if gate is not None and gate > refilled_w[0] + 2:
                    i += 1
                    continue
                conv_queue.pop(i)
                for o in range(0, width, 256):
                    w = min(256, width - o)
                    sh = src_ap.shape
                    if len(sh) == 3:
                        K = sh[2]
                        if K <= 256:
                            sap = src_ap[:, o // K:(o + w) // K, :]
                        else:
                            sap = src_ap[:, o // K:o // K + 1,
                                         (o % K):(o % K) + w]
                    else:
                        sap = src_ap[:, o:o + w]
                    slot = conv_rr[0] % 4
                    conv_rr[0] += 1
                    t.dma(wci, wst2[slot][:, 0:w], sap,
                          writes=[("wst2", slot)])
                    eng = CAST_ENGS[cast_rr[0] % len(CAST_ENGS)]
                    cast_rr[0] += 1
                    cast(wbst[slot][:, 0:w], wst2[slot][:, 0:w],
                         reads=[("wst2", slot)], writes=[("wbst", slot)],
                         engine=eng)
                    t.dma(wco, dst_ap[:, o:o + w], wbst[slot][:, 0:w],
                          reads=[("wbst", slot)], writes=[dkey])
                done += 1

        def ln_stats(y_aps, ykeys, W, eps, pbs, pbs_keys, srow, srow_keys,
                     sq_buf, sq_key0, sq_full=None, y_full=None):
            ksum, ksq, kmu_b, kr_b = pbs_keys
            kmu, kr = srow_keys[0], srow_keys[1]
            if sq_full is not None:
                fullap, fullkeys, sq_chunk0 = sq_full
                t.op("scalar", lambda s: s.activation(
                    fullap, y_full, AF.Square),
                    reads=ykeys, writes=fullkeys)
                sq_is_bf = True
                sq_chunk = sq_chunk0
                sq_key = lambda c: fullkeys[c]
            else:
                sq_is_bf = False
                sq_chunk = lambda c: sq_buf
                sq_key = lambda c: sq_key0
            for c in range(KC):
                if sq_full is None:
                    t.op("scalar", lambda s, c=c: s.activation(
                        sq_buf, y_aps[c], AF.Square),
                        reads=[ykeys[c]], writes=[sq_key0])
                t.op("tensor", lambda te, c=c: te.matmul(
                    pbs[0][0:1, 0:W], ones_col[:, 0:1], y_aps[c],
                    start=(c == 0), stop=(c == KC - 1)),
                    reads=[ykeys[c], ("ones_col",)], writes=[ksum])
                ones_sq = ones_col_bf if sq_is_bf else ones_col
                okey = ("ones_cb",) if sq_is_bf else ("ones_col",)
                t.op("tensor", lambda te, c=c, ones_sq=ones_sq: te.matmul(
                    pbs[1][0:1, 0:W], ones_sq[:, 0:1], sq_chunk(c),
                    start=(c == 0), stop=(c == KC - 1)),
                    reads=[sq_key(c), okey], writes=[ksq])
            mu = srow[0:1, 0:W]
            sc = srow[0:1, W:2 * W]
            rq = srow[0:1, 2 * W:3 * W]
            tq = srow[0:1, 3 * W:4 * W]
            I32 = mybir.dt.int32
            MAGIC = 0x5F3759DF
            t.op("vector", lambda v: v.tensor_scalar(
                mu, pbs[0][0:1, 0:W], 1.0 / D, None, op0=Op.mult),
                reads=[ksum], writes=[kmu])
            t.op("vector", lambda v: v.tensor_tensor(sc, mu, mu, op=Op.mult),
                 reads=[kmu], writes=[kr])
            t.op("vector", lambda v: v.scalar_tensor_tensor(
                sc, pbs[1][0:1, 0:W], 1.0 / D, sc, op0=Op.mult, op1=Op.subtract),
                reads=[ksq, kr], writes=[kr])
            t.op("vector", lambda v: v.tensor_scalar(
                sc, sc, eps, None, op0=Op.add), reads=[kr], writes=[kr])
            t.op("vector", lambda v: v.tensor_scalar(
                rq.bitcast(I32), sc.bitcast(I32), 1, None,
                op0=Op.arith_shift_right), reads=[kr], writes=[kr])
            t.op("vector", lambda v: v.tensor_scalar(
                rq.bitcast(I32), rq.bitcast(I32), -1, MAGIC,
                op0=Op.mult, op1=Op.add), reads=[kr], writes=[kr])
            for _ in range(2):
                t.op("vector", lambda v: v.tensor_tensor(tq, rq, rq, op=Op.mult),
                     reads=[kr], writes=[kr])
                t.op("vector", lambda v: v.tensor_tensor(tq, tq, sc, op=Op.mult),
                     reads=[kr], writes=[kr])
                t.op("vector", lambda v: v.tensor_scalar(
                    tq, tq, -0.5, 1.5, op0=Op.mult, op1=Op.add),
                    reads=[kr], writes=[kr])
                t.op("vector", lambda v: v.tensor_tensor(rq, rq, tq, op=Op.mult),
                     reads=[kr], writes=[kr])
            t.op("tensor", lambda te: te.matmul(
                pbs[2][:, 0:W], ones_row[0:1, :], mu, start=True, stop=True),
                reads=[kmu, ("ones_row",)], writes=[kmu_b])
            t.op("tensor", lambda te: te.matmul(
                pbs[3][:, 0:W], ones_row[0:1, :], rq, start=True, stop=True),
                reads=[kr, ("ones_row",)], writes=[kr_b])

        def ln_junction(tag, xin_v, xin_pref, add_v, add_pref, xsum_v, xsum_pref,
                        nout_v, nout_pref, eps):
            with contextlib.ExitStack() as seg:
                jb = seg.enter_context(nc.sbuf_tensor(f"jb_{tag}", [128, KC * TTA], F32))
                pool = [seg.enter_context(nc.sbuf_tensor(f"jpool_{tag}{i}", [128, TTA], F32))
                        for i in range(4)]
                srow = seg.enter_context(nc.sbuf_tensor(f"jsrow_{tag}", [1, 2048], F32))
                ob = [seg.enter_context(nc.sbuf_tensor(f"job_{tag}{i}", [128, TTA], BF16))
                      for i in range(2)]
                pbs = [seg.enter_context(nc.psum_tensor(f"jps_{tag}{i}", [128, 512], F32))
                       for i in range(4)]
                pk = [("jps", i) for i in range(4)]
                sk = [("jsrow", i) for i in range(4)]
                for ti in range(NTA):
                    sl = slice(ti * TTA, (ti + 1) * TTA)
                    for c in range(KC):
                        ysl = jb[:, c * TTA:(c + 1) * TTA]
                        if add_v is None:
                            t.dma(lx, ysl, xin_v[:, c, sl],
                                  reads=[(xin_pref, c, 2 * ti + h) for h in range(2)]
                                  if xin_pref else [],
                                  writes=[("jb", c)])
                        else:
                            t.dma(lx, pool[c % 2][:], xin_v[:, c, sl],
                                  reads=[(xin_pref, c, 2 * ti + h) for h in range(2)]
                                  if xin_pref else [],
                                  writes=[("jpool", c % 2)])
                            t.dma(lx, pool[2 + c % 2][:], add_v[:, c, sl],
                                  reads=[(add_pref, c, 2 * ti + h) for h in range(2)]
                                  if add_pref else [],
                                  writes=[("jpool", 2 + c % 2)])
                            t.op("vector", lambda v, c=c: v.tensor_tensor(
                                jb[:, c * TTA:(c + 1) * TTA], pool[c % 2][:],
                                pool[2 + c % 2][:], op=Op.add),
                                reads=[("jpool", c % 2), ("jpool", 2 + c % 2)],
                                writes=[("jb", c)])
                        if xsum_v is not None:
                            t.dma(sx, xsum_v[:, c, sl], jb[:, c * TTA:(c + 1) * TTA],
                                  reads=[("jb", c)],
                                  writes=[(xsum_pref, c, 2 * ti + h) for h in range(2)])
                    y_aps = [jb[:, c * TTA:(c + 1) * TTA] for c in range(KC)]
                    ykeys = [("jb", c) for c in range(KC)]
                    ln_stats(y_aps, ykeys, TTA, eps, pbs, pk, srow, sk,
                             pool[3][:], ("jpool", 3))
                    for c in range(KC):
                        t.op("vector", lambda v, c=c: v.scalar_tensor_tensor(
                            pool[c % 2][:], y_aps[c], 0.0, pbs[2][:, 0:TTA],
                            op0=Op.bypass, op1=Op.subtract),
                            reads=[ykeys[c], pk[2]], writes=[("jpool", c % 2)])
                        t.op("vector", lambda v, c=c: v.scalar_tensor_tensor(
                            ob[c % 2][:], pool[c % 2][:], 0.0, pbs[3][:, 0:TTA],
                            op0=Op.bypass, op1=Op.mult),
                            reads=[("jpool", c % 2), pk[3]], writes=[("job", c % 2)])
                        t.dma(sx, nout_v[:, c, sl], ob[c % 2][:],
                              reads=[("job", c % 2)],
                              writes=[(nout_pref, c, 2 * ti + h) for h in range(2)])
                t.barrier(STREAMS)

        ln_junction("j0", x0_v, None, None, None, None, None, strm_v, "strm", EPS_MOD)
        stop_now = (upto == "j0")

        if not stop_now:
         with contextlib.ExitStack() as seg:
            ssb = lambda name, shape, dt: seg.enter_context(
                nc.sbuf_tensor(name, list(shape), dt))
            w1_bf = ssb("w1_bf", (128, KC * FF), BF16)
            w2_bf = ssb("w2_bf", (128, FFC * D), BF16)
            wo_bf = ssb("wo_bf", (128, KC * D), BF16)
            wcb_bf = [ssb(f"wcb_bf{i}", (128, KC * 128), BF16) for i in range(2)]
            xa_bf = [ssb(f"xa_bf{i}", (128, KC * TTA), BF16) for i in range(1)]
            k_sc = ssb("k_sc", (128, TTA), F32)
            pool = [ssb(f"apool{i}", (128, TTA), F32) for i in range(2)]
            bfs = ssb("bfs", (128, TTA), BF16)
            kv_acc = ssb("kv_acc", (128, 32), F32)
            ks_acc = ssb("ks_acc", (128, 32), F32)
            kvks = ssb("kvks", (128, 32), F32)
            kvred = ssb("kvred", (128, 32), F32)
            ksr = ssb("ksr", (128, 16), F32)
            kvdks = ssb("kvdks", (128, 16), F32)
            y1 = ssb("y1", (128, KC * TTB), F32)
            xb_bf = ssb("xb_bf", (128, KC * TTB), BF16)
            qb_bf = ssb("qb_bf", (128, KC * TTB), BF16)
            h_bf = ssb("h_bf", (128, FFC * TTB), BF16)
            srow = ssb("asrow", (1, 1024), F32)
            cab = [ssb(f"cab{i}", (128, 104), F32) for i in range(2)]
            pb = [seg.enter_context(nc.psum_tensor(f"apb{i}", [128, 512], F32))
                  for i in range(8)]
            pbk = [("apb", i) for i in range(8)]
            srk = [("asrow", 0), ("asrow", 1)]
            pA, pB = pool[0][:, 0:TTB], pool[0][:, TTB:2 * TTB]
            pC, pD = pool[1][:, 0:TTB], pool[1][:, TTB:2 * TTB]
            kA, kB, kC, kD = ("ap0a",), ("ap0b",), ("ap1a",), ("ap1b",)
            k0full = [kA, kB]
            k1full = [kC, kD]
            LOCAL = {"bq": 0, "bk": 8, "bv": 16, "bo": 24, "bff2": 32,
                     "ln1s": 40, "ln1b": 48, "ln2s": 56, "ln2b": 64, "bff1": 72}

            def load_cab(blk):
                t.dma(wmo, cab[blk % 2][:],
                      consts_in.ap()[:, blk * 104:(blk + 1) * 104],
                      writes=[("cab", blk % 2)])

            WQKV = {"q": wq_cb, "k": wk_cb, "v": wv_cb}
            WQI = {"q": 0, "k": 1, "v": 2}

            def enq_conv_block(blk):
                for which in ("k", "v", "q"):
                    wi = WQI[which]
                    for mc in range(KC):
                        ap = WQKV[which].ap()[blk, mc].rearrange(
                            "(kc p) c -> p kc c", p=128)
                        for hf in range(2):
                            conv_queue.append((
                                None,
                                ap[:, 4 * hf:4 * hf + 4, :],
                                scr_qkvc.ap()[blk, wi, mc][:, hf * 512:(hf + 1) * 512],
                                ("cqkv", blk, wi, mc, hf), 512))
                for kc in range(KC):
                    ap = wo_w.ap()[blk, kc * 128:(kc + 1) * 128, :]
                    for hf in range(2):
                        conv_queue.append((
                            blk,
                            ap[:, hf * 512:(hf + 1) * 512],
                            scr_woc.ap()[blk % 2][:, kc * D + hf * 512:kc * D + (hf + 1) * 512],
                            ("cwo", blk % 2, 2 * kc + hf), 512))
                for kc in range(KC):
                    for q in range(8):
                        conv_queue.append((
                            blk,
                            w1_w.ap()[blk, kc * 128:(kc + 1) * 128, q * 512:(q + 1) * 512],
                            scr_w1c.ap()[blk % 2][:, kc * FF + q * 512:kc * FF + (q + 1) * 512],
                            ("cw1", blk % 2, kc * 8 + q), 512))
                for kh in range(FFC):
                    for hf in range(2):
                        conv_queue.append((
                            blk,
                            w2_w.ap()[blk, kh * 128:(kh + 1) * 128, hf * 512:(hf + 1) * 512],
                            scr_w2c.ap()[blk % 2][:, kh * D + hf * 512:kh * D + (hf + 1) * 512],
                            ("cw2", blk % 2, 2 * kh + hf), 512))

            def refill_qkv(which, blk, mc, slot):
                wi = WQI[which]
                t.dma(wcb, wcb_bf[slot][:], scr_qkvc.ap()[blk, wi, mc],
                      reads=[("cqkv", blk, wi, mc, hf) for hf in range(2)],
                      writes=[("wcb", slot, hf) for hf in range(4)])

            def refill_wo(blk):
                t.dma(wso, wo_bf[:], scr_woc.ap()[blk % 2],
                      reads=[("cwo", blk % 2, i) for i in range(16)],
                      writes=[("wo", kc, hf) for kc in range(KC) for hf in range(2)])

            def refill_w1(blk):
                t.dma(ws1, w1_bf[:], scr_w1c.ap()[blk % 2],
                      reads=[("cw1", blk % 2, i) for i in range(64)],
                      writes=[("w1", kc, q) for kc in range(KC) for q in range(8)])

            def refill_w2(blk):
                t.dma(ws2, w2_bf[:], scr_w2c.ap()[blk % 2],
                      reads=[("cw2", blk % 2, i) for i in range(64)],
                      writes=[("w2", kh, hf) for kh in range(FFC) for hf in range(2)])
                refilled_w[0] = max(refilled_w[0], blk)

            for b_ in range(NB):
                enq_conv_block(b_)
            for r_ in range(NEXP):
                apu = up_t.ap()[r_].rearrange("(kc p) e -> p kc e", p=128)
                for q in range(8):
                    conv_queue.append((None, apu[:, q:q + 1, :],
                                       scr_wupc.ap()[r_][:, q * 512:(q + 1) * 512],
                                       ("cwup", r_, q), 512))
                apd = down_t.ap()[r_].rearrange("(kc p) h -> p kc h", p=128)
                for q in range(8):
                    conv_queue.append((
                        None,
                        apd[:, q // 2:q // 2 + 1, (q % 2) * 512:(q % 2) * 512 + 512],
                        scr_wdnc.ap()[r_][:, q * 512:(q + 1) * 512],
                        ("cwdn", r_, q), 512))
            for kc in range(KC):
                ap = wqkvb_t.ap()[kc * 128:(kc + 1) * 128, :]
                for q in range(7):
                    w = min(512, QKVB_O - q * 512)
                    conv_queue.append((
                        None,
                        ap[:, q * 512:q * 512 + w],
                        scr_qkvbc.ap()[:, kc * QKVB_O + q * 512:kc * QKVB_O + q * 512 + w],
                        ("cqkvb", kc, q), w))
            for kc in range(KC):
                ap = wob_t.ap()[kc * 128:(kc + 1) * 128, :]
                for hf in range(2):
                    conv_queue.append((
                        None,
                        ap[:, hf * 512:(hf + 1) * 512],
                        scr_wobc.ap()[:, kc * D + hf * 512:kc * D + (hf + 1) * 512],
                        ("cwob", kc, hf), 512))

            load_cab(0)
            conv_pump(64)

            for blk in range(nblocks):
                cn = lambda name, j=0: cab[blk % 2][:, LOCAL[name] + j:LOCAL[name] + j + 1]
                ck = ("cab", blk % 2)

                for ti in range(NTA):
                    sl = slice(ti * TTA, (ti + 1) * TTA)
                    t.dma(lx, xa_bf[0][:].rearrange(
                        "p (kc c) -> p kc c", kc=KC), strm_v[:, :, sl],
                        reads=[("strm", c, 2 * ti + h)
                               for c in range(KC) for h in range(2)],
                        writes=[("xa", 0)])
                    seq = []
                    for c in range(KC):
                        seq += [("k", c), ("v", c)]
                    seq += [("q", mc) for mc in range(KC)]
                    refill_qkv(seq[0][0], blk, seq[0][1], 0)
                    refill_qkv(seq[1][0], blk, seq[1][1], 1)
                    for c in range(KC):
                        iK, iV = 2 * c, 2 * c + 1
                        bK = pb[c % 2]
                        for kc in range(KC):
                            t.op("tensor", lambda te, kc=kc, c=c, bK=bK: te.matmul(
                                bK[:], wcb_bf[0][:, kc * 128:(kc + 1) * 128],
                                xa_bf[0][:, kc * TTA:(kc + 1) * TTA],
                                start=(kc == 0), stop=(kc == KC - 1)),
                                reads=[("wcb", 0, kc // 2), ("xa", 0)],
                                writes=[pbk[c % 2]], inc=(kc == KC - 1))
                        t.op("vector", lambda v, bK=bK, c=c: v.tensor_scalar(
                            pool[0][:], bK[:], cn("bk", c), 0.0,
                            op0=Op.add, op1=Op.min),
                            reads=[pbk[c % 2], ck], writes=k0full)
                        t.op("scalar", lambda s: s.activation(
                            pool[0][:], pool[0][:], AF.Exp),
                            reads=k0full, writes=k0full)
                        t.op("vector", lambda v, bK=bK, c=c: v.tensor_scalar(
                            pool[1][:], bK[:], cn("bk", c), 0.0,
                            op0=Op.add, op1=Op.max),
                            reads=[pbk[c % 2], ck], writes=k1full)
                        t.op("vector", lambda v, c=c, ti=ti: v.scalar_tensor_tensor(
                            k_sc[:], pool[1][:], 0.0, pool[0][:],
                            op0=Op.bypass, op1=Op.add,
                            accum_out=ks_acc[:, c * 4 + ti:c * 4 + ti + 1]),
                            reads=k0full + k1full,
                            writes=[("k_sc",), ("ksacc", c, ti)])
                        conv_pump(2)
                        if iK + 2 < len(seq):
                            refill_qkv(seq[iK + 2][0], blk, seq[iK + 2][1], 0)
                        bV = pb[2 + c % 2]
                        for kc in range(KC):
                            t.op("tensor", lambda te, kc=kc, c=c, bV=bV: te.matmul(
                                bV[:], wcb_bf[1][:, kc * 128:(kc + 1) * 128],
                                xa_bf[0][:, kc * TTA:(kc + 1) * TTA],
                                start=(kc == 0), stop=(kc == KC - 1)),
                                reads=[("wcb", 1, kc // 2), ("xa", 0)],
                                writes=[pbk[2 + c % 2]], inc=(kc == KC - 1))
                        t.op("vector", lambda v, bV=bV, c=c, ti=ti: v.scalar_tensor_tensor(
                            k_sc[:], bV[:], cn("bv", c), k_sc[:],
                            op0=Op.add, op1=Op.mult,
                            accum_out=kv_acc[:, c * 4 + ti:c * 4 + ti + 1]),
                            reads=[pbk[2 + c % 2], ck, ("k_sc",)],
                            writes=[("k_sc",), ("kvacc", c, ti)])
                        conv_pump(2)
                        if iV + 2 < len(seq):
                            refill_qkv(seq[iV + 2][0], blk, seq[iV + 2][1], 1)
                    if ti == NTA - 1:
                        accr = [("kvacc", c, tt) for c in range(KC)
                                for tt in range(NTA)] + \
                               [("ksacc", c, tt) for c in range(KC)
                                for tt in range(NTA)]
                        t.op("vector", lambda v: v.tensor_tensor(
                            kvks[:, 0:32:4], kv_acc[:, 0:32:4], kv_acc[:, 1:32:4],
                            op=Op.add), reads=accr, writes=[("kvks",)], inc=False)
                        t.op("vector", lambda v: v.tensor_tensor(
                            kvks[:, 1:32:4], kv_acc[:, 2:32:4], kv_acc[:, 3:32:4],
                            op=Op.add), reads=accr, writes=[("kvks",)], inc=False)
                        t.op("vector", lambda v: v.tensor_tensor(
                            kvks[:, 2:32:4], ks_acc[:, 0:32:4], ks_acc[:, 1:32:4],
                            op=Op.add), reads=accr, writes=[("kvks",)], inc=False)
                        t.op("vector", lambda v: v.tensor_tensor(
                            kvks[:, 3:32:4], ks_acc[:, 2:32:4], ks_acc[:, 3:32:4],
                            op=Op.add), reads=accr, writes=[("kvks",)])
                        t.dma(sx, cc_in.ap(), kvks[:], reads=[("kvks",)],
                              writes=[("ccin",)])
                        if no_cc:
                            t.dma(sx, cc_out.ap(), cc_in.ap(),
                                  reads=[("ccin",)], writes=[("ccout",)])
                        else:
                            t.cc(ccs, "AllReduce", Op.add, [list(range(NCORE))],
                                 [cc_in.ap().opt()], [cc_out.ap().opt()],
                                 reads=[("ccin",)], writes=[("ccout",)])
                        t.dma(lx, kvred[:], cc_out.ap(), reads=[("ccout",)],
                              writes=[("kvred",)])
                        t.op("vector", lambda v: v.reciprocal(
                            ksr[:, 0:16:2], kvred[:, 2:32:4]),
                            reads=[("kvred",)], writes=[("ksr",)], inc=False)
                        t.op("vector", lambda v: v.reciprocal(
                            ksr[:, 1:16:2], kvred[:, 3:32:4]),
                            reads=[("kvred",)], writes=[("ksr",)])
                        t.op("vector", lambda v: v.tensor_tensor(
                            kvdks[:, 0:16:2], kvred[:, 0:32:4], ksr[:, 0:16:2],
                            op=Op.mult),
                            reads=[("kvred",), ("ksr",)], writes=[("kvdks",)],
                            inc=False)
                        t.op("vector", lambda v: v.tensor_tensor(
                            kvdks[:, 1:16:2], kvred[:, 1:32:4], ksr[:, 1:16:2],
                            op=Op.mult),
                            reads=[("kvred",), ("ksr",)], writes=[("kvdks",)])
                    for mc in range(KC):
                        i_q = 16 + mc
                        bQ = pb[4 + mc % 2]
                        for kc in range(KC):
                            t.op("tensor", lambda te, kc=kc, mc=mc, bQ=bQ: te.matmul(
                                bQ[:], wcb_bf[mc % 2][:, kc * 128:(kc + 1) * 128],
                                xa_bf[0][:, kc * TTA:(kc + 1) * TTA],
                                start=(kc == 0), stop=(kc == KC - 1)),
                                reads=[("wcb", mc % 2, kc // 2), ("xa", 0)],
                                writes=[pbk[4 + mc % 2]], inc=(kc == KC - 1))
                        t.op("vector", lambda v, bQ=bQ, mc=mc: v.tensor_scalar(
                            pool[0][:], bQ[:], cn("bq", mc), 0.0,
                            op0=Op.add, op1=Op.min),
                            reads=[pbk[4 + mc % 2], ck], writes=k0full)
                        t.op("scalar", lambda s: s.activation(
                            pool[0][:], pool[0][:], AF.Exp),
                            reads=k0full, writes=k0full)
                        t.op("vector", lambda v, bQ=bQ, mc=mc: v.tensor_scalar(
                            pool[1][:], bQ[:], cn("bq", mc), 0.0,
                            op0=Op.add, op1=Op.max),
                            reads=[pbk[4 + mc % 2], ck], writes=k1full)
                        t.op("vector", lambda v: v.scalar_tensor_tensor(
                            bfs[:], pool[1][:], 0.0, pool[0][:],
                            op0=Op.bypass, op1=Op.add),
                            reads=k0full + k1full, writes=[("bfs",)])
                        t.dma(sx, q_v[:, mc, sl], bfs[:],
                              reads=[("bfs",)],
                              writes=[("scrq", mc, 2 * ti), ("scrq", mc, 2 * ti + 1)])
                        conv_pump(4)
                        if i_q + 2 < len(seq):
                            refill_qkv(seq[i_q + 2][0], blk, seq[i_q + 2][1], mc % 2)


                if blk == 0:
                    refill_wo(0)
                    refill_w1(0)
                    refill_w2(0)
                vw = y1[:]

                def load_qb(j):
                    slj = slice(j * TTB, (j + 1) * TTB)
                    t.dma(lx, qb_bf[:].rearrange("p (kc c) -> p kc c", kc=KC),
                          q_v[:, :, slj],
                          reads=[("scrq", mc, j) for mc in range(KC)],
                          writes=[("qb", c) for c in range(KC)])

                def load_xb(j):
                    slj = slice(j * TTB, (j + 1) * TTB)
                    t.dma(lx, xb_bf[:].rearrange("p (kc c) -> p kc c", kc=KC),
                          strm_v[:, :, slj],
                          reads=[("strm", c, j) for c in range(KC)],
                          writes=[("xb", c) for c in range(KC)])

                def v_stage(j):
                    n = j // 4
                    for c in range(KC):
                        t.op("vector", lambda v, c=c, n=n: v.tensor_scalar(
                            vw[:, c * TTB:(c + 1) * TTB],
                            qb_bf[:, c * TTB:(c + 1) * TTB],
                            kvred[:, c * 4 + 2 + n:c * 4 + 3 + n], None, op0=Op.mult),
                            reads=[("qb", c), ("kvred",)], writes=[("y1", c)])
                    ally = [("y1", c) for c in range(KC)]
                    t.op("scalar", lambda s: s.activation(vw, vw, AF.Ln),
                         reads=ally, writes=ally)
                    t.op("scalar", lambda s: s.activation(
                        vw, vw, AF.Sigmoid, bias=NEG_LN_EPS_ATTN),
                        reads=ally + [("fconst",)], writes=ally)
                    for c in range(KC):
                        t.op("vector", lambda v, c=c, n=n: v.tensor_scalar(
                            qb_bf[:, c * TTB:(c + 1) * TTB],
                            vw[:, c * TTB:(c + 1) * TTB],
                            kvdks[:, c * 2 + n:c * 2 + n + 1], None, op0=Op.mult),
                            reads=[("y1", c), ("kvdks",)], writes=[("qb", c)])

                load_qb(0)
                load_xb(0)
                v_stage(0)
                for j in range(NTB):
                    slj = slice(j * TTB, (j + 1) * TTB)
                    if False:
                        pass
                    for mc in range(KC):
                        bO = pb[mc % 2]
                        for kc in range(KC):
                            t.op("tensor", lambda te, mc=mc, kc=kc, bO=bO: te.matmul(
                                bO[:, 0:TTB],
                                wo_bf[:, kc * D + mc * 128:kc * D + mc * 128 + 128],
                                qb_bf[:, kc * TTB:(kc + 1) * TTB],
                                start=(kc == 0), stop=(kc == KC - 1)),
                                reads=[("wo", kc, mc // 4), ("qb", kc)],
                                writes=[pbk[mc % 2]], inc=(kc == KC - 1))
                        t.op("scalar", lambda s, mc=mc, bO=bO: s.activation(
                            y1[:, mc * TTB:(mc + 1) * TTB], bO[:, 0:TTB], GELU_T,
                            bias=cn("bo", mc)),
                            reads=[pbk[mc % 2], ck], writes=[("y1", mc)])
                    t.op("vector", lambda v: v.scalar_tensor_tensor(
                        y1[:], y1[:], 0.0, xb_bf[:],
                        op0=Op.bypass, op1=Op.add),
                        reads=[("y1", m) for m in range(KC)] +
                              [("xb", c) for c in range(KC)],
                        writes=[("y1", m) for m in range(KC)])
                    y_aps = [y1[:, c * TTB:(c + 1) * TTB] for c in range(KC)]
                    ykeys = [("y1", c) for c in range(KC)]
                    ln_stats(y_aps, ykeys, TTB, EPS_LN_A, pb[4:8], pbk[4:8],
                             srow, srk, pD, kD)
                    mub = pb[6][:, 0:TTB].rearrange(
                        "p (r c) -> p r c", r=1).broadcast_to((128, KC, TTB))
                    t.op("vector", lambda v: v.scalar_tensor_tensor(
                        y1[:].rearrange("p (r c) -> p r c", r=KC),
                        y1[:].rearrange("p (r c) -> p r c", r=KC),
                        0.0, mub, op0=Op.bypass, op1=Op.subtract),
                        reads=ykeys + [pbk[6]], writes=ykeys)
                    for c in range(KC):
                        t.op("vector", lambda v, c=c: v.scalar_tensor_tensor(
                            pB, y_aps[c], cn("ln1s", c),
                            pb[7][:, 0:TTB], op0=Op.mult, op1=Op.mult),
                            reads=[ykeys[c], pbk[7], ck], writes=[kB])
                        t.op("vector", lambda v, c=c: v.tensor_scalar(
                            xb_bf[:, c * TTB:(c + 1) * TTB], pB,
                            cn("ln1b", c), None, op0=Op.add),
                            reads=[kB, ck], writes=[("xb", c)])
                    if j + 1 < NTB and HOIST_V:
                        load_qb(j + 1)
                        v_stage(j + 1)
                    for mh in range(FFC):
                        bH = pb[2 + mh % 2]
                        for kc in range(KC):
                            t.op("tensor", lambda te, mh=mh, kc=kc, bH=bH: te.matmul(
                                bH[:, 0:TTB],
                                w1_bf[:, kc * FF + mh * 128:kc * FF + mh * 128 + 128],
                                xb_bf[:, kc * TTB:(kc + 1) * TTB],
                                start=(kc == 0), stop=(kc == KC - 1)),
                                reads=[("w1", kc, mh // 4), ("xb", kc)],
                                writes=[pbk[2 + mh % 2]], inc=(kc == KC - 1))
                        t.op("scalar", lambda s, mh=mh, bH=bH: s.activation(
                            h_bf[:, mh * TTB:(mh + 1) * TTB], bH[:, 0:TTB], GELU_T,
                            bias=cn("bff1", mh)),
                            reads=[pbk[2 + mh % 2], ck], writes=[("h", mh)])
                    conv_pump(26)
                    for mc in range(KC):
                        bF = pb[mc % 2]
                        for kh in range(FFC):
                            t.op("tensor", lambda te, mc=mc, kh=kh, bF=bF: te.matmul(
                                bF[:, 0:TTB],
                                w2_bf[:, kh * D + mc * 128:kh * D + mc * 128 + 128],
                                h_bf[:, kh * TTB:(kh + 1) * TTB],
                                start=(kh == 0), stop=(kh == FFC - 1)),
                                reads=[("w2", kh, mc // 4), ("h", kh)],
                                writes=[pbk[mc % 2]], inc=(kh == FFC - 1))
                        t.op("scalar", lambda s, mc=mc, bF=bF: s.activation(
                            pC, bF[:, 0:TTB], GELU_T, bias=cn("bff2", mc)),
                            reads=[pbk[mc % 2], ck], writes=[kC])
                        t.op("vector", lambda v, mc=mc: v.scalar_tensor_tensor(
                            y1[:, mc * TTB:(mc + 1) * TTB], pC, 0.0,
                            xb_bf[:, mc * TTB:(mc + 1) * TTB],
                            op0=Op.bypass, op1=Op.add),
                            reads=[kC, ("xb", mc)], writes=[("y1", mc)])
                    ln_stats(y_aps, ykeys, TTB, EPS_LN_A, pb[4:8], pbk[4:8],
                             srow, srk, pD, kD)
                    mub = pb[6][:, 0:TTB].rearrange(
                        "p (r c) -> p r c", r=1).broadcast_to((128, KC, TTB))
                    t.op("vector", lambda v: v.scalar_tensor_tensor(
                        y1[:].rearrange("p (r c) -> p r c", r=KC),
                        y1[:].rearrange("p (r c) -> p r c", r=KC),
                        0.0, mub, op0=Op.bypass, op1=Op.subtract),
                        reads=ykeys + [pbk[6]], writes=ykeys)
                    for c in range(KC):
                        t.op("vector", lambda v, c=c: v.scalar_tensor_tensor(
                            pB, y_aps[c], cn("ln2s", c),
                            pb[7][:, 0:TTB], op0=Op.mult, op1=Op.mult),
                            reads=[kA, pbk[7], ck], writes=[kB])
                        if blk < NB - 1:
                            t.op("vector", lambda v, c=c: v.tensor_scalar(
                                xb_bf[:, c * TTB:(c + 1) * TTB], pB, cn("ln2b", c),
                                None, op0=Op.add),
                                reads=[kB, ck], writes=[("xb", c)])
                        else:
                            t.op("vector", lambda v, c=c: v.tensor_scalar(
                                y1[:, c * TTB:(c + 1) * TTB], pB, cn("ln2b", c),
                                None, op0=Op.add),
                                reads=[kB, ck], writes=[("y1", c)])
                            t.dma(sx, af32_v[:, c, slj],
                                  y1[:, c * TTB:(c + 1) * TTB],
                                  reads=[("y1", c)], writes=[("af32", c, j)])
                    if blk < NB - 1:
                        t.dma(sx, strm_v[:, :, slj],
                              xb_bf[:].rearrange("p (kc c) -> p kc c", kc=KC),
                              reads=[("xb", c) for c in range(KC)],
                              writes=[("strm", c, j) for c in range(KC)])
                    if j + 1 < NTB:
                        load_xb(j + 1)
                        if not HOIST_V:
                            load_qb(j + 1)
                            v_stage(j + 1)
                if blk < NB - 1:
                    load_cab(blk + 1)
                    refill_wo(blk + 1)
                    refill_w1(blk + 1)
                    refill_w2(blk + 1)
            conv_pump(10 ** 9)
            t.barrier(STREAMS)

        stop_now = stop_now or (upto == "aa")
        if not stop_now:
            ln_junction("j1", x0_v, None, af32_v, "af32", x1_v, "x1", n1_v, "n1", EPS_MOD)
        stop_now = stop_now or (upto == "j1")

        def moe_call(tag, xn_v, xn_pref, xp_v, xp_pref, xout_v, xout_pref):
            with contextlib.ExitStack() as seg:
                ssb = lambda name, shape, dt: seg.enter_context(
                    nc.sbuf_tensor(name, list(shape), dt))
                rtr_bf = ssb(f"rtr_bf_{tag}", (128, KC * 16), BF16)
                lgf = ssb(f"lgf_{tag}", (16, TTB), F32)
                lg_tm = ssb(f"lg_tm_{tag}", (128, 16), F32)
                e_tm = ssb(f"e_tm_{tag}", (128, 16), F32)
                est = ssb(f"est_{tag}", (128, 4), F32)
                cnt_tm = ssb(f"cnt_tm_{tag}", (128, 16), F32)
                gate_tm = ssb(f"gate_tm_{tag}", (128, 16), F32)
                gatef = ssb(f"gatef_{tag}", (16, T), F32)
                sel_all = ssb(f"sel_all_{tag}", (16, 16 * 128), F32)
                identF = ssb(f"identF_{tag}", (128, 128), F32)
                xn_sb = ssb(f"mxn_{tag}", (128, KC * TTB), BF16)
                wup_bf = [ssb(f"wup_{tag}{i}", (128, KC * 512), BF16) for i in range(8)]
                wdn_bf = [ssb(f"wdn_{tag}{i}", (128, 4 * D), BF16) for i in range(8)]
                act_bf = ssb(f"act_bf_{tag}", (128, 16 * TTB), BF16)
                acc_bf = ssb(f"acc_bf_{tag}", (128, KC * T), BF16)
                pool = [ssb(f"mpool_{tag}{i}", (128, TTB), F32) for i in range(5)]
                pbm = [seg.enter_context(nc.psum_tensor(f"mps_{tag}{i}", [128, 512], F32))
                       for i in range(8)]
                pmk = [("mps", i) for i in range(8)]

                t.op("gpsimd", lambda g: g.memset(identF[:], 0.0), writes=[("iF",)])
                t.op("gpsimd", lambda g: g.affine_select(
                    out=identF[:], in_=identF[:], compare_op=Op.not_equal,
                    fill=1.0, base=0, pattern=[[-1, 128]], channel_multiplier=1),
                    reads=[("iF",)], writes=[("iF",)])
                t.op("gpsimd", lambda g: g.memset(sel_all[:], 1.0), writes=[("sel",)])
                t.op("gpsimd", lambda g: g.affine_select(
                    out=sel_all[:], in_=sel_all[:], compare_op=Op.is_ge,
                    fill=0.0, base=0, pattern=[[1, 16 * 128]],
                    channel_multiplier=-128),
                    reads=[("sel",)], writes=[("sel",)])
                t.op("gpsimd", lambda g: g.affine_select(
                    out=sel_all[:], in_=sel_all[:], compare_op=Op.is_ge,
                    fill=0.0, base=127, pattern=[[-1, 16 * 128]],
                    channel_multiplier=128),
                    reads=[("sel",)], writes=[("sel",)])
                ap = router_t.ap().rearrange("(kc p) e -> p kc e", p=128)
                load_cast2(wmo, ap[:, :, :], rtr_bf[:], ("rtr",), 0, KC * 16)

                for j in range(NTB):
                    slj = slice(j * TTB, (j + 1) * TTB)
                    t.dma(lx, xn_sb[:].rearrange("p (kc c) -> p kc c", kc=KC),
                          xn_v[:, :, slj],
                          reads=[(xn_pref, c, j) for c in range(KC)],
                          writes=[("mxn",)])
                    bL = pbm[0]
                    for kc in range(KC):
                        t.op("tensor", lambda te, kc=kc, bL=bL: te.matmul(
                            bL[0:16, 0:TTB], rtr_bf[:, kc * 16:(kc + 1) * 16],
                            xn_sb[:, kc * TTB:(kc + 1) * TTB],
                            start=(kc == 0), stop=(kc == KC - 1)),
                            reads=[("rtr",), ("mxn",)], writes=[pmk[0]],
                            inc=(kc == KC - 1))
                    t.op("vector", lambda v, bL=bL: v.tensor_copy(
                        lgf[:, 0:TTB], bL[0:16, 0:TTB]),
                        reads=[pmk[0]], writes=[("lgf",)])
                    for sub in range(2):
                        ss = slice(sub * 128, (sub + 1) * 128)
                        bT = pbm[1]
                        t.op("tensor", lambda te, ss=ss, bT=bT: te.transpose(
                            bT[:, 0:16], lgf[0:16, ss], identF[0:16, 0:16]),
                            reads=[("lgf",), ("iF",)], writes=[pmk[1]])
                        t.op("vector", lambda v, bT=bT: v.tensor_copy(
                            lg_tm[:], bT[:, 0:16]),
                            reads=[pmk[1]], writes=[("lgtm",)])
                        t.op("vector", lambda v: v.tensor_reduce(
                            est[:, 0:1], lg_tm[:], axis=mybir.AxisListType.X,
                            op=Op.max),
                            reads=[("lgtm",)], writes=[("est", 0)])
                        t.op("vector", lambda v: v.tensor_scalar(
                            est[:, 1:2], est[:, 0:1], -1.0, None, op0=Op.mult),
                            reads=[("est", 0)], writes=[("est", 1)])
                        t.op("scalar", lambda s: s.activation(
                            e_tm[:], lg_tm[:], AF.Exp, bias=est[:, 1:2],
                            accum_out=est[:, 2:3]),
                            reads=[("lgtm",), ("est", 1)],
                            writes=[("etm",), ("est", 2)])
                        t.op("vector", lambda v: v.reciprocal(
                            est[:, 3:4], est[:, 2:3]),
                            reads=[("est", 2)], writes=[("est", 3)])
                        for k in range(16):
                            if k == 0:
                                t.op("vector", lambda v, k=k: v.tensor_scalar(
                                    cnt_tm[:], e_tm[:], e_tm[:, k:k + 1], None,
                                    op0=Op.is_lt),
                                    reads=[("etm",)], writes=[("cnt",)])
                            else:
                                t.op("vector", lambda v, k=k: v.scalar_tensor_tensor(
                                    cnt_tm[:], e_tm[:], e_tm[:, k:k + 1], cnt_tm[:],
                                    op0=Op.is_lt, op1=Op.add),
                                    reads=[("etm",), ("cnt",)], writes=[("cnt",)])
                        t.op("vector", lambda v: v.tensor_scalar(
                            cnt_tm[:], cnt_tm[:], 3.0, None, op0=Op.is_le),
                            reads=[("cnt",)], writes=[("cnt",)])
                        t.op("vector", lambda v: v.scalar_tensor_tensor(
                            gate_tm[:], e_tm[:], est[:, 3:4], cnt_tm[:],
                            op0=Op.mult, op1=Op.mult),
                            reads=[("etm",), ("est", 3), ("cnt",)], writes=[("gtm",)])
                        bT2 = pbm[2]
                        t.op("tensor", lambda te, bT2=bT2: te.transpose(
                            bT2[0:16, 0:128], gate_tm[:], identF[:]),
                            reads=[("gtm",), ("iF",)], writes=[pmk[2]])
                        t.op("vector", lambda v, j=j, sub=sub, bT2=bT2: v.tensor_copy(
                            gatef[0:16, j * TTB + sub * 128:j * TTB + (sub + 1) * 128],
                            bT2[0:16, 0:128]),
                            reads=[pmk[2]], writes=[("gatef", j, sub)])


                def load_group(gi):
                    grp = EG[gi]
                    reg = gi % 2
                    for ri, r in enumerate(grp):
                        t.dma(wmo, wup_bf[reg * 4 + ri][:], scr_wupc.ap()[r],
                              reads=[("cwup", r, q) for q in range(8)],
                              writes=[("wup", reg * 4 + ri, q) for q in range(8)])
                        t.dma(wmo, wdn_bf[reg * 4 + ri][:], scr_wdnc.ap()[r],
                              reads=[("cwdn", r, q) for q in range(8)],
                              writes=[("wdn", reg * 4 + ri, q) for q in range(8)])

                load_group(0)
                for gi, grp in enumerate(EG):
                    reg = gi % 2
                    for j in range(NTB):
                        if j == 1 and gi + 1 < len(EG):
                            load_group(gi + 1)
                        slj = slice(j * TTB, (j + 1) * TTB)
                        t.dma(lx, xn_sb[:].rearrange("p (kc c) -> p kc c", kc=KC),
                              xn_v[:, :, slj],
                              reads=[(xn_pref, c, j) for c in range(KC)],
                              writes=[("mxn",)])
                        for ri, r in enumerate(grp):
                            shared = (r >= 16)
                            wkeys = [("wup", reg * 4 + ri, q) for q in range(4)]
                            if not shared:
                                bG = pbm[2]
                                t.op("tensor", lambda te, r=r, j=j, bG=bG: te.matmul(
                                    bG[:, 0:TTB],
                                    sel_all[0:16, r * 128:(r + 1) * 128],
                                    gatef[0:16, j * TTB:(j + 1) * TTB],
                                    start=True, stop=True),
                                    reads=[("sel",)] + [("gatef", j, su) for su in range(2)],
                                    writes=[pmk[2]])
                            for ec in range(4):
                                bU = pbm[ec % 2]
                                for kc in range(KC):
                                    t.op("tensor", lambda te, ri=ri, ec=ec, kc=kc, bU=bU, reg=reg: te.matmul(
                                        bU[:, 0:TTB],
                                        wup_bf[reg * 4 + ri][:, kc * 512 + ec * 128:kc * 512 + ec * 128 + 128],
                                        xn_sb[:, kc * TTB:(kc + 1) * TTB],
                                        start=(kc == 0), stop=(kc == KC - 1)),
                                        reads=[("wup", reg * 4 + ri, kc), ("mxn",)],
                                        writes=[pmk[ec % 2]], inc=(kc == KC - 1))
                                if shared:
                                    t.op("scalar", lambda s, ri=ri, ec=ec, bU=bU: s.activation(
                                        act_bf[:, (ri * 4 + ec) * TTB:(ri * 4 + ec + 1) * TTB],
                                        bU[:, 0:TTB], GELU_E),
                                        reads=[pmk[ec % 2]], writes=[("act", ri, ec)])
                                else:
                                    t.op("scalar", lambda s, ec=ec, bU=bU: s.activation(
                                        pool[4][:], bU[:, 0:TTB], GELU_E),
                                        reads=[pmk[ec % 2]], writes=[("mpool", 4)])
                                    t.op("vector", lambda v, ri=ri, ec=ec: v.tensor_tensor(
                                        act_bf[:, (ri * 4 + ec) * TTB:(ri * 4 + ec + 1) * TTB],
                                        pool[4][:], pbm[2][:, 0:TTB], op=Op.mult),
                                        reads=[("mpool", 4), pmk[2]],
                                        writes=[("act", ri, ec)])
                        for mc in range(KC):
                            bD = pbm[3 + mc % 2]
                            nmm = len(grp) * 4
                            i = 0
                            for ri, r in enumerate(grp):

                                for ec in range(4):
                                    i += 1
                                    t.op("tensor", lambda te, ri=ri, ec=ec, mc=mc, bD=bD, reg=reg, i=i, nmm=nmm: te.matmul(
                                        bD[:, 0:TTB],
                                        wdn_bf[reg * 4 + ri][:, ec * D + mc * 128:ec * D + mc * 128 + 128],
                                        act_bf[:, (ri * 4 + ec) * TTB:(ri * 4 + ec + 1) * TTB],
                                        start=(i == 1), stop=(i == nmm)),
                                        reads=[("wdn", reg * 4 + ri, 2 * ec + mc // 4),
                                               ("act", ri, ec)],
                                        writes=[pmk[3 + mc % 2]], inc=(i == nmm))
                            asl = acc_bf[:, mc * T + j * TTB:mc * T + (j + 1) * TTB]
                            if gi == 0:
                                t.op("vector", lambda v, bD=bD, asl=asl: v.tensor_copy(
                                    asl, bD[:, 0:TTB]),
                                    reads=[pmk[3 + mc % 2]], writes=[("macc", mc, j)])
                            elif gi < len(EG) - 1:
                                t.op("vector", lambda v, bD=bD, asl=asl: v.scalar_tensor_tensor(
                                    asl, bD[:, 0:TTB], 0.0, asl,
                                    op0=Op.bypass, op1=Op.add),
                                    reads=[pmk[3 + mc % 2], ("macc", mc, j)],
                                    writes=[("macc", mc, j)])
                            else:
                                t.op("vector", lambda v, bD=bD, asl=asl, mc=mc: v.scalar_tensor_tensor(
                                    pool[mc % 2][:], bD[:, 0:TTB], 0.0, asl,
                                    op0=Op.bypass, op1=Op.add),
                                    reads=[pmk[3 + mc % 2], ("macc", mc, j)],
                                    writes=[("mpool", mc % 2)])
                                t.dma(lx, pool[2 + mc % 2][:], xp_v[:, mc, slj],
                                      reads=[(xp_pref, mc, j)],
                                      writes=[("mpool", 2 + mc % 2)])
                                t.op("vector", lambda v, mc=mc: v.tensor_tensor(
                                    pool[mc % 2][:], pool[mc % 2][:],
                                    pool[2 + mc % 2][:], op=Op.add),
                                    reads=[("mpool", mc % 2), ("mpool", 2 + mc % 2)],
                                    writes=[("mpool", mc % 2)])
                                t.dma(sx, xout_v[:, mc, slj], pool[mc % 2][:],
                                      reads=[("mpool", mc % 2)],
                                      writes=[(xout_pref, mc, j)])
                t.barrier(STREAMS)

        if not stop_now:
            moe_call("m1", n1_v, "n1", x1_v, "x1", x2_v, "x2")
        stop_now = stop_now or (upto == "m1")
        if not stop_now:
            ln_junction("j2", x2_v, "x2", None, None, None, None, n2_v, "n2", EPS_MOD)
        stop_now = stop_now or (upto == "j2")

        if not stop_now:
         with contextlib.ExitStack() as seg:
            ssb = lambda name, shape, dt: seg.enter_context(
                nc.sbuf_tensor(name, list(shape), dt))
            wqkvb_bf = ssb("wqkvb_bf", (128, KC * QKVB_O), BF16)
            wob_bf = ssb("wob_bf", (128, KC * D), BF16)
            identB = ssb("identB", (128, 128), BF16)
            identF2 = ssb("identF2", (128, 128), F32)
            xst = [ssb(f"bxst{i}", (128, KC * TTA), BF16) for i in range(2)]
            qt = [ssb(f"bqt{i}", (128, QKVB_O), F32) for i in range(2)]
            HH = NHEAD // 2
            ue = [ssb(f"bue{i}", (128, HH * 130), F32) for i in range(4)]
            pe = ssb("bpe", (128, HH * 130), F32)
            rsb = ssb("brs", (128, HH * 128), F32)
            prb = ssb("bpr", (128, HH * 128), F32)
            coef = ssb("bcoef", (128, 32 * 16), F32)
            lay = [ssb(f"blay{i}", (128, D), BF16) for i in range(2)]
            layfm = [ssb(f"blayfm{i}", (128, KC * TTA), BF16) for i in range(2)]
            jbB = ssb("bjb", (128, KC * TTA), F32)
            pool = [ssb(f"bpool{i}", (128, TTA), F32) for i in range(4)]
            srow = ssb("bsrow", (1, 2048), F32)
            pbb = [seg.enter_context(nc.psum_tensor(f"bps{i}", [128, 512], F32))
                   for i in range(7)]
            pbT = seg.enter_context(nc.psum_tensor("bpsT", [128, 1024], BF16))
            pbbk = [("bps", i) for i in range(7)]
            srkB = [("bsrow", i) for i in range(4)]

            cgb = ssb("cgb", (128, 16), F32)
            base_gb = CINDEX[("bg",)][0]
            t.dma(wmo, cgb[:], consts_in.ap()[:, base_gb:base_gb + 16],
                  writes=[("cgb",)])
            t.op("gpsimd", lambda g: g.memset(identB[:], 0.0), writes=[("iB",)])
            t.op("gpsimd", lambda g: g.affine_select(
                out=identB[:], in_=identB[:], compare_op=Op.not_equal,
                fill=1.0, base=0, pattern=[[-1, 128]], channel_multiplier=1),
                reads=[("iB",)], writes=[("iB",)])
            t.op("gpsimd", lambda g: g.memset(identF2[:], 0.0), writes=[("iF2",)])
            t.op("gpsimd", lambda g: g.affine_select(
                out=identF2[:], in_=identF2[:], compare_op=Op.not_equal,
                fill=1.0, base=0, pattern=[[-1, 128]], channel_multiplier=1),
                reads=[("iF2",)], writes=[("iF2",)])
            t.dma(wmo, wqkvb_bf[:], scr_qkvbc.ap()[:, :],
                  reads=[("cqkvb", kc, q) for kc in range(KC) for q in range(7)],
                  writes=[("wqkvb", kc, q) for kc in range(KC) for q in range(7)])
            t.dma(wmo, wob_bf[:], scr_wobc.ap()[:, :],
                  reads=[("cwob", kc, hf) for kc in range(KC) for hf in range(2)],
                  writes=[("wob", kc, hf) for kc in range(KC) for hf in range(2)])

            def hsl(buf, hb, off, w):
                return buf[:].rearrange("p (h o) -> p h o", h=NHEAD)[
                    :, hb * HH:(hb + 1) * HH, off:off + w]

            def uex(buf, off, w):
                return buf[:].rearrange("p (h o) -> p h o", h=HH)[:, :, off:off + w]

            CO = {}
            conames = ["sk0", "sk1", "sq0", "sq1", "ekk", "ek0q0", "ek0q1",
                       "ek1q1", "rsk0", "rsk1", "rsq0", "rsq1", "kd", "rkd",
                       "den0", "rden0", "den1", "rden1", "c00", "c10", "c11",
                       "ck", "sg0", "sg1", "b1p", "g0", "g1a", "g1b", "gt",
                       "tmp0", "tmp1", "tmp2"]
            for i, nm in enumerate(conames):
                CO[nm] = coef[:, i * 16:(i + 1) * 16]
            COK = {nm: ("coef", nm) for nm in conames}

            def co_op(dst, emit, reads, inc=True):
                t.op("vector", emit, reads=reads, writes=[COK[dst]], inc=inc)

            for p in range(2):
                s_tiles = (p, p + 2)
                for si, ti in enumerate(s_tiles):
                    sl = slice(ti * TTA, (ti + 1) * TTA)
                    t.dma(lx, xst[si][:].rearrange("p (kc c) -> p kc c", kc=KC),
                          n2_v[:, :, sl],
                          reads=[("n2", c, 2 * ti + h) for c in range(KC)
                                 for h in range(2)],
                          writes=[("bxst", si)])
                for sub in range(4):
                    for si in range(2):
                        for nb_ in range(7):
                            w = min(512, QKVB_O - nb_ * 512)
                            bQ = pbb[nb_]
                            for kc in range(KC):
                                t.op("tensor", lambda te, si=si, nb_=nb_, kc=kc, w=w, sub=sub, bQ=bQ: te.matmul(
                                    bQ[:, 0:w],
                                    xst[si][:, kc * TTA + sub * 128:kc * TTA + sub * 128 + 128],
                                    wqkvb_bf[:, kc * QKVB_O + nb_ * 512:kc * QKVB_O + nb_ * 512 + w],
                                    start=(kc == 0), stop=(kc == KC - 1)),
                                    reads=[("bxst", si), ("wqkvb", kc, nb_)],
                                    writes=[pbbk[nb_]],
                                    inc=(kc == KC - 1))
                        for nb_ in range(7):
                            w = min(512, QKVB_O - nb_ * 512)
                            bQ = pbb[nb_]
                            t.op("vector", lambda v, si=si, nb_=nb_, w=w, bQ=bQ: v.tensor_copy(
                                qt[si][:, nb_ * 512:nb_ * 512 + w], bQ[:, 0:w]),
                                reads=[pbbk[nb_]],
                                writes=[("qt", si)])
                    co_op("sg0", lambda v: v.tensor_copy(
                        CO["sg0"][:, 0:HH], hsl(qt[0], 0, 192, 1)), reads=[("qt", 0)],
                        inc=False)
                    co_op("sg0", lambda v: v.tensor_copy(
                        CO["sg0"][:, HH:NHEAD], hsl(qt[0], 1, 192, 1)),
                        reads=[("qt", 0)])
                    t.op("scalar", lambda s: s.activation(CO["sg0"], CO["sg0"], AF.Sigmoid),
                         reads=[COK["sg0"]], writes=[COK["sg0"]])
                    co_op("sg1", lambda v: v.tensor_copy(
                        CO["sg1"][:, 0:HH], hsl(qt[1], 0, 192, 1)), reads=[("qt", 1)],
                        inc=False)
                    co_op("sg1", lambda v: v.tensor_copy(
                        CO["sg1"][:, HH:NHEAD], hsl(qt[1], 1, 192, 1)),
                        reads=[("qt", 1)])
                    t.op("scalar", lambda s: s.activation(CO["sg1"], CO["sg1"], AF.Sigmoid),
                         reads=[COK["sg1"]], writes=[COK["sg1"]])
                    for hb in range(2):
                        hsl8 = slice(hb * HH, (hb + 1) * HH)
                        srcs = [(0, 0), (1, 0), (0, 64), (1, 64)]
                        for i, (bi, off) in enumerate(srcs):
                            t.op("vector", lambda v, i=i, bi=bi, off=off, hb=hb: v.tensor_scalar(
                                uex(ue[i], 2, 64), hsl(qt[bi], hb, off, 64), 0.0, None,
                                op0=Op.max),
                                reads=[("qt", bi)], writes=[("ue", i)], inc=False)
                            t.op("vector", lambda v, i=i, bi=bi, off=off, hb=hb: v.tensor_scalar(
                                uex(ue[i], 66, 64), hsl(qt[bi], hb, off, 64), -1.0, 0.0,
                                op0=Op.mult, op1=Op.max),
                                reads=[("qt", bi)], writes=[("ue", i)], inc=False)
                            t.op("vector", lambda v, i=i: v.tensor_copy(
                                uex(ue[i], 0, 2), uex(ue[i], 128, 2)),
                                reads=[("ue", i)], writes=[("ue", i)])
                        self_map = [("sq0", 0), ("sq1", 1), ("sk0", 2), ("sk1", 3)]
                        for nm, i in self_map:
                            t.op("vector", lambda v, i=i: v.tensor_tensor(
                                rsb[:].rearrange("p (h o) -> p h o", h=HH),
                                uex(ue[i], 1, 128), uex(ue[i], 0, 128), op=Op.add),
                                reads=[("ue", i)], writes=[("rsb",)])
                            t.op("vector", lambda v, i=i: v.tensor_tensor(
                                prb[:].rearrange("p (h o) -> p h o", h=HH),
                                uex(ue[i], 2, 128),
                                rsb[:].rearrange("p (h o) -> p h o", h=HH), op=Op.mult),
                                reads=[("ue", i), ("rsb",)], writes=[("prb",)])
                            t.op("vector", lambda v, nm=nm, hsl8=hsl8: v.tensor_reduce(
                                CO[nm][:, hsl8],
                                prb[:].rearrange("p (h o) -> p h o", h=HH),
                                axis=mybir.AxisListType.X, op=Op.add),
                                reads=[("prb",)], writes=[COK[nm]])
                        e_map = [("ekk", 2, 3), ("ek0q0", 2, 0), ("ek0q1", 2, 1),
                                 ("ek1q1", 3, 1)]
                        for nm, ia, ib in e_map:
                            t.op("vector", lambda v, ia=ia, ib=ib: v.tensor_tensor(
                                pe[:], ue[ia][:], ue[ib][:], op=Op.mult),
                                reads=[("ue", ia), ("ue", ib)], writes=[("pe",)])
                            t.op("vector", lambda v: v.tensor_tensor(
                                rsb[:].rearrange("p (h o) -> p h o", h=HH),
                                uex(pe, 1, 128), uex(pe, 0, 128), op=Op.add),
                                reads=[("pe",)], writes=[("rsb",)])
                            t.op("vector", lambda v: v.tensor_tensor(
                                prb[:].rearrange("p (h o) -> p h o", h=HH),
                                uex(pe, 2, 128),
                                rsb[:].rearrange("p (h o) -> p h o", h=HH), op=Op.mult),
                                reads=[("pe",), ("rsb",)], writes=[("prb",)])
                            t.op("vector", lambda v, nm=nm, hsl8=hsl8: v.tensor_reduce(
                                CO[nm][:, hsl8],
                                prb[:].rearrange("p (h o) -> p h o", h=HH),
                                axis=mybir.AxisListType.X, op=Op.add),
                                reads=[("prb",)], writes=[COK[nm]])
                    for nm in ("sk0", "sk1", "sq0", "sq1"):
                        co_op("r" + nm, lambda v, nm=nm: v.reciprocal(
                            CO["r" + nm], CO[nm]), reads=[COK[nm]])
                    co_op("tmp0", lambda v: v.tensor_tensor(
                        CO["tmp0"], CO["rsk0"], CO["rsk1"], op=Op.mult),
                        reads=[COK["rsk0"], COK["rsk1"]])
                    co_op("kd", lambda v: v.tensor_tensor(
                        CO["kd"], CO["ekk"], CO["tmp0"], op=Op.mult),
                        reads=[COK["ekk"], COK["tmp0"]])
                    co_op("tmp1", lambda v: v.tensor_scalar(
                        CO["tmp1"], CO["kd"], EPS_B, None, op0=Op.add),
                        reads=[COK["kd"]])
                    co_op("rkd", lambda v: v.reciprocal(CO["rkd"], CO["tmp1"]),
                          reads=[COK["tmp1"]])
                    co_op("tmp0", lambda v: v.tensor_tensor(
                        CO["tmp0"], CO["rsk0"], CO["rsq0"], op=Op.mult),
                        reads=[COK["rsk0"], COK["rsq0"]])
                    co_op("den0", lambda v: v.tensor_tensor(
                        CO["den0"], CO["ek0q0"], CO["tmp0"], op=Op.mult),
                        reads=[COK["ek0q0"], COK["tmp0"]])
                    co_op("c00", lambda v: v.tensor_scalar(
                        CO["c00"], CO["den0"], REC1, None, op0=Op.mult),
                        reads=[COK["den0"]])
                    co_op("tmp0", lambda v: v.tensor_scalar(
                        CO["tmp0"], CO["den0"], EPS_B, None, op0=Op.add),
                        reads=[COK["den0"]])
                    co_op("rden0", lambda v: v.reciprocal(CO["rden0"], CO["tmp0"]),
                          reads=[COK["tmp0"]])
                    co_op("tmp0", lambda v: v.tensor_tensor(
                        CO["tmp0"], CO["rsk0"], CO["rsq1"], op=Op.mult),
                        reads=[COK["rsk0"], COK["rsq1"]])
                    co_op("tmp0", lambda v: v.tensor_tensor(
                        CO["tmp0"], CO["ek0q1"], CO["tmp0"], op=Op.mult),
                        reads=[COK["ek0q1"], COK["tmp0"]])
                    co_op("tmp1", lambda v: v.tensor_tensor(
                        CO["tmp1"], CO["rsk1"], CO["rsq1"], op=Op.mult),
                        reads=[COK["rsk1"], COK["rsq1"]])
                    co_op("tmp1", lambda v: v.tensor_tensor(
                        CO["tmp1"], CO["ek1q1"], CO["tmp1"], op=Op.mult),
                        reads=[COK["ek1q1"], COK["tmp1"]])
                    co_op("den1", lambda v: v.tensor_tensor(
                        CO["den1"], CO["tmp0"], CO["tmp1"], op=Op.add),
                        reads=[COK["tmp0"], COK["tmp1"]])
                    co_op("c10", lambda v: v.tensor_scalar(
                        CO["c10"], CO["tmp0"], REC1, None, op0=Op.mult),
                        reads=[COK["tmp0"]])
                    co_op("c11", lambda v: v.tensor_tensor(
                        CO["c11"], CO["tmp1"], CO["rkd"], op=Op.mult),
                        reads=[COK["tmp1"], COK["rkd"]])
                    co_op("tmp2", lambda v: v.tensor_scalar(
                        CO["tmp2"], CO["den1"], EPS_B, None, op0=Op.add),
                        reads=[COK["den1"]])
                    co_op("rden1", lambda v: v.reciprocal(CO["rden1"], CO["tmp2"]),
                          reads=[COK["tmp2"]])
                    co_op("ck", lambda v: v.tensor_tensor(
                        CO["ck"], CO["kd"], CO["rkd"], op=Op.mult),
                        reads=[COK["kd"], COK["rkd"]])
                    co_op("ck", lambda v: v.tensor_scalar(
                        CO["ck"], CO["ck"], REC1, None, op0=Op.mult),
                        reads=[COK["ck"]])
                    co_op("b1p", lambda v: v.tensor_tensor(
                        CO["b1p"], CO["sg1"], CO["kd"], op=Op.mult),
                        reads=[COK["sg1"], COK["kd"]])
                    co_op("g0", lambda v: v.tensor_tensor(
                        CO["g0"], CO["c00"], CO["sg0"], op=Op.mult),
                        reads=[COK["c00"], COK["sg0"]])
                    co_op("g0", lambda v: v.tensor_tensor(
                        CO["g0"], CO["g0"], CO["rden0"], op=Op.mult),
                        reads=[COK["g0"], COK["rden0"]])
                    co_op("g0", lambda v: v.tensor_scalar(
                        CO["g0"], CO["g0"], SCALE_B, None, op0=Op.mult),
                        reads=[COK["g0"]])
                    co_op("g1a", lambda v: v.tensor_tensor(
                        CO["g1a"], CO["c10"], CO["sg0"], op=Op.mult),
                        reads=[COK["c10"], COK["sg0"]])
                    co_op("g1a", lambda v: v.tensor_tensor(
                        CO["g1a"], CO["g1a"], CO["rden1"], op=Op.mult),
                        reads=[COK["g1a"], COK["rden1"]])
                    co_op("g1a", lambda v: v.tensor_scalar(
                        CO["g1a"], CO["g1a"], SCALE_B, None, op0=Op.mult),
                        reads=[COK["g1a"]])
                    co_op("g1b", lambda v: v.tensor_tensor(
                        CO["g1b"], CO["c11"], CO["b1p"], op=Op.mult),
                        reads=[COK["c11"], COK["b1p"]])
                    co_op("g1b", lambda v: v.tensor_tensor(
                        CO["g1b"], CO["g1b"], CO["rden1"], op=Op.mult),
                        reads=[COK["g1b"], COK["rden1"]])
                    co_op("g1b", lambda v: v.tensor_scalar(
                        CO["g1b"], CO["g1b"], SCALE_B, None, op0=Op.mult),
                        reads=[COK["g1b"]])
                    co_op("gt", lambda v: v.tensor_tensor(
                        CO["gt"], CO["ck"], CO["sg0"], op=Op.mult),
                        reads=[COK["ck"], COK["sg0"]])
                    co_op("gt", lambda v: v.tensor_scalar(
                        CO["gt"], CO["gt"], -1.0, None, op0=Op.mult),
                        reads=[COK["gt"]])
                    for hh in range(NHEAD):
                        v0 = qt[0][:, hh * 193 + 128:hh * 193 + 192]
                        v1 = qt[1][:, hh * 193 + 128:hh * 193 + 192]
                        t.op("vector", lambda v, hh=hh, v0=v0: v.tensor_scalar(
                            lay[0][:, hh * 64:(hh + 1) * 64], v0,
                            CO["g0"][:, hh:hh + 1], None, op0=Op.mult),
                            reads=[("qt", 0), COK["g0"]], writes=[("lay", 0)],
                            inc=(hh == NHEAD - 1))
                        t.op("vector", lambda v, hh=hh, v0=v0, v1=v1: v.scalar_tensor_tensor(
                            pool[0][:, 0:64], v0, CO["gt"][:, hh:hh + 1], v1,
                            op0=Op.mult, op1=Op.add),
                            reads=[("qt", 0), ("qt", 1), COK["gt"]],
                            writes=[("bpool", 0)])
                        t.op("vector", lambda v, hh=hh, v0=v0: v.tensor_scalar(
                            pool[1][:, 0:64], v0, CO["g1a"][:, hh:hh + 1], None,
                            op0=Op.mult),
                            reads=[("qt", 0), COK["g1a"]], writes=[("bpool", 1)])
                        t.op("vector", lambda v, hh=hh: v.scalar_tensor_tensor(
                            lay[1][:, hh * 64:(hh + 1) * 64], pool[0][:, 0:64],
                            CO["g1b"][:, hh:hh + 1], pool[1][:, 0:64],
                            op0=Op.mult, op1=Op.add),
                            reads=[("bpool", 0), ("bpool", 1), COK["g1b"]],
                            writes=[("lay", 1)], inc=(hh == NHEAD - 1))
                    for si in range(2):
                        for ch in range(KC):
                            t.op("tensor", lambda te, si=si, ch=ch: te.transpose(
                                pbT[:, 0:128], lay[si][:, ch * 128:(ch + 1) * 128],
                                identB[:]),
                                reads=[("lay", si), ("iB",)], writes=[("bpsT",)])
                            t.op("vector", lambda v, si=si, ch=ch, sub=sub: v.tensor_copy(
                                layfm[si][:, ch * TTA + sub * 128:ch * TTA + sub * 128 + 128],
                                pbT[:, 0:128]),
                                reads=[("bpsT",)], writes=[("layfm", si, ch, sub)])
                for si, ti in enumerate(s_tiles):
                    sl = slice(ti * TTA, (ti + 1) * TTA)
                    for mc in range(KC):
                        bO = pbb[mc % 2]
                        for kc in range(KC):
                            t.op("tensor", lambda te, si=si, mc=mc, kc=kc, bO=bO: te.matmul(
                                bO[:], wob_bf[:, kc * D + mc * 128:kc * D + mc * 128 + 128],
                                layfm[si][:, kc * TTA:(kc + 1) * TTA],
                                start=(kc == 0), stop=(kc == KC - 1)),
                                reads=[("wob", kc, mc // 4)] + [("layfm", si, kc, su)
                                                                for su in range(4)],
                                writes=[pbbk[mc % 2]], inc=(kc == KC - 1))
                        t.op("vector", lambda v, si=si, mc=mc, bO=bO: v.scalar_tensor_tensor(
                            jbB[:, mc * TTA:(mc + 1) * TTA], bO[:], 0.0,
                            xst[si][:, mc * TTA:(mc + 1) * TTA],
                            op0=Op.bypass, op1=Op.add),
                            reads=[pbbk[mc % 2], ("bxst", si)], writes=[("bjb", mc)])
                    y_aps = [jbB[:, c * TTA:(c + 1) * TTA] for c in range(KC)]
                    ykeys = [("bjb", c) for c in range(KC)]
                    ln_stats(y_aps, ykeys, TTA, LNEPS_B, pbb[2:6], pbbk[2:6],
                             srow, srkB, pool[3][:], ("bpool", 3))
                    for c in range(KC):
                        t.op("vector", lambda v, c=c: v.scalar_tensor_tensor(
                            pool[0][:], y_aps[c], 0.0, pbb[4][:, 0:TTA],
                            op0=Op.bypass, op1=Op.subtract),
                            reads=[ykeys[c], pbbk[4]], writes=[("bpool", 0)])
                        t.op("vector", lambda v, c=c: v.scalar_tensor_tensor(
                            pool[1][:], pool[0][:], cgb[:, c:c + 1],
                            pbb[5][:, 0:TTA], op0=Op.mult, op1=Op.mult),
                            reads=[("bpool", 0), pbbk[5], ("cgb",)],
                            writes=[("bpool", 1)])
                        t.op("vector", lambda v, c=c: v.tensor_scalar(
                            pool[2][:], pool[1][:], cgb[:, 8 + c:8 + c + 1], None,
                            op0=Op.add),
                            reads=[("bpool", 1), ("cgb",)], writes=[("bpool", 2)])
                        t.dma(lx, pool[0][:], x2_v[:, c, sl],
                              reads=[("x2", c, 2 * ti), ("x2", c, 2 * ti + 1)],
                              writes=[("bpool", 0)])
                        t.op("vector", lambda v, c=c: v.tensor_tensor(
                            pool[2][:], pool[2][:], pool[0][:], op=Op.add),
                            reads=[("bpool", 2), ("bpool", 0)], writes=[("bpool", 2)])
                        t.dma(sx, x3_v[:, c, sl], pool[2][:],
                              reads=[("bpool", 2)],
                              writes=[("x3", c, 2 * ti), ("x3", c, 2 * ti + 1)])
            t.barrier(STREAMS)

        stop_now = stop_now or (upto == "ab")
        if not stop_now:
            ln_junction("j3", x3_v, "x3", None, None, None, None, n3_v, "n3", EPS_MOD)
        stop_now = stop_now or (upto == "j3")
        if not stop_now:
            moe_call("m2", n3_v, "n3", x3_v, "x3", out_v, "outk")

        t.barrier(STREAMS)

    es.close()
    nc.finalize()
    return nc



def _prep(inputs):
    f32 = np.float32
    g = lambda k: np.ascontiguousarray(np.asarray(inputs[k], dtype=f32))
    Wq, Wk, Wv = g("L0_attn_a_Wq"), g("L0_attn_a_Wk"), g("L0_attn_a_Wv")

    def cb(W):
        return np.ascontiguousarray(
            W.reshape(NB, D, KC, 128).transpose(0, 2, 1, 3))

    cp = ConstPack()
    for i in range(NB):
        cp.add(("bq", i), inputs["L0_attn_a_bq"][i])
        cp.add(("bk", i), inputs["L0_attn_a_bk"][i])
        cp.add(("bv", i), inputs["L0_attn_a_bv"][i])
        cp.add(("bo", i), inputs["L0_attn_a_bo"][i])
        cp.add(("bff2", i), inputs["L0_attn_a_b2"][i])
        cp.add(("ln1s", i), inputs["L0_attn_a_ln1_s"][i])
        cp.add(("ln1b", i), inputs["L0_attn_a_ln1_b"][i])
        cp.add(("ln2s", i), inputs["L0_attn_a_ln2_s"][i])
        cp.add(("ln2b", i), inputs["L0_attn_a_ln2_b"][i])
        cp.add(("bff1", i), inputs["L0_attn_a_b1"][i])
    cp.add(("bg",), inputs["L0_attn_b_ln_g"])
    cp.add(("bb",), inputs["L0_attn_b_ln_b"])
    consts = cp.array()
    assert cp.index == CINDEX, "const layout mismatch"
    assert consts.shape[1] == NCONST

    up_all = np.concatenate([
        np.asarray(inputs["L0_ffn_routed_up"], dtype=f32),
        np.asarray(inputs["L0_ffn_shared_up"], dtype=f32)], axis=0)
    down_all = np.concatenate([
        np.asarray(inputs["L0_ffn_routed_down"], dtype=f32),
        np.asarray(inputs["L0_ffn_shared_down"], dtype=f32)], axis=0)

    shared = {
        "wq_cb": cb(Wq), "wk_cb": cb(Wk), "wv_cb": cb(Wv),
        "wo_w": g("L0_attn_a_Wo"), "w1_w": g("L0_attn_a_W1"),
        "w2_w": g("L0_attn_a_W2"),
        "router_t": np.ascontiguousarray(g("L0_ffn_router_w").T),
        "up_t": np.ascontiguousarray(up_all.transpose(0, 2, 1)),
        "down_t": np.ascontiguousarray(down_all.transpose(0, 2, 1)),
        "wqkvb_t": np.ascontiguousarray(g("L0_attn_b_W_qkvb").T),
        "wob_t": np.ascontiguousarray(g("L0_attn_b_W_o").T),
        "consts": consts,
    }
    x = np.asarray(inputs["L0_attn_a_x"], dtype=f32)
    xs = []
    for c in range(NCORE):
        xc = x[:, c * 1024:(c + 1) * 1024, :].reshape(T, D)
        xs.append(np.ascontiguousarray(xc.T))
    return shared, xs


def kernel(**inputs):
    shared, xs = _prep(inputs)
    nc = build_program()
    in_maps = [dict(shared, x0=xs[c]) for c in range(NCORE)]
    res = run_bass_kernel_spmd(nc, in_maps, core_ids=list(range(NCORE)))
    out = np.empty((2, 8192, D), np.float32)
    for c in range(NCORE):
        oc = np.asarray(res.results[c]["out"])
        out[:, c * 1024:(c + 1) * 1024, :] = oc.T.reshape(2, 1024, D)
    return out


# version 19 (final)
# speedup vs baseline: 1.1838x; 1.1838x over previous
import contextlib
import numpy as np
import ml_dtypes
import concourse.bass as bass
import concourse.mybir as mybir
from concourse.bass_utils import run_bass_kernel_spmd
from concourse.alu_op_type import AluOpType as Op

F32 = mybir.dt.float32
BF16 = mybir.dt.bfloat16
AF = mybir.ActivationFunctionType

NCORE = 8
D = 1024
KC = 8
T = 2048
TTA = 512
NTA = T // TTA
TTB = 256
NTB = T // TTB
FF = 4096
FFC = 32
NB = 8
NEXP = 18
EG = [list(range(0, 4)), list(range(4, 8)), list(range(8, 12)),
      list(range(12, 16)), [16, 17]]
NHEAD = 16
QKVB_O = 3088
EPS_LN_A = 1e-6
EPS_MOD = 1e-5
EPS_B = 1e-5
LNEPS_B = 1e-5
NEG_LN_EPS_ATTN = float(-np.log(1e-6))
HOIST_V = False
SCALE_B = 1.0 / 8.0
REC1 = float(1.0 / (1.0 + EPS_B))
GELU_T = AF.Gelu_apprx_tanh
GELU_E = AF.Gelu



class DmaStream:
    def __init__(self, nc, es, name, engine="sync", step=16):
        self.name = name
        self.engine = engine
        self.step = step
        self.sem = es.enter_context(nc.semaphore(f"dma_{name}"))
        self.count = 0


class Trk:
    ENGINES = ("tensor", "vector", "scalar", "gpsimd", "sync")

    def __init__(self, nc, es):
        self.nc = nc
        self.sem = {e: es.enter_context(nc.semaphore(f"prog_{e}"))
                    for e in self.ENGINES}
        self.cnt = {e: 0 for e in self.ENGINES}
        self.known = {}
        self.buf = {}
        self.eng = {e: getattr(nc, e) for e in self.ENGINES}

    def _wait(self, engine, dep):
        if dep[0] == "eng":
            _, e, c = dep
            if e == engine and c > self.cnt[e]:
                return
            k = (engine, "e", e)
            if self.known.get(k, 0) >= c:
                return
            self.eng[engine].wait_ge(self.sem[e], c)
            self.known[k] = c
        else:
            _, stream, c = dep
            c = max(c, stream.step * stream.count)
            k = (engine, "d", stream.name)
            if self.known.get(k, 0) >= c:
                return
            self.eng[engine].wait_ge(stream.sem, c)
            self.known[k] = c

    def depend(self, engine, reads=(), writes=()):
        deps = []
        for key in reads:
            b = self.buf.get(key)
            if b and b["w"] is not None:
                deps.append(b["w"])
        for key in writes:
            b = self.buf.get(key)
            if b:
                if b["w"] is not None:
                    deps.append(b["w"])
                deps.extend(b["r"])
        best = {}
        for d in deps:
            k = (d[0], d[1] if d[0] == "eng" else d[1].name)
            if k not in best or best[k][2] < d[2]:
                best[k] = d
        for d in best.values():
            self._wait(engine, d)

    def _record(self, dep, reads, writes):
        for key in writes:
            self.buf[key] = {"w": dep, "r": []}
        for key in reads:
            b = self.buf.setdefault(key, {"w": None, "r": []})
            b["r"].append(dep)

    def op(self, engine, emit, reads=(), writes=(), inc=True):
        self.depend(engine, reads, writes)
        inst = emit(self.eng[engine])
        if inc:
            self.cnt[engine] += 1
            inst.then_inc(self.sem[engine], 1)
            dep = ("eng", engine, self.cnt[engine])
        else:
            dep = ("eng", engine, self.cnt[engine] + 1)
        self._record(dep, reads, writes)
        return inst

    def dma(self, stream, out, in_, reads=(), writes=()):
        self.depend(stream.engine, reads, writes)
        inst = self.eng[stream.engine].dma_start(out, in_)
        stream.count += 1
        inst.then_inc(stream.sem, stream.step)
        self._record(("dma", stream, stream.step * stream.count), reads, writes)
        return inst

    def cc(self, stream, kind, op, replica_groups, ins, outs, reads=(), writes=()):
        self.depend(stream.engine, reads, writes)
        inst = self.eng[stream.engine].collective_compute(
            kind, op, replica_groups=replica_groups, ins=ins, outs=outs)
        stream.count += 1
        inst.then_inc(stream.sem, stream.step)
        self._record(("dma", stream, stream.step * stream.count), reads, writes)
        return inst

    def barrier(self, streams=()):
        for e in self.ENGINES:
            for f in self.ENGINES:
                if f != e and self.cnt[f] > 0:
                    self._wait(e, ("eng", f, self.cnt[f]))
            for s in streams:
                if s.count > 0:
                    self._wait(e, ("dma", s, s.step * s.count))



class ConstPack:
    def __init__(self):
        self.cols = []
        self.index = {}

    def add(self, name, vec):
        vec = np.asarray(vec, dtype=np.float32).reshape(-1)
        assert vec.size % 128 == 0
        n = vec.size // 128
        self.index[name] = (len(self.cols), n)
        arr = vec.reshape(n, 128).T
        for j in range(n):
            self.cols.append(np.ascontiguousarray(arr[:, j]))

    def array(self):
        return np.ascontiguousarray(np.stack(self.cols, axis=1))


def make_const_index():
    idx = {}
    pos = 0

    def add(name, n):
        nonlocal pos
        idx[name] = (pos, n)
        pos += n
    for i in range(NB):
        for nm in ("bq", "bk", "bv", "bo", "bff2", "ln1s", "ln1b", "ln2s", "ln2b"):
            add((nm, i), KC)
        add(("bff1", i), FFC)
    add(("bg",), KC)
    add(("bb",), KC)
    return idx, pos


CINDEX, NCONST = make_const_index()



def build_program(dump=(), upto=None, nblocks=NB, no_cc=False):
    nc = bass.Bass("TRN2", target_bir_lowering=False)
    es = contextlib.ExitStack()

    def din(name, shape):
        return nc.dram_tensor(name, list(shape), F32, kind="ExternalInput")

    lean = (upto == "j0")
    x0 = din("x0", (D, T))
    if not lean:
        wq_cb = din("wq_cb", (NB, KC, D, 128))
        wk_cb = din("wk_cb", (NB, KC, D, 128))
        wv_cb = din("wv_cb", (NB, KC, D, 128))
        wo_w = din("wo_w", (NB, D, D))
        w1_w = din("w1_w", (NB, D, FF))
        w2_w = din("w2_w", (NB, FF, D))
        router_t = din("router_t", (D, 16))
        up_t = din("up_t", (NEXP, D, 512))
        down_t = din("down_t", (NEXP, 512, D))
        wqkvb_t = din("wqkvb_t", (D, QKVB_O))
        wob_t = din("wob_t", (D, D))
        consts_in = din("consts", (128, NCONST))
    out_t = nc.dram_tensor("out", [D, T], F32, kind="ExternalOutput")

    def scrt(name, shape, dt):
        kind = "ExternalOutput" if name in dump else "Internal"
        return nc.dram_tensor(name, list(shape), dt, kind=kind)

    scr_stream = scrt("scr_stream", (D, T), BF16)
    scr_af32 = scrt("scr_af32", (D, T), F32)
    scr_q = scrt("scr_q", (D, T), BF16)
    scr_x1 = scrt("scr_x1", (D, T), F32)
    scr_x2 = scrt("scr_x2", (D, T), F32)
    scr_x3 = scrt("scr_x3", (D, T), F32)
    scr_n1 = scrt("scr_n1", (D, T), BF16)
    scr_n2 = scrt("scr_n2", (D, T), BF16)
    scr_n3 = scrt("scr_n3", (D, T), BF16)
    scr_w1c = nc.dram_tensor("scr_w1c", [2, 128, KC * FF], BF16)
    scr_w2c = nc.dram_tensor("scr_w2c", [2, 128, FFC * D], BF16)
    scr_woc = nc.dram_tensor("scr_woc", [2, 128, KC * D], BF16)
    scr_qkvc = nc.dram_tensor("scr_qkvc", [NB, 3, KC, 128, KC * 128], BF16)
    scr_wupc = nc.dram_tensor("scr_wupc", [NEXP, 128, KC * 512], BF16)
    scr_wdnc = nc.dram_tensor("scr_wdnc", [NEXP, 128, 4 * D], BF16)
    scr_qkvbc = nc.dram_tensor("scr_qkvbc", [128, KC * QKVB_O], BF16)
    scr_wobc = nc.dram_tensor("scr_wobc", [128, KC * D], BF16)
    cc_in = nc.dram_tensor("cc_in", [128, 32], F32)
    cc_out = nc.dram_tensor("cc_out", [128, 32], F32)

    def fmview(t_):
        return t_.ap().rearrange("(kc p) t -> p kc t", p=128)

    x0_v = fmview(x0)
    strm_v = fmview(scr_stream)
    af32_v = fmview(scr_af32)
    q_v = fmview(scr_q)
    x1_v = fmview(scr_x1)
    x2_v = fmview(scr_x2)
    x3_v = fmview(scr_x3)
    n1_v = fmview(scr_n1)
    n2_v = fmview(scr_n2)
    n3_v = fmview(scr_n3)
    out_v = fmview(out_t)

    sb = lambda name, shape, dt: es.enter_context(
        nc.sbuf_tensor(name, list(shape), dt))
    ones_col = sb("ones_col", (128, 1), F32)
    ones_row = sb("ones_row", (1, 128), F32)
    ones_col_bf = sb("ones_col_bf", (128, 1), BF16)
    wst2 = [sb(f"wst2_{i}", (128, 512), F32) for i in range(2)]
    wbst = [sb(f"wbst{i}", (128, 512), BF16) for i in range(2)]
    fconst = sb("fconst", (128, 4), F32)

    t = Trk(nc, es)
    ws1 = DmaStream(nc, es, "ws1", "sync")
    ws2 = DmaStream(nc, es, "ws2", "sync")
    wso = DmaStream(nc, es, "wso", "sync")
    wcb = DmaStream(nc, es, "wcb", "scalar")
    wmo = DmaStream(nc, es, "wmo", "sync")
    wci = DmaStream(nc, es, "wci", "sync")
    wco = DmaStream(nc, es, "wco", "gpsimd")
    lx = DmaStream(nc, es, "lx", "scalar")
    sx = DmaStream(nc, es, "sx", "scalar")
    ccs = DmaStream(nc, es, "ccs", "gpsimd", step=1)
    STREAMS = [ws1, ws2, wso, wcb, wmo, wci, wco, lx, sx, ccs]

    with nc.Block():
        t.op("vector", lambda v: v.memset(ones_col[:], 1.0), writes=[("ones_col",)])
        t.op("vector", lambda v: v.memset(ones_row[:], 1.0), writes=[("ones_row",)])
        t.op("vector", lambda v: v.memset(ones_col_bf[:], 1.0), writes=[("ones_cb",)])
        for i, val in enumerate((EPS_MOD, EPS_LN_A, NEG_LN_EPS_ATTN)):
            t.op("vector", lambda v, i=i, val=val: v.memset(
                fconst[:, i:i + 1], val), writes=[("fconst",)])
            nc.const_aps.register(F32, val, fconst[:, i:i + 1])

        def cast(dst_ap, src_ap, reads, writes, engine="gpsimd"):
            if engine == "scalar":
                t.op(engine, lambda e: e.activation(dst_ap, src_ap, AF.Copy),
                     reads=reads, writes=writes)
            else:
                t.op(engine, lambda e: e.tensor_copy(dst_ap, src_ap),
                     reads=reads, writes=writes)

        cast_rr = [0]
        CAST_ENGS = ("gpsimd",)

        def load_cast2(stream, dram_ap, dst_ap, dkey, slot, width):
            st = wst2[slot % 2]
            skey = ("wst2", slot % 2)
            t.dma(stream, st[:, 0:width], dram_ap, writes=[skey])
            eng = CAST_ENGS[cast_rr[0] % len(CAST_ENGS)]
            cast_rr[0] += 1
            cast(dst_ap, st[:, 0:width], reads=[skey], writes=[dkey], engine=eng)

        conv_queue = []
        conv_rr = [0]
        refilled_w = [-1]

        def conv_pump(n):
            done = 0
            i = 0
            while done < n and i < len(conv_queue):
                gate, src_ap, dst_ap, dkey, width = conv_queue[i]
                if gate is not None and gate > refilled_w[0] + 2:
                    i += 1
                    continue
                conv_queue.pop(i)
                slot = conv_rr[0] % 2
                conv_rr[0] += 1
                t.dma(wci, wst2[slot][:, 0:width], src_ap,
                      writes=[("wst2", slot)])
                eng = CAST_ENGS[cast_rr[0] % len(CAST_ENGS)]
                cast_rr[0] += 1
                cast(wbst[slot][:, 0:width], wst2[slot][:, 0:width],
                     reads=[("wst2", slot)], writes=[("wbst", slot)],
                     engine=eng)
                t.dma(wco, dst_ap, wbst[slot][:, 0:width],
                      reads=[("wbst", slot)], writes=[dkey])
                done += 1

        def ln_stats(y_aps, ykeys, W, eps, pbs, pbs_keys, srow, srow_keys,
                     sq_buf, sq_key0, sq_full=None, y_full=None):
            ksum, ksq, kmu_b, kr_b = pbs_keys
            kmu, kr = srow_keys[0], srow_keys[1]
            if sq_full is not None:
                fullap, fullkeys, sq_chunk0 = sq_full
                t.op("scalar", lambda s: s.activation(
                    fullap, y_full, AF.Square),
                    reads=ykeys, writes=fullkeys)
                sq_is_bf = True
                sq_chunk = sq_chunk0
                sq_key = lambda c: fullkeys[c]
            else:
                sq_is_bf = False
                sq_chunk = lambda c: sq_buf
                sq_key = lambda c: sq_key0
            for c in range(KC):
                if sq_full is None:
                    t.op("scalar", lambda s, c=c: s.activation(
                        sq_buf, y_aps[c], AF.Square),
                        reads=[ykeys[c]], writes=[sq_key0])
                t.op("tensor", lambda te, c=c: te.matmul(
                    pbs[0][0:1, 0:W], ones_col[:, 0:1], y_aps[c],
                    start=(c == 0), stop=(c == KC - 1)),
                    reads=[ykeys[c], ("ones_col",)], writes=[ksum])
                ones_sq = ones_col_bf if sq_is_bf else ones_col
                okey = ("ones_cb",) if sq_is_bf else ("ones_col",)
                t.op("tensor", lambda te, c=c, ones_sq=ones_sq: te.matmul(
                    pbs[1][0:1, 0:W], ones_sq[:, 0:1], sq_chunk(c),
                    start=(c == 0), stop=(c == KC - 1)),
                    reads=[sq_key(c), okey], writes=[ksq])
            mu = srow[0:1, 0:W]
            sc = srow[0:1, W:2 * W]
            rq = srow[0:1, 2 * W:3 * W]
            tq = srow[0:1, 3 * W:4 * W]
            I32 = mybir.dt.int32
            MAGIC = 0x5F3759DF
            t.op("vector", lambda v: v.tensor_scalar(
                mu, pbs[0][0:1, 0:W], 1.0 / D, None, op0=Op.mult),
                reads=[ksum], writes=[kmu])
            t.op("vector", lambda v: v.tensor_tensor(sc, mu, mu, op=Op.mult),
                 reads=[kmu], writes=[kr])
            t.op("vector", lambda v: v.scalar_tensor_tensor(
                sc, pbs[1][0:1, 0:W], 1.0 / D, sc, op0=Op.mult, op1=Op.subtract),
                reads=[ksq, kr], writes=[kr])
            t.op("vector", lambda v: v.tensor_scalar(
                sc, sc, eps, None, op0=Op.add), reads=[kr], writes=[kr])
            t.op("vector", lambda v: v.tensor_scalar(
                rq.bitcast(I32), sc.bitcast(I32), 1, None,
                op0=Op.arith_shift_right), reads=[kr], writes=[kr])
            t.op("vector", lambda v: v.tensor_scalar(
                rq.bitcast(I32), rq.bitcast(I32), -1, MAGIC,
                op0=Op.mult, op1=Op.add), reads=[kr], writes=[kr])
            for _ in range(2):
                t.op("vector", lambda v: v.tensor_tensor(tq, rq, rq, op=Op.mult),
                     reads=[kr], writes=[kr])
                t.op("vector", lambda v: v.tensor_tensor(tq, tq, sc, op=Op.mult),
                     reads=[kr], writes=[kr])
                t.op("vector", lambda v: v.tensor_scalar(
                    tq, tq, -0.5, 1.5, op0=Op.mult, op1=Op.add),
                    reads=[kr], writes=[kr])
                t.op("vector", lambda v: v.tensor_tensor(rq, rq, tq, op=Op.mult),
                     reads=[kr], writes=[kr])
            t.op("tensor", lambda te: te.matmul(
                pbs[2][:, 0:W], ones_row[0:1, :], mu, start=True, stop=True),
                reads=[kmu, ("ones_row",)], writes=[kmu_b])
            t.op("tensor", lambda te: te.matmul(
                pbs[3][:, 0:W], ones_row[0:1, :], rq, start=True, stop=True),
                reads=[kr, ("ones_row",)], writes=[kr_b])

        def ln_junction(tag, xin_v, xin_pref, add_v, add_pref, xsum_v, xsum_pref,
                        nout_v, nout_pref, eps):
            with contextlib.ExitStack() as seg:
                jb = seg.enter_context(nc.sbuf_tensor(f"jb_{tag}", [128, KC * TTA], F32))
                pool = [seg.enter_context(nc.sbuf_tensor(f"jpool_{tag}{i}", [128, TTA], F32))
                        for i in range(5)]
                srow = seg.enter_context(nc.sbuf_tensor(f"jsrow_{tag}", [1, 2048], F32))
                ob = [seg.enter_context(nc.sbuf_tensor(f"job_{tag}{i}", [128, TTA], BF16))
                      for i in range(4)]
                pbs = [seg.enter_context(nc.psum_tensor(f"jps_{tag}{i}", [128, 512], F32))
                       for i in range(4)]
                pk = [("jps", i) for i in range(4)]
                sk = [("jsrow", i) for i in range(4)]
                for ti in range(NTA):
                    sl = slice(ti * TTA, (ti + 1) * TTA)
                    for c in range(KC):
                        ysl = jb[:, c * TTA:(c + 1) * TTA]
                        if add_v is None:
                            t.dma(lx, ysl, xin_v[:, c, sl],
                                  reads=[(xin_pref, c, 2 * ti + h) for h in range(2)]
                                  if xin_pref else [],
                                  writes=[("jb", c)])
                        else:
                            t.dma(lx, pool[c % 2][:], xin_v[:, c, sl],
                                  reads=[(xin_pref, c, 2 * ti + h) for h in range(2)]
                                  if xin_pref else [],
                                  writes=[("jpool", c % 2)])
                            t.dma(lx, pool[2 + c % 2][:], add_v[:, c, sl],
                                  reads=[(add_pref, c, 2 * ti + h) for h in range(2)]
                                  if add_pref else [],
                                  writes=[("jpool", 2 + c % 2)])
                            t.op("vector", lambda v, c=c: v.tensor_tensor(
                                jb[:, c * TTA:(c + 1) * TTA], pool[c % 2][:],
                                pool[2 + c % 2][:], op=Op.add),
                                reads=[("jpool", c % 2), ("jpool", 2 + c % 2)],
                                writes=[("jb", c)])
                        if xsum_v is not None:
                            t.dma(sx, xsum_v[:, c, sl], jb[:, c * TTA:(c + 1) * TTA],
                                  reads=[("jb", c)],
                                  writes=[(xsum_pref, c, 2 * ti + h) for h in range(2)])
                    y_aps = [jb[:, c * TTA:(c + 1) * TTA] for c in range(KC)]
                    ykeys = [("jb", c) for c in range(KC)]
                    ln_stats(y_aps, ykeys, TTA, eps, pbs, pk, srow, sk,
                             pool[4][:], ("jpool", 4))
                    for c in range(KC):
                        t.op("vector", lambda v, c=c: v.scalar_tensor_tensor(
                            pool[c % 4][:], y_aps[c], 0.0, pbs[2][:, 0:TTA],
                            op0=Op.bypass, op1=Op.subtract),
                            reads=[ykeys[c], pk[2]], writes=[("jpool", c % 4)])
                        t.op("vector", lambda v, c=c: v.scalar_tensor_tensor(
                            ob[c % 4][:], pool[c % 4][:], 0.0, pbs[3][:, 0:TTA],
                            op0=Op.bypass, op1=Op.mult),
                            reads=[("jpool", c % 4), pk[3]], writes=[("job", c % 4)])
                        t.dma(sx, nout_v[:, c, sl], ob[c % 4][:],
                              reads=[("job", c % 4)],
                              writes=[(nout_pref, c, 2 * ti + h) for h in range(2)])
                t.barrier(STREAMS)

        ln_junction("j0", x0_v, None, None, None, None, None, strm_v, "strm", EPS_MOD)
        stop_now = (upto == "j0")

        if not stop_now:
         with contextlib.ExitStack() as seg:
            ssb = lambda name, shape, dt: seg.enter_context(
                nc.sbuf_tensor(name, list(shape), dt))
            w1_bf = ssb("w1_bf", (128, KC * FF), BF16)
            w2_bf = ssb("w2_bf", (128, FFC * D), BF16)
            wo_bf = ssb("wo_bf", (128, KC * D), BF16)
            wcb_bf = [ssb(f"wcb_bf{i}", (128, KC * 128), BF16) for i in range(2)]
            xa_bf = [ssb(f"xa_bf{i}", (128, KC * TTA), BF16) for i in range(1)]
            k_sc = ssb("k_sc", (128, TTA), F32)
            pool = [ssb(f"apool{i}", (128, TTA), F32) for i in range(2)]
            bfs = ssb("bfs", (128, TTA), BF16)
            kv_acc = ssb("kv_acc", (128, 32), F32)
            ks_acc = ssb("ks_acc", (128, 32), F32)
            kvks = ssb("kvks", (128, 32), F32)
            kvred = ssb("kvred", (128, 32), F32)
            ksr = ssb("ksr", (128, 16), F32)
            kvdks = ssb("kvdks", (128, 16), F32)
            y1 = ssb("y1", (128, KC * TTB), F32)
            xb_bf = ssb("xb_bf", (128, KC * TTB), BF16)
            qb_bf = ssb("qb_bf", (128, KC * TTB), BF16)
            h_bf = ssb("h_bf", (128, FFC * TTB), BF16)
            srow = ssb("asrow", (1, 1024), F32)
            cab = [ssb(f"cab{i}", (128, 104), F32) for i in range(2)]
            pb = [seg.enter_context(nc.psum_tensor(f"apb{i}", [128, 512], F32))
                  for i in range(8)]
            pbk = [("apb", i) for i in range(8)]
            srk = [("asrow", 0), ("asrow", 1)]
            pA, pB = pool[0][:, 0:TTB], pool[0][:, TTB:2 * TTB]
            pC, pD = pool[1][:, 0:TTB], pool[1][:, TTB:2 * TTB]
            kA, kB, kC, kD = ("ap0a",), ("ap0b",), ("ap1a",), ("ap1b",)
            k0full = [kA, kB]
            k1full = [kC, kD]
            LOCAL = {"bq": 0, "bk": 8, "bv": 16, "bo": 24, "bff2": 32,
                     "ln1s": 40, "ln1b": 48, "ln2s": 56, "ln2b": 64, "bff1": 72}

            def load_cab(blk):
                t.dma(wmo, cab[blk % 2][:],
                      consts_in.ap()[:, blk * 104:(blk + 1) * 104],
                      writes=[("cab", blk % 2)])

            WQKV = {"q": wq_cb, "k": wk_cb, "v": wv_cb}
            WQI = {"q": 0, "k": 1, "v": 2}

            def enq_conv_block(blk):
                for which in ("k", "v", "q"):
                    wi = WQI[which]
                    for mc in range(KC):
                        ap = WQKV[which].ap()[blk, mc].rearrange(
                            "(kc p) c -> p kc c", p=128)
                        for hf in range(2):
                            conv_queue.append((
                                None,
                                ap[:, 4 * hf:4 * hf + 4, :],
                                scr_qkvc.ap()[blk, wi, mc][:, hf * 512:(hf + 1) * 512],
                                ("cqkv", blk, wi, mc, hf), 512))
                for kc in range(KC):
                    ap = wo_w.ap()[blk, kc * 128:(kc + 1) * 128, :]
                    for hf in range(2):
                        conv_queue.append((
                            blk,
                            ap[:, hf * 512:(hf + 1) * 512],
                            scr_woc.ap()[blk % 2][:, kc * D + hf * 512:kc * D + (hf + 1) * 512],
                            ("cwo", blk % 2, 2 * kc + hf), 512))
                for kc in range(KC):
                    for q in range(8):
                        conv_queue.append((
                            blk,
                            w1_w.ap()[blk, kc * 128:(kc + 1) * 128, q * 512:(q + 1) * 512],
                            scr_w1c.ap()[blk % 2][:, kc * FF + q * 512:kc * FF + (q + 1) * 512],
                            ("cw1", blk % 2, kc * 8 + q), 512))
                for kh in range(FFC):
                    for hf in range(2):
                        conv_queue.append((
                            blk,
                            w2_w.ap()[blk, kh * 128:(kh + 1) * 128, hf * 512:(hf + 1) * 512],
                            scr_w2c.ap()[blk % 2][:, kh * D + hf * 512:kh * D + (hf + 1) * 512],
                            ("cw2", blk % 2, 2 * kh + hf), 512))

            def refill_qkv(which, blk, mc, slot):
                wi = WQI[which]
                t.dma(wcb, wcb_bf[slot][:], scr_qkvc.ap()[blk, wi, mc],
                      reads=[("cqkv", blk, wi, mc, hf) for hf in range(2)],
                      writes=[("wcb", slot, hf) for hf in range(4)])

            def refill_wo(blk):
                t.dma(wso, wo_bf[:], scr_woc.ap()[blk % 2],
                      reads=[("cwo", blk % 2, i) for i in range(16)],
                      writes=[("wo", kc, hf) for kc in range(KC) for hf in range(2)])

            def refill_w1(blk):
                t.dma(ws1, w1_bf[:], scr_w1c.ap()[blk % 2],
                      reads=[("cw1", blk % 2, i) for i in range(64)],
                      writes=[("w1", kc, q) for kc in range(KC) for q in range(8)])

            def refill_w2(blk):
                t.dma(ws2, w2_bf[:], scr_w2c.ap()[blk % 2],
                      reads=[("cw2", blk % 2, i) for i in range(64)],
                      writes=[("w2", kh, hf) for kh in range(FFC) for hf in range(2)])
                refilled_w[0] = max(refilled_w[0], blk)

            for b_ in range(NB):
                enq_conv_block(b_)
            for r_ in range(NEXP):
                apu = up_t.ap()[r_].rearrange("(kc p) e -> p kc e", p=128)
                for q in range(8):
                    conv_queue.append((None, apu[:, q:q + 1, :],
                                       scr_wupc.ap()[r_][:, q * 512:(q + 1) * 512],
                                       ("cwup", r_, q), 512))
                apd = down_t.ap()[r_].rearrange("(kc p) h -> p kc h", p=128)
                for q in range(8):
                    conv_queue.append((
                        None,
                        apd[:, q // 2:q // 2 + 1, (q % 2) * 512:(q % 2) * 512 + 512],
                        scr_wdnc.ap()[r_][:, q * 512:(q + 1) * 512],
                        ("cwdn", r_, q), 512))
            for kc in range(KC):
                ap = wqkvb_t.ap()[kc * 128:(kc + 1) * 128, :]
                for q in range(7):
                    w = min(512, QKVB_O - q * 512)
                    conv_queue.append((
                        None,
                        ap[:, q * 512:q * 512 + w],
                        scr_qkvbc.ap()[:, kc * QKVB_O + q * 512:kc * QKVB_O + q * 512 + w],
                        ("cqkvb", kc, q), w))
            for kc in range(KC):
                ap = wob_t.ap()[kc * 128:(kc + 1) * 128, :]
                for hf in range(2):
                    conv_queue.append((
                        None,
                        ap[:, hf * 512:(hf + 1) * 512],
                        scr_wobc.ap()[:, kc * D + hf * 512:kc * D + (hf + 1) * 512],
                        ("cwob", kc, hf), 512))

            load_cab(0)
            conv_pump(64)

            for blk in range(nblocks):
                cn = lambda name, j=0: cab[blk % 2][:, LOCAL[name] + j:LOCAL[name] + j + 1]
                ck = ("cab", blk % 2)

                for ti in range(NTA):
                    sl = slice(ti * TTA, (ti + 1) * TTA)
                    t.dma(lx, xa_bf[0][:].rearrange(
                        "p (kc c) -> p kc c", kc=KC), strm_v[:, :, sl],
                        reads=[("strm", c, 2 * ti + h)
                               for c in range(KC) for h in range(2)],
                        writes=[("xa", 0)])
                    seq = []
                    for c in range(KC):
                        seq += [("k", c), ("v", c)]
                    seq += [("q", mc) for mc in range(KC)]
                    refill_qkv(seq[0][0], blk, seq[0][1], 0)
                    refill_qkv(seq[1][0], blk, seq[1][1], 1)
                    for c in range(KC):
                        iK, iV = 2 * c, 2 * c + 1
                        bK = pb[c % 2]
                        for kc in range(KC):
                            t.op("tensor", lambda te, kc=kc, c=c, bK=bK: te.matmul(
                                bK[:], wcb_bf[0][:, kc * 128:(kc + 1) * 128],
                                xa_bf[0][:, kc * TTA:(kc + 1) * TTA],
                                start=(kc == 0), stop=(kc == KC - 1)),
                                reads=[("wcb", 0, kc // 2), ("xa", 0)],
                                writes=[pbk[c % 2]], inc=(kc == KC - 1))
                        t.op("vector", lambda v, bK=bK, c=c: v.tensor_scalar(
                            pool[0][:], bK[:], cn("bk", c), 0.0,
                            op0=Op.add, op1=Op.min),
                            reads=[pbk[c % 2], ck], writes=k0full)
                        t.op("scalar", lambda s: s.activation(
                            pool[0][:], pool[0][:], AF.Exp),
                            reads=k0full, writes=k0full)
                        t.op("vector", lambda v, bK=bK, c=c: v.tensor_scalar(
                            pool[1][:], bK[:], cn("bk", c), 0.0,
                            op0=Op.add, op1=Op.max),
                            reads=[pbk[c % 2], ck], writes=k1full)
                        t.op("vector", lambda v, c=c, ti=ti: v.scalar_tensor_tensor(
                            k_sc[:], pool[1][:], 0.0, pool[0][:],
                            op0=Op.bypass, op1=Op.add,
                            accum_out=ks_acc[:, c * 4 + ti:c * 4 + ti + 1]),
                            reads=k0full + k1full,
                            writes=[("k_sc",), ("ksacc", c, ti)])
                        conv_pump(2)
                        if iK + 2 < len(seq):
                            refill_qkv(seq[iK + 2][0], blk, seq[iK + 2][1], 0)
                        bV = pb[2 + c % 2]
                        for kc in range(KC):
                            t.op("tensor", lambda te, kc=kc, c=c, bV=bV: te.matmul(
                                bV[:], wcb_bf[1][:, kc * 128:(kc + 1) * 128],
                                xa_bf[0][:, kc * TTA:(kc + 1) * TTA],
                                start=(kc == 0), stop=(kc == KC - 1)),
                                reads=[("wcb", 1, kc // 2), ("xa", 0)],
                                writes=[pbk[2 + c % 2]], inc=(kc == KC - 1))
                        t.op("vector", lambda v, bV=bV, c=c, ti=ti: v.scalar_tensor_tensor(
                            k_sc[:], bV[:], cn("bv", c), k_sc[:],
                            op0=Op.add, op1=Op.mult,
                            accum_out=kv_acc[:, c * 4 + ti:c * 4 + ti + 1]),
                            reads=[pbk[2 + c % 2], ck, ("k_sc",)],
                            writes=[("k_sc",), ("kvacc", c, ti)])
                        conv_pump(2)
                        if iV + 2 < len(seq):
                            refill_qkv(seq[iV + 2][0], blk, seq[iV + 2][1], 1)
                    if ti == NTA - 1:
                        accr = [("kvacc", c, tt) for c in range(KC)
                                for tt in range(NTA)] + \
                               [("ksacc", c, tt) for c in range(KC)
                                for tt in range(NTA)]
                        t.op("vector", lambda v: v.tensor_tensor(
                            kvks[:, 0:32:4], kv_acc[:, 0:32:4], kv_acc[:, 1:32:4],
                            op=Op.add), reads=accr, writes=[("kvks",)], inc=False)
                        t.op("vector", lambda v: v.tensor_tensor(
                            kvks[:, 1:32:4], kv_acc[:, 2:32:4], kv_acc[:, 3:32:4],
                            op=Op.add), reads=accr, writes=[("kvks",)], inc=False)
                        t.op("vector", lambda v: v.tensor_tensor(
                            kvks[:, 2:32:4], ks_acc[:, 0:32:4], ks_acc[:, 1:32:4],
                            op=Op.add), reads=accr, writes=[("kvks",)], inc=False)
                        t.op("vector", lambda v: v.tensor_tensor(
                            kvks[:, 3:32:4], ks_acc[:, 2:32:4], ks_acc[:, 3:32:4],
                            op=Op.add), reads=accr, writes=[("kvks",)])
                        t.dma(sx, cc_in.ap(), kvks[:], reads=[("kvks",)],
                              writes=[("ccin",)])
                        if no_cc:
                            t.dma(sx, cc_out.ap(), cc_in.ap(),
                                  reads=[("ccin",)], writes=[("ccout",)])
                        else:
                            t.cc(ccs, "AllReduce", Op.add, [list(range(NCORE))],
                                 [cc_in.ap().opt()], [cc_out.ap().opt()],
                                 reads=[("ccin",)], writes=[("ccout",)])
                        t.dma(lx, kvred[:], cc_out.ap(), reads=[("ccout",)],
                              writes=[("kvred",)])
                        t.op("vector", lambda v: v.reciprocal(
                            ksr[:, 0:16:2], kvred[:, 2:32:4]),
                            reads=[("kvred",)], writes=[("ksr",)], inc=False)
                        t.op("vector", lambda v: v.reciprocal(
                            ksr[:, 1:16:2], kvred[:, 3:32:4]),
                            reads=[("kvred",)], writes=[("ksr",)])
                        t.op("vector", lambda v: v.tensor_tensor(
                            kvdks[:, 0:16:2], kvred[:, 0:32:4], ksr[:, 0:16:2],
                            op=Op.mult),
                            reads=[("kvred",), ("ksr",)], writes=[("kvdks",)],
                            inc=False)
                        t.op("vector", lambda v: v.tensor_tensor(
                            kvdks[:, 1:16:2], kvred[:, 1:32:4], ksr[:, 1:16:2],
                            op=Op.mult),
                            reads=[("kvred",), ("ksr",)], writes=[("kvdks",)])
                    for mc in range(KC):
                        i_q = 16 + mc
                        bQ = pb[4 + mc % 2]
                        for kc in range(KC):
                            t.op("tensor", lambda te, kc=kc, mc=mc, bQ=bQ: te.matmul(
                                bQ[:], wcb_bf[mc % 2][:, kc * 128:(kc + 1) * 128],
                                xa_bf[0][:, kc * TTA:(kc + 1) * TTA],
                                start=(kc == 0), stop=(kc == KC - 1)),
                                reads=[("wcb", mc % 2, kc // 2), ("xa", 0)],
                                writes=[pbk[4 + mc % 2]], inc=(kc == KC - 1))
                        t.op("vector", lambda v, bQ=bQ, mc=mc: v.tensor_scalar(
                            pool[0][:], bQ[:], cn("bq", mc), 0.0,
                            op0=Op.add, op1=Op.min),
                            reads=[pbk[4 + mc % 2], ck], writes=k0full)
                        t.op("scalar", lambda s: s.activation(
                            pool[0][:], pool[0][:], AF.Exp),
                            reads=k0full, writes=k0full)
                        t.op("vector", lambda v, bQ=bQ, mc=mc: v.tensor_scalar(
                            pool[1][:], bQ[:], cn("bq", mc), 0.0,
                            op0=Op.add, op1=Op.max),
                            reads=[pbk[4 + mc % 2], ck], writes=k1full)
                        t.op("vector", lambda v: v.scalar_tensor_tensor(
                            bfs[:], pool[1][:], 0.0, pool[0][:],
                            op0=Op.bypass, op1=Op.add),
                            reads=k0full + k1full, writes=[("bfs",)])
                        t.dma(sx, q_v[:, mc, sl], bfs[:],
                              reads=[("bfs",)],
                              writes=[("scrq", mc, 2 * ti), ("scrq", mc, 2 * ti + 1)])
                        conv_pump(4)
                        if i_q + 2 < len(seq):
                            refill_qkv(seq[i_q + 2][0], blk, seq[i_q + 2][1], mc % 2)


                if blk == 0:
                    refill_wo(0)
                    refill_w1(0)
                    refill_w2(0)
                vw = y1[:]

                def load_qb(j):
                    slj = slice(j * TTB, (j + 1) * TTB)
                    t.dma(lx, qb_bf[:].rearrange("p (kc c) -> p kc c", kc=KC),
                          q_v[:, :, slj],
                          reads=[("scrq", mc, j) for mc in range(KC)],
                          writes=[("qb", c) for c in range(KC)])

                def load_xb(j):
                    slj = slice(j * TTB, (j + 1) * TTB)
                    t.dma(lx, xb_bf[:].rearrange("p (kc c) -> p kc c", kc=KC),
                          strm_v[:, :, slj],
                          reads=[("strm", c, j) for c in range(KC)],
                          writes=[("xb", c) for c in range(KC)])

                def v_stage(j):
                    n = j // 4
                    for c in range(KC):
                        t.op("vector", lambda v, c=c, n=n: v.tensor_scalar(
                            vw[:, c * TTB:(c + 1) * TTB],
                            qb_bf[:, c * TTB:(c + 1) * TTB],
                            kvred[:, c * 4 + 2 + n:c * 4 + 3 + n], None, op0=Op.mult),
                            reads=[("qb", c), ("kvred",)], writes=[("y1", c)])
                    ally = [("y1", c) for c in range(KC)]
                    t.op("scalar", lambda s: s.activation(vw, vw, AF.Ln),
                         reads=ally, writes=ally)
                    t.op("scalar", lambda s: s.activation(
                        vw, vw, AF.Sigmoid, bias=NEG_LN_EPS_ATTN),
                        reads=ally + [("fconst",)], writes=ally)
                    for c in range(KC):
                        t.op("vector", lambda v, c=c, n=n: v.tensor_scalar(
                            qb_bf[:, c * TTB:(c + 1) * TTB],
                            vw[:, c * TTB:(c + 1) * TTB],
                            kvdks[:, c * 2 + n:c * 2 + n + 1], None, op0=Op.mult),
                            reads=[("y1", c), ("kvdks",)], writes=[("qb", c)])

                load_qb(0)
                load_xb(0)
                v_stage(0)
                for j in range(NTB):
                    slj = slice(j * TTB, (j + 1) * TTB)
                    if False:
                        pass
                    for mc in range(KC):
                        bO = pb[mc % 2]
                        for kc in range(KC):
                            t.op("tensor", lambda te, mc=mc, kc=kc, bO=bO: te.matmul(
                                bO[:, 0:TTB],
                                wo_bf[:, kc * D + mc * 128:kc * D + mc * 128 + 128],
                                qb_bf[:, kc * TTB:(kc + 1) * TTB],
                                start=(kc == 0), stop=(kc == KC - 1)),
                                reads=[("wo", kc, mc // 4), ("qb", kc)],
                                writes=[pbk[mc % 2]], inc=(kc == KC - 1))
                        t.op("scalar", lambda s, mc=mc, bO=bO: s.activation(
                            y1[:, mc * TTB:(mc + 1) * TTB], bO[:, 0:TTB], GELU_T,
                            bias=cn("bo", mc)),
                            reads=[pbk[mc % 2], ck], writes=[("y1", mc)])
                    t.op("vector", lambda v: v.scalar_tensor_tensor(
                        y1[:], y1[:], 0.0, xb_bf[:],
                        op0=Op.bypass, op1=Op.add),
                        reads=[("y1", m) for m in range(KC)] +
                              [("xb", c) for c in range(KC)],
                        writes=[("y1", m) for m in range(KC)])
                    y_aps = [y1[:, c * TTB:(c + 1) * TTB] for c in range(KC)]
                    ykeys = [("y1", c) for c in range(KC)]
                    ln_stats(y_aps, ykeys, TTB, EPS_LN_A, pb[4:8], pbk[4:8],
                             srow, srk, pD, kD)
                    mub = pb[6][:, 0:TTB].rearrange(
                        "p (r c) -> p r c", r=1).broadcast_to((128, KC, TTB))
                    t.op("vector", lambda v: v.scalar_tensor_tensor(
                        y1[:].rearrange("p (r c) -> p r c", r=KC),
                        y1[:].rearrange("p (r c) -> p r c", r=KC),
                        0.0, mub, op0=Op.bypass, op1=Op.subtract),
                        reads=ykeys + [pbk[6]], writes=ykeys)
                    for c in range(KC):
                        t.op("vector", lambda v, c=c: v.scalar_tensor_tensor(
                            pB, y_aps[c], cn("ln1s", c),
                            pb[7][:, 0:TTB], op0=Op.mult, op1=Op.mult),
                            reads=[ykeys[c], pbk[7], ck], writes=[kB])
                        t.op("vector", lambda v, c=c: v.tensor_scalar(
                            xb_bf[:, c * TTB:(c + 1) * TTB], pB,
                            cn("ln1b", c), None, op0=Op.add),
                            reads=[kB, ck], writes=[("xb", c)])
                    if j + 1 < NTB and HOIST_V:
                        load_qb(j + 1)
                        v_stage(j + 1)
                    for mh in range(FFC):
                        bH = pb[2 + mh % 2]
                        for kc in range(KC):
                            t.op("tensor", lambda te, mh=mh, kc=kc, bH=bH: te.matmul(
                                bH[:, 0:TTB],
                                w1_bf[:, kc * FF + mh * 128:kc * FF + mh * 128 + 128],
                                xb_bf[:, kc * TTB:(kc + 1) * TTB],
                                start=(kc == 0), stop=(kc == KC - 1)),
                                reads=[("w1", kc, mh // 4), ("xb", kc)],
                                writes=[pbk[2 + mh % 2]], inc=(kc == KC - 1))
                        t.op("scalar", lambda s, mh=mh, bH=bH: s.activation(
                            h_bf[:, mh * TTB:(mh + 1) * TTB], bH[:, 0:TTB], GELU_T,
                            bias=cn("bff1", mh)),
                            reads=[pbk[2 + mh % 2], ck], writes=[("h", mh)])
                    conv_pump(26)
                    for mc in range(KC):
                        bF = pb[mc % 2]
                        for kh in range(FFC):
                            t.op("tensor", lambda te, mc=mc, kh=kh, bF=bF: te.matmul(
                                bF[:, 0:TTB],
                                w2_bf[:, kh * D + mc * 128:kh * D + mc * 128 + 128],
                                h_bf[:, kh * TTB:(kh + 1) * TTB],
                                start=(kh == 0), stop=(kh == FFC - 1)),
                                reads=[("w2", kh, mc // 4), ("h", kh)],
                                writes=[pbk[mc % 2]], inc=(kh == FFC - 1))
                        t.op("scalar", lambda s, mc=mc, bF=bF: s.activation(
                            pC, bF[:, 0:TTB], GELU_T, bias=cn("bff2", mc)),
                            reads=[pbk[mc % 2], ck], writes=[kC])
                        t.op("vector", lambda v, mc=mc: v.scalar_tensor_tensor(
                            y1[:, mc * TTB:(mc + 1) * TTB], pC, 0.0,
                            xb_bf[:, mc * TTB:(mc + 1) * TTB],
                            op0=Op.bypass, op1=Op.add),
                            reads=[kC, ("xb", mc)], writes=[("y1", mc)])
                    ln_stats(y_aps, ykeys, TTB, EPS_LN_A, pb[4:8], pbk[4:8],
                             srow, srk, pD, kD)
                    mub = pb[6][:, 0:TTB].rearrange(
                        "p (r c) -> p r c", r=1).broadcast_to((128, KC, TTB))
                    t.op("vector", lambda v: v.scalar_tensor_tensor(
                        y1[:].rearrange("p (r c) -> p r c", r=KC),
                        y1[:].rearrange("p (r c) -> p r c", r=KC),
                        0.0, mub, op0=Op.bypass, op1=Op.subtract),
                        reads=ykeys + [pbk[6]], writes=ykeys)
                    for c in range(KC):
                        t.op("vector", lambda v, c=c: v.scalar_tensor_tensor(
                            pB, y_aps[c], cn("ln2s", c),
                            pb[7][:, 0:TTB], op0=Op.mult, op1=Op.mult),
                            reads=[kA, pbk[7], ck], writes=[kB])
                        if blk < NB - 1:
                            t.op("vector", lambda v, c=c: v.tensor_scalar(
                                xb_bf[:, c * TTB:(c + 1) * TTB], pB, cn("ln2b", c),
                                None, op0=Op.add),
                                reads=[kB, ck], writes=[("xb", c)])
                        else:
                            t.op("vector", lambda v, c=c: v.tensor_scalar(
                                y1[:, c * TTB:(c + 1) * TTB], pB, cn("ln2b", c),
                                None, op0=Op.add),
                                reads=[kB, ck], writes=[("y1", c)])
                            t.dma(sx, af32_v[:, c, slj],
                                  y1[:, c * TTB:(c + 1) * TTB],
                                  reads=[("y1", c)], writes=[("af32", c, j)])
                    if blk < NB - 1:
                        t.dma(sx, strm_v[:, :, slj],
                              xb_bf[:].rearrange("p (kc c) -> p kc c", kc=KC),
                              reads=[("xb", c) for c in range(KC)],
                              writes=[("strm", c, j) for c in range(KC)])
                    if j + 1 < NTB:
                        load_xb(j + 1)
                        if not HOIST_V:
                            load_qb(j + 1)
                            v_stage(j + 1)
                if blk < NB - 1:
                    load_cab(blk + 1)
                    refill_wo(blk + 1)
                    refill_w1(blk + 1)
                    refill_w2(blk + 1)
            conv_pump(10 ** 9)
            t.barrier(STREAMS)

        stop_now = stop_now or (upto == "aa")
        if not stop_now:
            ln_junction("j1", x0_v, None, af32_v, "af32", x1_v, "x1", n1_v, "n1", EPS_MOD)
        stop_now = stop_now or (upto == "j1")

        def moe_call(tag, xn_v, xn_pref, xp_v, xp_pref, xout_v, xout_pref):
            with contextlib.ExitStack() as seg:
                ssb = lambda name, shape, dt: seg.enter_context(
                    nc.sbuf_tensor(name, list(shape), dt))
                rtr_bf = ssb(f"rtr_bf_{tag}", (128, KC * 16), BF16)
                lgf = ssb(f"lgf_{tag}", (16, TTB), F32)
                lg_tm = ssb(f"lg_tm_{tag}", (128, 16), F32)
                e_tm = ssb(f"e_tm_{tag}", (128, 16), F32)
                est = ssb(f"est_{tag}", (128, 4), F32)
                cnt_tm = ssb(f"cnt_tm_{tag}", (128, 16), F32)
                gate_tm = ssb(f"gate_tm_{tag}", (128, 16), F32)
                gatef = ssb(f"gatef_{tag}", (16, T), F32)
                sel_all = ssb(f"sel_all_{tag}", (16, 16 * 128), F32)
                identF = ssb(f"identF_{tag}", (128, 128), F32)
                xn_sb = ssb(f"mxn_{tag}", (128, KC * TTB), BF16)
                wup_bf = [ssb(f"wup_{tag}{i}", (128, KC * 512), BF16) for i in range(8)]
                wdn_bf = [ssb(f"wdn_{tag}{i}", (128, 4 * D), BF16) for i in range(8)]
                act_bf = ssb(f"act_bf_{tag}", (128, 16 * TTB), BF16)
                acc_bf = ssb(f"acc_bf_{tag}", (128, KC * T), BF16)
                pool = [ssb(f"mpool_{tag}{i}", (128, TTB), F32) for i in range(5)]
                pbm = [seg.enter_context(nc.psum_tensor(f"mps_{tag}{i}", [128, 512], F32))
                       for i in range(8)]
                pmk = [("mps", i) for i in range(8)]

                t.op("gpsimd", lambda g: g.memset(identF[:], 0.0), writes=[("iF",)])
                t.op("gpsimd", lambda g: g.affine_select(
                    out=identF[:], in_=identF[:], compare_op=Op.not_equal,
                    fill=1.0, base=0, pattern=[[-1, 128]], channel_multiplier=1),
                    reads=[("iF",)], writes=[("iF",)])
                t.op("gpsimd", lambda g: g.memset(sel_all[:], 1.0), writes=[("sel",)])
                t.op("gpsimd", lambda g: g.affine_select(
                    out=sel_all[:], in_=sel_all[:], compare_op=Op.is_ge,
                    fill=0.0, base=0, pattern=[[1, 16 * 128]],
                    channel_multiplier=-128),
                    reads=[("sel",)], writes=[("sel",)])
                t.op("gpsimd", lambda g: g.affine_select(
                    out=sel_all[:], in_=sel_all[:], compare_op=Op.is_ge,
                    fill=0.0, base=127, pattern=[[-1, 16 * 128]],
                    channel_multiplier=128),
                    reads=[("sel",)], writes=[("sel",)])
                ap = router_t.ap().rearrange("(kc p) e -> p kc e", p=128)
                load_cast2(wmo, ap[:, :, :], rtr_bf[:], ("rtr",), 0, KC * 16)

                for j in range(NTB):
                    slj = slice(j * TTB, (j + 1) * TTB)
                    t.dma(lx, xn_sb[:].rearrange("p (kc c) -> p kc c", kc=KC),
                          xn_v[:, :, slj],
                          reads=[(xn_pref, c, j) for c in range(KC)],
                          writes=[("mxn",)])
                    bL = pbm[0]
                    for kc in range(KC):
                        t.op("tensor", lambda te, kc=kc, bL=bL: te.matmul(
                            bL[0:16, 0:TTB], rtr_bf[:, kc * 16:(kc + 1) * 16],
                            xn_sb[:, kc * TTB:(kc + 1) * TTB],
                            start=(kc == 0), stop=(kc == KC - 1)),
                            reads=[("rtr",), ("mxn",)], writes=[pmk[0]],
                            inc=(kc == KC - 1))
                    t.op("vector", lambda v, bL=bL: v.tensor_copy(
                        lgf[:, 0:TTB], bL[0:16, 0:TTB]),
                        reads=[pmk[0]], writes=[("lgf",)])
                    for sub in range(2):
                        ss = slice(sub * 128, (sub + 1) * 128)
                        bT = pbm[1]
                        t.op("tensor", lambda te, ss=ss, bT=bT: te.transpose(
                            bT[:, 0:16], lgf[0:16, ss], identF[0:16, 0:16]),
                            reads=[("lgf",), ("iF",)], writes=[pmk[1]])
                        t.op("vector", lambda v, bT=bT: v.tensor_copy(
                            lg_tm[:], bT[:, 0:16]),
                            reads=[pmk[1]], writes=[("lgtm",)])
                        t.op("vector", lambda v: v.tensor_reduce(
                            est[:, 0:1], lg_tm[:], axis=mybir.AxisListType.X,
                            op=Op.max),
                            reads=[("lgtm",)], writes=[("est", 0)])
                        t.op("vector", lambda v: v.tensor_scalar(
                            est[:, 1:2], est[:, 0:1], -1.0, None, op0=Op.mult),
                            reads=[("est", 0)], writes=[("est", 1)])
                        t.op("scalar", lambda s: s.activation(
                            e_tm[:], lg_tm[:], AF.Exp, bias=est[:, 1:2],
                            accum_out=est[:, 2:3]),
                            reads=[("lgtm",), ("est", 1)],
                            writes=[("etm",), ("est", 2)])
                        t.op("vector", lambda v: v.reciprocal(
                            est[:, 3:4], est[:, 2:3]),
                            reads=[("est", 2)], writes=[("est", 3)])
                        for k in range(16):
                            if k == 0:
                                t.op("vector", lambda v, k=k: v.tensor_scalar(
                                    cnt_tm[:], e_tm[:], e_tm[:, k:k + 1], None,
                                    op0=Op.is_lt),
                                    reads=[("etm",)], writes=[("cnt",)])
                            else:
                                t.op("vector", lambda v, k=k: v.scalar_tensor_tensor(
                                    cnt_tm[:], e_tm[:], e_tm[:, k:k + 1], cnt_tm[:],
                                    op0=Op.is_lt, op1=Op.add),
                                    reads=[("etm",), ("cnt",)], writes=[("cnt",)])
                        t.op("vector", lambda v: v.tensor_scalar(
                            cnt_tm[:], cnt_tm[:], 3.0, None, op0=Op.is_le),
                            reads=[("cnt",)], writes=[("cnt",)])
                        t.op("vector", lambda v: v.scalar_tensor_tensor(
                            gate_tm[:], e_tm[:], est[:, 3:4], cnt_tm[:],
                            op0=Op.mult, op1=Op.mult),
                            reads=[("etm",), ("est", 3), ("cnt",)], writes=[("gtm",)])
                        bT2 = pbm[2]
                        t.op("tensor", lambda te, bT2=bT2: te.transpose(
                            bT2[0:16, 0:128], gate_tm[:], identF[:]),
                            reads=[("gtm",), ("iF",)], writes=[pmk[2]])
                        t.op("vector", lambda v, j=j, sub=sub, bT2=bT2: v.tensor_copy(
                            gatef[0:16, j * TTB + sub * 128:j * TTB + (sub + 1) * 128],
                            bT2[0:16, 0:128]),
                            reads=[pmk[2]], writes=[("gatef", j, sub)])


                def load_group(gi):
                    grp = EG[gi]
                    reg = gi % 2
                    for ri, r in enumerate(grp):
                        t.dma(wmo, wup_bf[reg * 4 + ri][:], scr_wupc.ap()[r],
                              reads=[("cwup", r, q) for q in range(8)],
                              writes=[("wup", reg * 4 + ri, q) for q in range(8)])
                        t.dma(wmo, wdn_bf[reg * 4 + ri][:], scr_wdnc.ap()[r],
                              reads=[("cwdn", r, q) for q in range(8)],
                              writes=[("wdn", reg * 4 + ri, q) for q in range(8)])

                load_group(0)
                for gi, grp in enumerate(EG):
                    reg = gi % 2
                    for j in range(NTB):
                        if j == 1 and gi + 1 < len(EG):
                            load_group(gi + 1)
                        slj = slice(j * TTB, (j + 1) * TTB)
                        t.dma(lx, xn_sb[:].rearrange("p (kc c) -> p kc c", kc=KC),
                              xn_v[:, :, slj],
                              reads=[(xn_pref, c, j) for c in range(KC)],
                              writes=[("mxn",)])
                        for ri, r in enumerate(grp):
                            shared = (r >= 16)
                            wkeys = [("wup", reg * 4 + ri, q) for q in range(4)]
                            if not shared:
                                bG = pbm[2]
                                t.op("tensor", lambda te, r=r, j=j, bG=bG: te.matmul(
                                    bG[:, 0:TTB],
                                    sel_all[0:16, r * 128:(r + 1) * 128],
                                    gatef[0:16, j * TTB:(j + 1) * TTB],
                                    start=True, stop=True),
                                    reads=[("sel",)] + [("gatef", j, su) for su in range(2)],
                                    writes=[pmk[2]])
                            for ec in range(4):
                                bU = pbm[ec % 2]
                                for kc in range(KC):
                                    t.op("tensor", lambda te, ri=ri, ec=ec, kc=kc, bU=bU, reg=reg: te.matmul(
                                        bU[:, 0:TTB],
                                        wup_bf[reg * 4 + ri][:, kc * 512 + ec * 128:kc * 512 + ec * 128 + 128],
                                        xn_sb[:, kc * TTB:(kc + 1) * TTB],
                                        start=(kc == 0), stop=(kc == KC - 1)),
                                        reads=[("wup", reg * 4 + ri, kc), ("mxn",)],
                                        writes=[pmk[ec % 2]], inc=(kc == KC - 1))
                                if shared:
                                    t.op("scalar", lambda s, ri=ri, ec=ec, bU=bU: s.activation(
                                        act_bf[:, (ri * 4 + ec) * TTB:(ri * 4 + ec + 1) * TTB],
                                        bU[:, 0:TTB], GELU_E),
                                        reads=[pmk[ec % 2]], writes=[("act", ri, ec)])
                                else:
                                    t.op("scalar", lambda s, ec=ec, bU=bU: s.activation(
                                        pool[4][:], bU[:, 0:TTB], GELU_E),
                                        reads=[pmk[ec % 2]], writes=[("mpool", 4)])
                                    t.op("vector", lambda v, ri=ri, ec=ec: v.tensor_tensor(
                                        act_bf[:, (ri * 4 + ec) * TTB:(ri * 4 + ec + 1) * TTB],
                                        pool[4][:], pbm[2][:, 0:TTB], op=Op.mult),
                                        reads=[("mpool", 4), pmk[2]],
                                        writes=[("act", ri, ec)])
                        for mc in range(KC):
                            bD = pbm[3 + mc % 2]
                            nmm = len(grp) * 4
                            i = 0
                            for ri, r in enumerate(grp):

                                for ec in range(4):
                                    i += 1
                                    t.op("tensor", lambda te, ri=ri, ec=ec, mc=mc, bD=bD, reg=reg, i=i, nmm=nmm: te.matmul(
                                        bD[:, 0:TTB],
                                        wdn_bf[reg * 4 + ri][:, ec * D + mc * 128:ec * D + mc * 128 + 128],
                                        act_bf[:, (ri * 4 + ec) * TTB:(ri * 4 + ec + 1) * TTB],
                                        start=(i == 1), stop=(i == nmm)),
                                        reads=[("wdn", reg * 4 + ri, 2 * ec + mc // 4),
                                               ("act", ri, ec)],
                                        writes=[pmk[3 + mc % 2]], inc=(i == nmm))
                            asl = acc_bf[:, mc * T + j * TTB:mc * T + (j + 1) * TTB]
                            if gi == 0:
                                t.op("vector", lambda v, bD=bD, asl=asl: v.tensor_copy(
                                    asl, bD[:, 0:TTB]),
                                    reads=[pmk[3 + mc % 2]], writes=[("macc", mc, j)])
                            elif gi < len(EG) - 1:
                                t.op("vector", lambda v, bD=bD, asl=asl: v.scalar_tensor_tensor(
                                    asl, bD[:, 0:TTB], 0.0, asl,
                                    op0=Op.bypass, op1=Op.add),
                                    reads=[pmk[3 + mc % 2], ("macc", mc, j)],
                                    writes=[("macc", mc, j)])
                            else:
                                t.op("vector", lambda v, bD=bD, asl=asl, mc=mc: v.scalar_tensor_tensor(
                                    pool[mc % 2][:], bD[:, 0:TTB], 0.0, asl,
                                    op0=Op.bypass, op1=Op.add),
                                    reads=[pmk[3 + mc % 2], ("macc", mc, j)],
                                    writes=[("mpool", mc % 2)])
                                t.dma(lx, pool[2 + mc % 2][:], xp_v[:, mc, slj],
                                      reads=[(xp_pref, mc, j)],
                                      writes=[("mpool", 2 + mc % 2)])
                                t.op("vector", lambda v, mc=mc: v.tensor_tensor(
                                    pool[mc % 2][:], pool[mc % 2][:],
                                    pool[2 + mc % 2][:], op=Op.add),
                                    reads=[("mpool", mc % 2), ("mpool", 2 + mc % 2)],
                                    writes=[("mpool", mc % 2)])
                                t.dma(sx, xout_v[:, mc, slj], pool[mc % 2][:],
                                      reads=[("mpool", mc % 2)],
                                      writes=[(xout_pref, mc, j)])
                t.barrier(STREAMS)

        if not stop_now:
            moe_call("m1", n1_v, "n1", x1_v, "x1", x2_v, "x2")
        stop_now = stop_now or (upto == "m1")
        if not stop_now:
            ln_junction("j2", x2_v, "x2", None, None, None, None, n2_v, "n2", EPS_MOD)
        stop_now = stop_now or (upto == "j2")

        if not stop_now:
         with contextlib.ExitStack() as seg:
            ssb = lambda name, shape, dt: seg.enter_context(
                nc.sbuf_tensor(name, list(shape), dt))
            wqkvb_bf = ssb("wqkvb_bf", (128, KC * QKVB_O), BF16)
            wob_bf = ssb("wob_bf", (128, KC * D), BF16)
            identB = ssb("identB", (128, 128), BF16)
            identF2 = ssb("identF2", (128, 128), F32)
            xst = [ssb(f"bxst{i}", (128, KC * TTA), BF16) for i in range(2)]
            qt = [ssb(f"bqt{i}", (128, QKVB_O), F32) for i in range(2)]
            HH = NHEAD // 2
            ue = [ssb(f"bue{i}", (128, HH * 130), F32) for i in range(4)]
            pe = ssb("bpe", (128, HH * 130), F32)
            rsb = ssb("brs", (128, HH * 128), F32)
            prb = ssb("bpr", (128, HH * 128), F32)
            coef = ssb("bcoef", (128, 32 * 16), F32)
            lay = [ssb(f"blay{i}", (128, D), BF16) for i in range(2)]
            layfm = [ssb(f"blayfm{i}", (128, KC * TTA), BF16) for i in range(2)]
            jbB = ssb("bjb", (128, KC * TTA), F32)
            pool = [ssb(f"bpool{i}", (128, TTA), F32) for i in range(4)]
            srow = ssb("bsrow", (1, 2048), F32)
            pbb = [seg.enter_context(nc.psum_tensor(f"bps{i}", [128, 512], F32))
                   for i in range(7)]
            pbT = seg.enter_context(nc.psum_tensor("bpsT", [128, 1024], BF16))
            pbbk = [("bps", i) for i in range(7)]
            srkB = [("bsrow", i) for i in range(4)]

            cgb = ssb("cgb", (128, 16), F32)
            base_gb = CINDEX[("bg",)][0]
            t.dma(wmo, cgb[:], consts_in.ap()[:, base_gb:base_gb + 16],
                  writes=[("cgb",)])
            t.op("gpsimd", lambda g: g.memset(identB[:], 0.0), writes=[("iB",)])
            t.op("gpsimd", lambda g: g.affine_select(
                out=identB[:], in_=identB[:], compare_op=Op.not_equal,
                fill=1.0, base=0, pattern=[[-1, 128]], channel_multiplier=1),
                reads=[("iB",)], writes=[("iB",)])
            t.op("gpsimd", lambda g: g.memset(identF2[:], 0.0), writes=[("iF2",)])
            t.op("gpsimd", lambda g: g.affine_select(
                out=identF2[:], in_=identF2[:], compare_op=Op.not_equal,
                fill=1.0, base=0, pattern=[[-1, 128]], channel_multiplier=1),
                reads=[("iF2",)], writes=[("iF2",)])
            t.dma(wmo, wqkvb_bf[:], scr_qkvbc.ap()[:, :],
                  reads=[("cqkvb", kc, q) for kc in range(KC) for q in range(7)],
                  writes=[("wqkvb", kc, q) for kc in range(KC) for q in range(7)])
            t.dma(wmo, wob_bf[:], scr_wobc.ap()[:, :],
                  reads=[("cwob", kc, hf) for kc in range(KC) for hf in range(2)],
                  writes=[("wob", kc, hf) for kc in range(KC) for hf in range(2)])

            def hsl(buf, hb, off, w):
                return buf[:].rearrange("p (h o) -> p h o", h=NHEAD)[
                    :, hb * HH:(hb + 1) * HH, off:off + w]

            def uex(buf, off, w):
                return buf[:].rearrange("p (h o) -> p h o", h=HH)[:, :, off:off + w]

            CO = {}
            conames = ["sk0", "sk1", "sq0", "sq1", "ekk", "ek0q0", "ek0q1",
                       "ek1q1", "rsk0", "rsk1", "rsq0", "rsq1", "kd", "rkd",
                       "den0", "rden0", "den1", "rden1", "c00", "c10", "c11",
                       "ck", "sg0", "sg1", "b1p", "g0", "g1a", "g1b", "gt",
                       "tmp0", "tmp1", "tmp2"]
            for i, nm in enumerate(conames):
                CO[nm] = coef[:, i * 16:(i + 1) * 16]
            COK = {nm: ("coef", nm) for nm in conames}

            def co_op(dst, emit, reads, inc=True):
                t.op("vector", emit, reads=reads, writes=[COK[dst]], inc=inc)

            for p in range(2):
                s_tiles = (p, p + 2)
                for si, ti in enumerate(s_tiles):
                    sl = slice(ti * TTA, (ti + 1) * TTA)
                    t.dma(lx, xst[si][:].rearrange("p (kc c) -> p kc c", kc=KC),
                          n2_v[:, :, sl],
                          reads=[("n2", c, 2 * ti + h) for c in range(KC)
                                 for h in range(2)],
                          writes=[("bxst", si)])
                for sub in range(4):
                    for si in range(2):
                        for nb_ in range(7):
                            w = min(512, QKVB_O - nb_ * 512)
                            bQ = pbb[nb_]
                            for kc in range(KC):
                                t.op("tensor", lambda te, si=si, nb_=nb_, kc=kc, w=w, sub=sub, bQ=bQ: te.matmul(
                                    bQ[:, 0:w],
                                    xst[si][:, kc * TTA + sub * 128:kc * TTA + sub * 128 + 128],
                                    wqkvb_bf[:, kc * QKVB_O + nb_ * 512:kc * QKVB_O + nb_ * 512 + w],
                                    start=(kc == 0), stop=(kc == KC - 1)),
                                    reads=[("bxst", si), ("wqkvb", kc, nb_)],
                                    writes=[pbbk[nb_]],
                                    inc=(kc == KC - 1))
                        for nb_ in range(7):
                            w = min(512, QKVB_O - nb_ * 512)
                            bQ = pbb[nb_]
                            t.op("vector", lambda v, si=si, nb_=nb_, w=w, bQ=bQ: v.tensor_copy(
                                qt[si][:, nb_ * 512:nb_ * 512 + w], bQ[:, 0:w]),
                                reads=[pbbk[nb_]],
                                writes=[("qt", si)])
                    co_op("sg0", lambda v: v.tensor_copy(
                        CO["sg0"][:, 0:HH], hsl(qt[0], 0, 192, 1)), reads=[("qt", 0)],
                        inc=False)
                    co_op("sg0", lambda v: v.tensor_copy(
                        CO["sg0"][:, HH:NHEAD], hsl(qt[0], 1, 192, 1)),
                        reads=[("qt", 0)])
                    t.op("scalar", lambda s: s.activation(CO["sg0"], CO["sg0"], AF.Sigmoid),
                         reads=[COK["sg0"]], writes=[COK["sg0"]])
                    co_op("sg1", lambda v: v.tensor_copy(
                        CO["sg1"][:, 0:HH], hsl(qt[1], 0, 192, 1)), reads=[("qt", 1)],
                        inc=False)
                    co_op("sg1", lambda v: v.tensor_copy(
                        CO["sg1"][:, HH:NHEAD], hsl(qt[1], 1, 192, 1)),
                        reads=[("qt", 1)])
                    t.op("scalar", lambda s: s.activation(CO["sg1"], CO["sg1"], AF.Sigmoid),
                         reads=[COK["sg1"]], writes=[COK["sg1"]])
                    for hb in range(2):
                        hsl8 = slice(hb * HH, (hb + 1) * HH)
                        srcs = [(0, 0), (1, 0), (0, 64), (1, 64)]
                        for i, (bi, off) in enumerate(srcs):
                            t.op("vector", lambda v, i=i, bi=bi, off=off, hb=hb: v.tensor_scalar(
                                uex(ue[i], 2, 64), hsl(qt[bi], hb, off, 64), 0.0, None,
                                op0=Op.max),
                                reads=[("qt", bi)], writes=[("ue", i)], inc=False)
                            t.op("vector", lambda v, i=i, bi=bi, off=off, hb=hb: v.tensor_scalar(
                                uex(ue[i], 66, 64), hsl(qt[bi], hb, off, 64), -1.0, 0.0,
                                op0=Op.mult, op1=Op.max),
                                reads=[("qt", bi)], writes=[("ue", i)], inc=False)
                            t.op("vector", lambda v, i=i: v.tensor_copy(
                                uex(ue[i], 0, 2), uex(ue[i], 128, 2)),
                                reads=[("ue", i)], writes=[("ue", i)])
                        self_map = [("sq0", 0), ("sq1", 1), ("sk0", 2), ("sk1", 3)]
                        for nm, i in self_map:
                            t.op("vector", lambda v, i=i: v.tensor_tensor(
                                rsb[:].rearrange("p (h o) -> p h o", h=HH),
                                uex(ue[i], 1, 128), uex(ue[i], 0, 128), op=Op.add),
                                reads=[("ue", i)], writes=[("rsb",)])
                            t.op("vector", lambda v, i=i: v.tensor_tensor(
                                prb[:].rearrange("p (h o) -> p h o", h=HH),
                                uex(ue[i], 2, 128),
                                rsb[:].rearrange("p (h o) -> p h o", h=HH), op=Op.mult),
                                reads=[("ue", i), ("rsb",)], writes=[("prb",)])
                            t.op("vector", lambda v, nm=nm, hsl8=hsl8: v.tensor_reduce(
                                CO[nm][:, hsl8],
                                prb[:].rearrange("p (h o) -> p h o", h=HH),
                                axis=mybir.AxisListType.X, op=Op.add),
                                reads=[("prb",)], writes=[COK[nm]])
                        e_map = [("ekk", 2, 3), ("ek0q0", 2, 0), ("ek0q1", 2, 1),
                                 ("ek1q1", 3, 1)]
                        for nm, ia, ib in e_map:
                            t.op("vector", lambda v, ia=ia, ib=ib: v.tensor_tensor(
                                pe[:], ue[ia][:], ue[ib][:], op=Op.mult),
                                reads=[("ue", ia), ("ue", ib)], writes=[("pe",)])
                            t.op("vector", lambda v: v.tensor_tensor(
                                rsb[:].rearrange("p (h o) -> p h o", h=HH),
                                uex(pe, 1, 128), uex(pe, 0, 128), op=Op.add),
                                reads=[("pe",)], writes=[("rsb",)])
                            t.op("vector", lambda v: v.tensor_tensor(
                                prb[:].rearrange("p (h o) -> p h o", h=HH),
                                uex(pe, 2, 128),
                                rsb[:].rearrange("p (h o) -> p h o", h=HH), op=Op.mult),
                                reads=[("pe",), ("rsb",)], writes=[("prb",)])
                            t.op("vector", lambda v, nm=nm, hsl8=hsl8: v.tensor_reduce(
                                CO[nm][:, hsl8],
                                prb[:].rearrange("p (h o) -> p h o", h=HH),
                                axis=mybir.AxisListType.X, op=Op.add),
                                reads=[("prb",)], writes=[COK[nm]])
                    for nm in ("sk0", "sk1", "sq0", "sq1"):
                        co_op("r" + nm, lambda v, nm=nm: v.reciprocal(
                            CO["r" + nm], CO[nm]), reads=[COK[nm]])
                    co_op("tmp0", lambda v: v.tensor_tensor(
                        CO["tmp0"], CO["rsk0"], CO["rsk1"], op=Op.mult),
                        reads=[COK["rsk0"], COK["rsk1"]])
                    co_op("kd", lambda v: v.tensor_tensor(
                        CO["kd"], CO["ekk"], CO["tmp0"], op=Op.mult),
                        reads=[COK["ekk"], COK["tmp0"]])
                    co_op("tmp1", lambda v: v.tensor_scalar(
                        CO["tmp1"], CO["kd"], EPS_B, None, op0=Op.add),
                        reads=[COK["kd"]])
                    co_op("rkd", lambda v: v.reciprocal(CO["rkd"], CO["tmp1"]),
                          reads=[COK["tmp1"]])
                    co_op("tmp0", lambda v: v.tensor_tensor(
                        CO["tmp0"], CO["rsk0"], CO["rsq0"], op=Op.mult),
                        reads=[COK["rsk0"], COK["rsq0"]])
                    co_op("den0", lambda v: v.tensor_tensor(
                        CO["den0"], CO["ek0q0"], CO["tmp0"], op=Op.mult),
                        reads=[COK["ek0q0"], COK["tmp0"]])
                    co_op("c00", lambda v: v.tensor_scalar(
                        CO["c00"], CO["den0"], REC1, None, op0=Op.mult),
                        reads=[COK["den0"]])
                    co_op("tmp0", lambda v: v.tensor_scalar(
                        CO["tmp0"], CO["den0"], EPS_B, None, op0=Op.add),
                        reads=[COK["den0"]])
                    co_op("rden0", lambda v: v.reciprocal(CO["rden0"], CO["tmp0"]),
                          reads=[COK["tmp0"]])
                    co_op("tmp0", lambda v: v.tensor_tensor(
                        CO["tmp0"], CO["rsk0"], CO["rsq1"], op=Op.mult),
                        reads=[COK["rsk0"], COK["rsq1"]])
                    co_op("tmp0", lambda v: v.tensor_tensor(
                        CO["tmp0"], CO["ek0q1"], CO["tmp0"], op=Op.mult),
                        reads=[COK["ek0q1"], COK["tmp0"]])
                    co_op("tmp1", lambda v: v.tensor_tensor(
                        CO["tmp1"], CO["rsk1"], CO["rsq1"], op=Op.mult),
                        reads=[COK["rsk1"], COK["rsq1"]])
                    co_op("tmp1", lambda v: v.tensor_tensor(
                        CO["tmp1"], CO["ek1q1"], CO["tmp1"], op=Op.mult),
                        reads=[COK["ek1q1"], COK["tmp1"]])
                    co_op("den1", lambda v: v.tensor_tensor(
                        CO["den1"], CO["tmp0"], CO["tmp1"], op=Op.add),
                        reads=[COK["tmp0"], COK["tmp1"]])
                    co_op("c10", lambda v: v.tensor_scalar(
                        CO["c10"], CO["tmp0"], REC1, None, op0=Op.mult),
                        reads=[COK["tmp0"]])
                    co_op("c11", lambda v: v.tensor_tensor(
                        CO["c11"], CO["tmp1"], CO["rkd"], op=Op.mult),
                        reads=[COK["tmp1"], COK["rkd"]])
                    co_op("tmp2", lambda v: v.tensor_scalar(
                        CO["tmp2"], CO["den1"], EPS_B, None, op0=Op.add),
                        reads=[COK["den1"]])
                    co_op("rden1", lambda v: v.reciprocal(CO["rden1"], CO["tmp2"]),
                          reads=[COK["tmp2"]])
                    co_op("ck", lambda v: v.tensor_tensor(
                        CO["ck"], CO["kd"], CO["rkd"], op=Op.mult),
                        reads=[COK["kd"], COK["rkd"]])
                    co_op("ck", lambda v: v.tensor_scalar(
                        CO["ck"], CO["ck"], REC1, None, op0=Op.mult),
                        reads=[COK["ck"]])
                    co_op("b1p", lambda v: v.tensor_tensor(
                        CO["b1p"], CO["sg1"], CO["kd"], op=Op.mult),
                        reads=[COK["sg1"], COK["kd"]])
                    co_op("g0", lambda v: v.tensor_tensor(
                        CO["g0"], CO["c00"], CO["sg0"], op=Op.mult),
                        reads=[COK["c00"], COK["sg0"]])
                    co_op("g0", lambda v: v.tensor_tensor(
                        CO["g0"], CO["g0"], CO["rden0"], op=Op.mult),
                        reads=[COK["g0"], COK["rden0"]])
                    co_op("g0", lambda v: v.tensor_scalar(
                        CO["g0"], CO["g0"], SCALE_B, None, op0=Op.mult),
                        reads=[COK["g0"]])
                    co_op("g1a", lambda v: v.tensor_tensor(
                        CO["g1a"], CO["c10"], CO["sg0"], op=Op.mult),
                        reads=[COK["c10"], COK["sg0"]])
                    co_op("g1a", lambda v: v.tensor_tensor(
                        CO["g1a"], CO["g1a"], CO["rden1"], op=Op.mult),
                        reads=[COK["g1a"], COK["rden1"]])
                    co_op("g1a", lambda v: v.tensor_scalar(
                        CO["g1a"], CO["g1a"], SCALE_B, None, op0=Op.mult),
                        reads=[COK["g1a"]])
                    co_op("g1b", lambda v: v.tensor_tensor(
                        CO["g1b"], CO["c11"], CO["b1p"], op=Op.mult),
                        reads=[COK["c11"], COK["b1p"]])
                    co_op("g1b", lambda v: v.tensor_tensor(
                        CO["g1b"], CO["g1b"], CO["rden1"], op=Op.mult),
                        reads=[COK["g1b"], COK["rden1"]])
                    co_op("g1b", lambda v: v.tensor_scalar(
                        CO["g1b"], CO["g1b"], SCALE_B, None, op0=Op.mult),
                        reads=[COK["g1b"]])
                    co_op("gt", lambda v: v.tensor_tensor(
                        CO["gt"], CO["ck"], CO["sg0"], op=Op.mult),
                        reads=[COK["ck"], COK["sg0"]])
                    co_op("gt", lambda v: v.tensor_scalar(
                        CO["gt"], CO["gt"], -1.0, None, op0=Op.mult),
                        reads=[COK["gt"]])
                    for hh in range(NHEAD):
                        v0 = qt[0][:, hh * 193 + 128:hh * 193 + 192]
                        v1 = qt[1][:, hh * 193 + 128:hh * 193 + 192]
                        t.op("vector", lambda v, hh=hh, v0=v0: v.tensor_scalar(
                            lay[0][:, hh * 64:(hh + 1) * 64], v0,
                            CO["g0"][:, hh:hh + 1], None, op0=Op.mult),
                            reads=[("qt", 0), COK["g0"]], writes=[("lay", 0)],
                            inc=(hh == NHEAD - 1))
                        t.op("vector", lambda v, hh=hh, v0=v0, v1=v1: v.scalar_tensor_tensor(
                            pool[0][:, 0:64], v0, CO["gt"][:, hh:hh + 1], v1,
                            op0=Op.mult, op1=Op.add),
                            reads=[("qt", 0), ("qt", 1), COK["gt"]],
                            writes=[("bpool", 0)])
                        t.op("vector", lambda v, hh=hh, v0=v0: v.tensor_scalar(
                            pool[1][:, 0:64], v0, CO["g1a"][:, hh:hh + 1], None,
                            op0=Op.mult),
                            reads=[("qt", 0), COK["g1a"]], writes=[("bpool", 1)])
                        t.op("vector", lambda v, hh=hh: v.scalar_tensor_tensor(
                            lay[1][:, hh * 64:(hh + 1) * 64], pool[0][:, 0:64],
                            CO["g1b"][:, hh:hh + 1], pool[1][:, 0:64],
                            op0=Op.mult, op1=Op.add),
                            reads=[("bpool", 0), ("bpool", 1), COK["g1b"]],
                            writes=[("lay", 1)], inc=(hh == NHEAD - 1))
                    for si in range(2):
                        for ch in range(KC):
                            t.op("tensor", lambda te, si=si, ch=ch: te.transpose(
                                pbT[:, 0:128], lay[si][:, ch * 128:(ch + 1) * 128],
                                identB[:]),
                                reads=[("lay", si), ("iB",)], writes=[("bpsT",)])
                            t.op("vector", lambda v, si=si, ch=ch, sub=sub: v.tensor_copy(
                                layfm[si][:, ch * TTA + sub * 128:ch * TTA + sub * 128 + 128],
                                pbT[:, 0:128]),
                                reads=[("bpsT",)], writes=[("layfm", si, ch, sub)])
                for si, ti in enumerate(s_tiles):
                    sl = slice(ti * TTA, (ti + 1) * TTA)
                    for mc in range(KC):
                        bO = pbb[mc % 2]
                        for kc in range(KC):
                            t.op("tensor", lambda te, si=si, mc=mc, kc=kc, bO=bO: te.matmul(
                                bO[:], wob_bf[:, kc * D + mc * 128:kc * D + mc * 128 + 128],
                                layfm[si][:, kc * TTA:(kc + 1) * TTA],
                                start=(kc == 0), stop=(kc == KC - 1)),
                                reads=[("wob", kc, mc // 4)] + [("layfm", si, kc, su)
                                                                for su in range(4)],
                                writes=[pbbk[mc % 2]], inc=(kc == KC - 1))
                        t.op("vector", lambda v, si=si, mc=mc, bO=bO: v.scalar_tensor_tensor(
                            jbB[:, mc * TTA:(mc + 1) * TTA], bO[:], 0.0,
                            xst[si][:, mc * TTA:(mc + 1) * TTA],
                            op0=Op.bypass, op1=Op.add),
                            reads=[pbbk[mc % 2], ("bxst", si)], writes=[("bjb", mc)])
                    y_aps = [jbB[:, c * TTA:(c + 1) * TTA] for c in range(KC)]
                    ykeys = [("bjb", c) for c in range(KC)]
                    ln_stats(y_aps, ykeys, TTA, LNEPS_B, pbb[2:6], pbbk[2:6],
                             srow, srkB, pool[3][:], ("bpool", 3))
                    for c in range(KC):
                        t.op("vector", lambda v, c=c: v.scalar_tensor_tensor(
                            pool[0][:], y_aps[c], 0.0, pbb[4][:, 0:TTA],
                            op0=Op.bypass, op1=Op.subtract),
                            reads=[ykeys[c], pbbk[4]], writes=[("bpool", 0)])
                        t.op("vector", lambda v, c=c: v.scalar_tensor_tensor(
                            pool[1][:], pool[0][:], cgb[:, c:c + 1],
                            pbb[5][:, 0:TTA], op0=Op.mult, op1=Op.mult),
                            reads=[("bpool", 0), pbbk[5], ("cgb",)],
                            writes=[("bpool", 1)])
                        t.op("vector", lambda v, c=c: v.tensor_scalar(
                            pool[2][:], pool[1][:], cgb[:, 8 + c:8 + c + 1], None,
                            op0=Op.add),
                            reads=[("bpool", 1), ("cgb",)], writes=[("bpool", 2)])
                        t.dma(lx, pool[0][:], x2_v[:, c, sl],
                              reads=[("x2", c, 2 * ti), ("x2", c, 2 * ti + 1)],
                              writes=[("bpool", 0)])
                        t.op("vector", lambda v, c=c: v.tensor_tensor(
                            pool[2][:], pool[2][:], pool[0][:], op=Op.add),
                            reads=[("bpool", 2), ("bpool", 0)], writes=[("bpool", 2)])
                        t.dma(sx, x3_v[:, c, sl], pool[2][:],
                              reads=[("bpool", 2)],
                              writes=[("x3", c, 2 * ti), ("x3", c, 2 * ti + 1)])
            t.barrier(STREAMS)

        stop_now = stop_now or (upto == "ab")
        if not stop_now:
            ln_junction("j3", x3_v, "x3", None, None, None, None, n3_v, "n3", EPS_MOD)
        stop_now = stop_now or (upto == "j3")
        if not stop_now:
            moe_call("m2", n3_v, "n3", x3_v, "x3", out_v, "outk")

        t.barrier(STREAMS)

    es.close()
    nc.finalize()
    return nc



def _prep(inputs):
    f32 = np.float32
    g = lambda k: np.ascontiguousarray(np.asarray(inputs[k], dtype=f32))
    Wq, Wk, Wv = g("L0_attn_a_Wq"), g("L0_attn_a_Wk"), g("L0_attn_a_Wv")

    def cb(W):
        return np.ascontiguousarray(
            W.reshape(NB, D, KC, 128).transpose(0, 2, 1, 3))

    cp = ConstPack()
    for i in range(NB):
        cp.add(("bq", i), inputs["L0_attn_a_bq"][i])
        cp.add(("bk", i), inputs["L0_attn_a_bk"][i])
        cp.add(("bv", i), inputs["L0_attn_a_bv"][i])
        cp.add(("bo", i), inputs["L0_attn_a_bo"][i])
        cp.add(("bff2", i), inputs["L0_attn_a_b2"][i])
        cp.add(("ln1s", i), inputs["L0_attn_a_ln1_s"][i])
        cp.add(("ln1b", i), inputs["L0_attn_a_ln1_b"][i])
        cp.add(("ln2s", i), inputs["L0_attn_a_ln2_s"][i])
        cp.add(("ln2b", i), inputs["L0_attn_a_ln2_b"][i])
        cp.add(("bff1", i), inputs["L0_attn_a_b1"][i])
    cp.add(("bg",), inputs["L0_attn_b_ln_g"])
    cp.add(("bb",), inputs["L0_attn_b_ln_b"])
    consts = cp.array()
    assert cp.index == CINDEX, "const layout mismatch"
    assert consts.shape[1] == NCONST

    up_all = np.concatenate([
        np.asarray(inputs["L0_ffn_routed_up"], dtype=f32),
        np.asarray(inputs["L0_ffn_shared_up"], dtype=f32)], axis=0)
    down_all = np.concatenate([
        np.asarray(inputs["L0_ffn_routed_down"], dtype=f32),
        np.asarray(inputs["L0_ffn_shared_down"], dtype=f32)], axis=0)

    shared = {
        "wq_cb": cb(Wq), "wk_cb": cb(Wk), "wv_cb": cb(Wv),
        "wo_w": g("L0_attn_a_Wo"), "w1_w": g("L0_attn_a_W1"),
        "w2_w": g("L0_attn_a_W2"),
        "router_t": np.ascontiguousarray(g("L0_ffn_router_w").T),
        "up_t": np.ascontiguousarray(up_all.transpose(0, 2, 1)),
        "down_t": np.ascontiguousarray(down_all.transpose(0, 2, 1)),
        "wqkvb_t": np.ascontiguousarray(g("L0_attn_b_W_qkvb").T),
        "wob_t": np.ascontiguousarray(g("L0_attn_b_W_o").T),
        "consts": consts,
    }
    x = np.asarray(inputs["L0_attn_a_x"], dtype=f32)
    xs = []
    for c in range(NCORE):
        xc = x[:, c * 1024:(c + 1) * 1024, :].reshape(T, D)
        xs.append(np.ascontiguousarray(xc.T))
    return shared, xs


def kernel(**inputs):
    shared, xs = _prep(inputs)
    nc = build_program()
    in_maps = [dict(shared, x0=xs[c]) for c in range(NCORE)]
    res = run_bass_kernel_spmd(nc, in_maps, core_ids=list(range(NCORE)))
    out = np.empty((2, 8192, D), np.float32)
    for c in range(NCORE):
        oc = np.asarray(res.results[c]["out"])
        out[:, c * 1024:(c + 1) * 1024, :] = oc.T.reshape(2, 1024, D)
    return out


# version 20
# speedup vs baseline: 1.2883x; 1.0882x over previous
import contextlib
import numpy as np
import ml_dtypes
import concourse.bass as bass
import concourse.mybir as mybir
from concourse.bass_utils import run_bass_kernel_spmd
from concourse.alu_op_type import AluOpType as Op

F32 = mybir.dt.float32
BF16 = mybir.dt.bfloat16
AF = mybir.ActivationFunctionType

NCORE = 8
D = 1024
KC = 8
T = 2048
TTA = 512
NTA = T // TTA
TTB = 256
NTB = T // TTB
FF = 4096
FFC = 32
NB = 8
NEXP = 18
EG = [list(range(0, 4)), list(range(4, 8)), list(range(8, 12)),
      list(range(12, 16)), [16, 17]]
NHEAD = 16
QKVB_O = 3088
EPS_LN_A = 1e-6
EPS_MOD = 1e-5
EPS_B = 1e-5
LNEPS_B = 1e-5
NEG_LN_EPS_ATTN = float(-np.log(1e-6))
HOIST_V = False
SCALE_B = 1.0 / 8.0
REC1 = float(1.0 / (1.0 + EPS_B))
GELU_T = AF.Gelu_apprx_tanh
GELU_E = AF.Gelu



class DmaStream:
    def __init__(self, nc, es, name, engine="sync", step=16):
        self.name = name
        self.engine = engine
        self.step = step
        self.sem = es.enter_context(nc.semaphore(f"dma_{name}"))
        self.count = 0


class Trk:
    ENGINES = ("tensor", "vector", "scalar", "gpsimd", "sync")

    def __init__(self, nc, es):
        self.nc = nc
        self.sem = {e: es.enter_context(nc.semaphore(f"prog_{e}"))
                    for e in self.ENGINES}
        self.cnt = {e: 0 for e in self.ENGINES}
        self.known = {}
        self.buf = {}
        self.eng = {e: getattr(nc, e) for e in self.ENGINES}

    def _wait(self, engine, dep):
        if dep[0] == "eng":
            _, e, c = dep
            if e == engine and c > self.cnt[e]:
                return
            k = (engine, "e", e)
            if self.known.get(k, 0) >= c:
                return
            self.eng[engine].wait_ge(self.sem[e], c)
            self.known[k] = c
        else:
            _, stream, c = dep
            c = max(c, stream.step * stream.count)
            k = (engine, "d", stream.name)
            if self.known.get(k, 0) >= c:
                return
            self.eng[engine].wait_ge(stream.sem, c)
            self.known[k] = c

    def depend(self, engine, reads=(), writes=()):
        deps = []
        for key in reads:
            b = self.buf.get(key)
            if b and b["w"] is not None:
                deps.append(b["w"])
        for key in writes:
            b = self.buf.get(key)
            if b:
                if b["w"] is not None:
                    deps.append(b["w"])
                deps.extend(b["r"])
        best = {}
        for d in deps:
            k = (d[0], d[1] if d[0] == "eng" else d[1].name)
            if k not in best or best[k][2] < d[2]:
                best[k] = d
        for d in best.values():
            self._wait(engine, d)

    def _record(self, dep, reads, writes):
        for key in writes:
            self.buf[key] = {"w": dep, "r": []}
        for key in reads:
            b = self.buf.setdefault(key, {"w": None, "r": []})
            b["r"].append(dep)

    def op(self, engine, emit, reads=(), writes=(), inc=True):
        self.depend(engine, reads, writes)
        inst = emit(self.eng[engine])
        if inc:
            self.cnt[engine] += 1
            inst.then_inc(self.sem[engine], 1)
            dep = ("eng", engine, self.cnt[engine])
        else:
            dep = ("eng", engine, self.cnt[engine] + 1)
        self._record(dep, reads, writes)
        return inst

    def dma(self, stream, out, in_, reads=(), writes=()):
        self.depend(stream.engine, reads, writes)
        inst = self.eng[stream.engine].dma_start(out, in_)
        stream.count += 1
        inst.then_inc(stream.sem, stream.step)
        self._record(("dma", stream, stream.step * stream.count), reads, writes)
        return inst

    def cc(self, stream, kind, op, replica_groups, ins, outs, reads=(), writes=()):
        self.depend(stream.engine, reads, writes)
        inst = self.eng[stream.engine].collective_compute(
            kind, op, replica_groups=replica_groups, ins=ins, outs=outs)
        stream.count += 1
        inst.then_inc(stream.sem, stream.step)
        self._record(("dma", stream, stream.step * stream.count), reads, writes)
        return inst

    def barrier(self, streams=()):
        for e in self.ENGINES:
            for f in self.ENGINES:
                if f != e and self.cnt[f] > 0:
                    self._wait(e, ("eng", f, self.cnt[f]))
            for s in streams:
                if s.count > 0:
                    self._wait(e, ("dma", s, s.step * s.count))



class ConstPack:
    def __init__(self):
        self.cols = []
        self.index = {}

    def add(self, name, vec):
        vec = np.asarray(vec, dtype=np.float32).reshape(-1)
        assert vec.size % 128 == 0
        n = vec.size // 128
        self.index[name] = (len(self.cols), n)
        arr = vec.reshape(n, 128).T
        for j in range(n):
            self.cols.append(np.ascontiguousarray(arr[:, j]))

    def array(self):
        return np.ascontiguousarray(np.stack(self.cols, axis=1))


def make_const_index():
    idx = {}
    pos = 0

    def add(name, n):
        nonlocal pos
        idx[name] = (pos, n)
        pos += n
    for i in range(NB):
        for nm in ("bq", "bk", "bv", "bo", "bff2", "ln1s", "ln1b", "ln2s", "ln2b"):
            add((nm, i), KC)
        add(("bff1", i), FFC)
    add(("bg",), KC)
    add(("bb",), KC)
    return idx, pos


CINDEX, NCONST = make_const_index()



def build_program(dump=(), upto=None, nblocks=NB, no_cc=False):
    nc = bass.Bass("TRN2", target_bir_lowering=False)
    es = contextlib.ExitStack()

    def din(name, shape):
        return nc.dram_tensor(name, list(shape), F32, kind="ExternalInput")

    lean = (upto == "j0")
    x0 = din("x0", (D, T))
    if not lean:
        wq_cb = din("wq_cb", (NB, KC, D, 128))
        wk_cb = din("wk_cb", (NB, KC, D, 128))
        wv_cb = din("wv_cb", (NB, KC, D, 128))
        wo_w = din("wo_w", (NB, D, D))
        w1_w = din("w1_w", (NB, D, FF))
        w2_w = din("w2_w", (NB, FF, D))
        router_t = din("router_t", (D, 16))
        up_t = din("up_t", (NEXP, D, 512))
        down_t = din("down_t", (NEXP, 512, D))
        wqkvb_t = din("wqkvb_t", (D, QKVB_O))
        wob_t = din("wob_t", (D, D))
        consts_in = din("consts", (128, NCONST))
    out_t = nc.dram_tensor("out", [D, T], F32, kind="ExternalOutput")

    def scrt(name, shape, dt):
        kind = "ExternalOutput" if name in dump else "Internal"
        return nc.dram_tensor(name, list(shape), dt, kind=kind)

    scr_stream = scrt("scr_stream", (D, T), BF16)
    scr_af32 = scrt("scr_af32", (D, T), F32)
    scr_q = scrt("scr_q", (D, T), BF16)
    scr_x1 = scrt("scr_x1", (D, T), F32)
    scr_x2 = scrt("scr_x2", (D, T), F32)
    scr_x3 = scrt("scr_x3", (D, T), F32)
    scr_n1 = scrt("scr_n1", (D, T), BF16)
    scr_n2 = scrt("scr_n2", (D, T), BF16)
    scr_n3 = scrt("scr_n3", (D, T), BF16)
    scr_w1c = nc.dram_tensor("scr_w1c", [2, 128, KC * FF], BF16)
    scr_w2c = nc.dram_tensor("scr_w2c", [2, 128, FFC * D], BF16)
    scr_woc = nc.dram_tensor("scr_woc", [2, 128, KC * D], BF16)
    scr_qkvc = nc.dram_tensor("scr_qkvc", [NB, 3, KC, 128, KC * 128], BF16)
    scr_wupc = nc.dram_tensor("scr_wupc", [NEXP, 128, KC * 512], BF16)
    scr_wdnc = nc.dram_tensor("scr_wdnc", [NEXP, 128, 4 * D], BF16)
    scr_qkvbc = nc.dram_tensor("scr_qkvbc", [128, KC * QKVB_O], BF16)
    scr_wobc = nc.dram_tensor("scr_wobc", [128, KC * D], BF16)
    cc_in = nc.dram_tensor("cc_in", [128, 32], F32)
    cc_out = nc.dram_tensor("cc_out", [128, 32], F32)

    def fmview(t_):
        return t_.ap().rearrange("(kc p) t -> p kc t", p=128)

    x0_v = fmview(x0)
    strm_v = fmview(scr_stream)
    af32_v = fmview(scr_af32)
    q_v = fmview(scr_q)
    x1_v = fmview(scr_x1)
    x2_v = fmview(scr_x2)
    x3_v = fmview(scr_x3)
    n1_v = fmview(scr_n1)
    n2_v = fmview(scr_n2)
    n3_v = fmview(scr_n3)
    out_v = fmview(out_t)

    sb = lambda name, shape, dt: es.enter_context(
        nc.sbuf_tensor(name, list(shape), dt))
    ones_col = sb("ones_col", (128, 1), F32)
    ones_row = sb("ones_row", (1, 128), F32)
    ones_col_bf = sb("ones_col_bf", (128, 1), BF16)
    wst2 = [sb(f"wst2_{i}", (128, 512), F32) for i in range(2)]
    wbst = [sb(f"wbst{i}", (128, 512), BF16) for i in range(2)]
    fconst = sb("fconst", (128, 4), F32)

    t = Trk(nc, es)
    ws1 = DmaStream(nc, es, "ws1", "sync")
    ws2 = DmaStream(nc, es, "ws2", "sync")
    wso = DmaStream(nc, es, "wso", "sync")
    wcb = DmaStream(nc, es, "wcb", "scalar")
    wmo = DmaStream(nc, es, "wmo", "sync")
    wci = DmaStream(nc, es, "wci", "sync")
    wco = DmaStream(nc, es, "wco", "gpsimd")
    lx = DmaStream(nc, es, "lx", "scalar")
    sx = DmaStream(nc, es, "sx", "scalar")
    ccs = DmaStream(nc, es, "ccs", "gpsimd", step=1)
    STREAMS = [ws1, ws2, wso, wcb, wmo, wci, wco, lx, sx, ccs]

    with nc.Block():
        t.op("vector", lambda v: v.memset(ones_col[:], 1.0), writes=[("ones_col",)])
        t.op("vector", lambda v: v.memset(ones_row[:], 1.0), writes=[("ones_row",)])
        t.op("vector", lambda v: v.memset(ones_col_bf[:], 1.0), writes=[("ones_cb",)])
        for i, val in enumerate((EPS_MOD, EPS_LN_A, NEG_LN_EPS_ATTN)):
            t.op("vector", lambda v, i=i, val=val: v.memset(
                fconst[:, i:i + 1], val), writes=[("fconst",)])
            nc.const_aps.register(F32, val, fconst[:, i:i + 1])

        def cast(dst_ap, src_ap, reads, writes, engine="gpsimd"):
            if engine == "scalar":
                t.op(engine, lambda e: e.activation(dst_ap, src_ap, AF.Copy),
                     reads=reads, writes=writes)
            else:
                t.op(engine, lambda e: e.tensor_copy(dst_ap, src_ap),
                     reads=reads, writes=writes)

        cast_rr = [0]
        CAST_ENGS = ("gpsimd",)

        def load_cast2(stream, dram_ap, dst_ap, dkey, slot, width):
            st = wst2[slot % 2]
            skey = ("wst2", slot % 2)
            t.dma(stream, st[:, 0:width], dram_ap, writes=[skey])
            eng = CAST_ENGS[cast_rr[0] % len(CAST_ENGS)]
            cast_rr[0] += 1
            cast(dst_ap, st[:, 0:width], reads=[skey], writes=[dkey], engine=eng)

        conv_queue = []
        conv_rr = [0]
        refilled_w = [-1]

        def conv_pump(n):
            done = 0
            i = 0
            while done < n and i < len(conv_queue):
                gate, src_ap, dst_ap, dkey, width = conv_queue[i]
                if gate is not None and gate > refilled_w[0] + 2:
                    i += 1
                    continue
                conv_queue.pop(i)
                slot = conv_rr[0] % 2
                conv_rr[0] += 1
                t.dma(wci, wst2[slot][:, 0:width], src_ap,
                      writes=[("wst2", slot)])
                eng = CAST_ENGS[cast_rr[0] % len(CAST_ENGS)]
                cast_rr[0] += 1
                cast(wbst[slot][:, 0:width], wst2[slot][:, 0:width],
                     reads=[("wst2", slot)], writes=[("wbst", slot)],
                     engine=eng)
                t.dma(wco, dst_ap, wbst[slot][:, 0:width],
                      reads=[("wbst", slot)], writes=[dkey])
                done += 1

        def ln_stats(y_aps, ykeys, W, eps, pbs, pbs_keys, srow, srow_keys,
                     sq_buf, sq_key0, sq_full=None, y_full=None):
            ksum, ksq, kmu_b, kr_b = pbs_keys
            kmu, kr = srow_keys[0], srow_keys[1]
            if sq_full is not None:
                fullap, fullkeys, sq_chunk0 = sq_full
                t.op("scalar", lambda s: s.activation(
                    fullap, y_full, AF.Square),
                    reads=ykeys, writes=fullkeys)
                sq_is_bf = True
                sq_chunk = sq_chunk0
                sq_key = lambda c: fullkeys[c]
            else:
                sq_is_bf = False
                sq_chunk = lambda c: sq_buf
                sq_key = lambda c: sq_key0
            for c in range(KC):
                if sq_full is None:
                    t.op("scalar", lambda s, c=c: s.activation(
                        sq_buf, y_aps[c], AF.Square),
                        reads=[ykeys[c]], writes=[sq_key0])
                t.op("tensor", lambda te, c=c: te.matmul(
                    pbs[0][0:1, 0:W], ones_col[:, 0:1], y_aps[c],
                    start=(c == 0), stop=(c == KC - 1)),
                    reads=[ykeys[c], ("ones_col",)], writes=[ksum])
                ones_sq = ones_col_bf if sq_is_bf else ones_col
                okey = ("ones_cb",) if sq_is_bf else ("ones_col",)
                t.op("tensor", lambda te, c=c, ones_sq=ones_sq: te.matmul(
                    pbs[1][0:1, 0:W], ones_sq[:, 0:1], sq_chunk(c),
                    start=(c == 0), stop=(c == KC - 1)),
                    reads=[sq_key(c), okey], writes=[ksq])
            mu = srow[0:1, 0:W]
            sc = srow[0:1, W:2 * W]
            rq = srow[0:1, 2 * W:3 * W]
            tq = srow[0:1, 3 * W:4 * W]
            I32 = mybir.dt.int32
            MAGIC = 0x5F3759DF
            t.op("vector", lambda v: v.tensor_scalar(
                mu, pbs[0][0:1, 0:W], 1.0 / D, None, op0=Op.mult),
                reads=[ksum], writes=[kmu])
            t.op("vector", lambda v: v.tensor_tensor(sc, mu, mu, op=Op.mult),
                 reads=[kmu], writes=[kr])
            t.op("vector", lambda v: v.scalar_tensor_tensor(
                sc, pbs[1][0:1, 0:W], 1.0 / D, sc, op0=Op.mult, op1=Op.subtract),
                reads=[ksq, kr], writes=[kr])
            t.op("vector", lambda v: v.tensor_scalar(
                sc, sc, eps, None, op0=Op.add), reads=[kr], writes=[kr])
            t.op("vector", lambda v: v.tensor_scalar(
                rq.bitcast(I32), sc.bitcast(I32), 1, None,
                op0=Op.arith_shift_right), reads=[kr], writes=[kr])
            t.op("vector", lambda v: v.tensor_scalar(
                rq.bitcast(I32), rq.bitcast(I32), -1, MAGIC,
                op0=Op.mult, op1=Op.add), reads=[kr], writes=[kr])
            for _ in range(2):
                t.op("vector", lambda v: v.tensor_tensor(tq, rq, rq, op=Op.mult),
                     reads=[kr], writes=[kr])
                t.op("vector", lambda v: v.tensor_tensor(tq, tq, sc, op=Op.mult),
                     reads=[kr], writes=[kr])
                t.op("vector", lambda v: v.tensor_scalar(
                    tq, tq, -0.5, 1.5, op0=Op.mult, op1=Op.add),
                    reads=[kr], writes=[kr])
                t.op("vector", lambda v: v.tensor_tensor(rq, rq, tq, op=Op.mult),
                     reads=[kr], writes=[kr])
            t.op("tensor", lambda te: te.matmul(
                pbs[2][:, 0:W], ones_row[0:1, :], mu, start=True, stop=True),
                reads=[kmu, ("ones_row",)], writes=[kmu_b])
            t.op("tensor", lambda te: te.matmul(
                pbs[3][:, 0:W], ones_row[0:1, :], rq, start=True, stop=True),
                reads=[kr, ("ones_row",)], writes=[kr_b])

        def ln_junction(tag, xin_v, xin_pref, add_v, add_pref, xsum_v, xsum_pref,
                        nout_v, nout_pref, eps):
            with contextlib.ExitStack() as seg:
                jb = seg.enter_context(nc.sbuf_tensor(f"jb_{tag}", [128, KC * TTA], F32))
                pool = [seg.enter_context(nc.sbuf_tensor(f"jpool_{tag}{i}", [128, TTA], F32))
                        for i in range(5)]
                srow = seg.enter_context(nc.sbuf_tensor(f"jsrow_{tag}", [1, 2048], F32))
                ob = [seg.enter_context(nc.sbuf_tensor(f"job_{tag}{i}", [128, TTA], BF16))
                      for i in range(4)]
                pbs = [seg.enter_context(nc.psum_tensor(f"jps_{tag}{i}", [128, 512], F32))
                       for i in range(4)]
                pk = [("jps", i) for i in range(4)]
                sk = [("jsrow", i) for i in range(4)]
                for ti in range(NTA):
                    sl = slice(ti * TTA, (ti + 1) * TTA)
                    for c in range(KC):
                        ysl = jb[:, c * TTA:(c + 1) * TTA]
                        if add_v is None:
                            t.dma(lx, ysl, xin_v[:, c, sl],
                                  reads=[(xin_pref, c, 2 * ti + h) for h in range(2)]
                                  if xin_pref else [],
                                  writes=[("jb", c)])
                        else:
                            t.dma(lx, pool[c % 2][:], xin_v[:, c, sl],
                                  reads=[(xin_pref, c, 2 * ti + h) for h in range(2)]
                                  if xin_pref else [],
                                  writes=[("jpool", c % 2)])
                            t.dma(lx, pool[2 + c % 2][:], add_v[:, c, sl],
                                  reads=[(add_pref, c, 2 * ti + h) for h in range(2)]
                                  if add_pref else [],
                                  writes=[("jpool", 2 + c % 2)])
                            t.op("vector", lambda v, c=c: v.tensor_tensor(
                                jb[:, c * TTA:(c + 1) * TTA], pool[c % 2][:],
                                pool[2 + c % 2][:], op=Op.add),
                                reads=[("jpool", c % 2), ("jpool", 2 + c % 2)],
                                writes=[("jb", c)])
                        if xsum_v is not None:
                            t.dma(sx, xsum_v[:, c, sl], jb[:, c * TTA:(c + 1) * TTA],
                                  reads=[("jb", c)],
                                  writes=[(xsum_pref, c, 2 * ti + h) for h in range(2)])
                    y_aps = [jb[:, c * TTA:(c + 1) * TTA] for c in range(KC)]
                    ykeys = [("jb", c) for c in range(KC)]
                    ln_stats(y_aps, ykeys, TTA, eps, pbs, pk, srow, sk,
                             pool[4][:], ("jpool", 4))
                    for c in range(KC):
                        t.op("vector", lambda v, c=c: v.scalar_tensor_tensor(
                            pool[c % 4][:], y_aps[c], 0.0, pbs[2][:, 0:TTA],
                            op0=Op.bypass, op1=Op.subtract),
                            reads=[ykeys[c], pk[2]], writes=[("jpool", c % 4)])
                        t.op("vector", lambda v, c=c: v.scalar_tensor_tensor(
                            ob[c % 4][:], pool[c % 4][:], 0.0, pbs[3][:, 0:TTA],
                            op0=Op.bypass, op1=Op.mult),
                            reads=[("jpool", c % 4), pk[3]], writes=[("job", c % 4)])
                        t.dma(sx, nout_v[:, c, sl], ob[c % 4][:],
                              reads=[("job", c % 4)],
                              writes=[(nout_pref, c, 2 * ti + h) for h in range(2)])
                t.barrier(STREAMS)

        ln_junction("j0", x0_v, None, None, None, None, None, strm_v, "strm", EPS_MOD)
        stop_now = (upto == "j0")

        if not stop_now:
         with contextlib.ExitStack() as seg:
            ssb = lambda name, shape, dt: seg.enter_context(
                nc.sbuf_tensor(name, list(shape), dt))
            w1_bf = ssb("w1_bf", (128, KC * FF), BF16)
            w2_bf = ssb("w2_bf", (128, FFC * D), BF16)
            wo_bf = ssb("wo_bf", (128, KC * D), BF16)
            wcb_bf = [ssb(f"wcb_bf{i}", (128, KC * 128), BF16) for i in range(2)]
            xa_bf = [ssb(f"xa_bf{i}", (128, KC * TTA), BF16) for i in range(1)]
            k_sc = ssb("k_sc", (128, TTA), F32)
            pool = [ssb(f"apool{i}", (128, TTA), F32) for i in range(2)]
            bfs = ssb("bfs", (128, TTA), BF16)
            kv_acc = ssb("kv_acc", (128, 32), F32)
            ks_acc = ssb("ks_acc", (128, 32), F32)
            kvks = ssb("kvks", (128, 32), F32)
            kvred = ssb("kvred", (128, 32), F32)
            ksr = ssb("ksr", (128, 16), F32)
            kvdks = ssb("kvdks", (128, 16), F32)
            y1 = ssb("y1", (128, KC * TTB), F32)
            xb_bf = ssb("xb_bf", (128, KC * TTB), BF16)
            qb_bf = ssb("qb_bf", (128, KC * TTB), BF16)
            h_bf = ssb("h_bf", (128, FFC * TTB), BF16)
            srow = ssb("asrow", (1, 1024), F32)
            cab = [ssb(f"cab{i}", (128, 104), F32) for i in range(2)]
            pb = [seg.enter_context(nc.psum_tensor(f"apb{i}", [128, 512], F32))
                  for i in range(8)]
            pbk = [("apb", i) for i in range(8)]
            srk = [("asrow", 0), ("asrow", 1)]
            pA, pB = pool[0][:, 0:TTB], pool[0][:, TTB:2 * TTB]
            pC, pD = pool[1][:, 0:TTB], pool[1][:, TTB:2 * TTB]
            kA, kB, kC, kD = ("ap0a",), ("ap0b",), ("ap1a",), ("ap1b",)
            k0full = [kA, kB]
            k1full = [kC, kD]
            LOCAL = {"bq": 0, "bk": 8, "bv": 16, "bo": 24, "bff2": 32,
                     "ln1s": 40, "ln1b": 48, "ln2s": 56, "ln2b": 64, "bff1": 72}

            def load_cab(blk):
                t.dma(wmo, cab[blk % 2][:],
                      consts_in.ap()[:, blk * 104:(blk + 1) * 104],
                      writes=[("cab", blk % 2)])

            WQKV = {"q": wq_cb, "k": wk_cb, "v": wv_cb}
            WQI = {"q": 0, "k": 1, "v": 2}

            def enq_conv_block(blk):
                for which in ("k", "v", "q"):
                    wi = WQI[which]
                    for mc in range(KC):
                        ap = WQKV[which].ap()[blk, mc].rearrange(
                            "(kc p) c -> p kc c", p=128)
                        for hf in range(2):
                            conv_queue.append((
                                None,
                                ap[:, 4 * hf:4 * hf + 4, :],
                                scr_qkvc.ap()[blk, wi, mc][:, hf * 512:(hf + 1) * 512],
                                ("cqkv", blk, wi, mc, hf), 512))
                for kc in range(KC):
                    ap = wo_w.ap()[blk, kc * 128:(kc + 1) * 128, :]
                    for hf in range(2):
                        conv_queue.append((
                            blk,
                            ap[:, hf * 512:(hf + 1) * 512],
                            scr_woc.ap()[blk % 2][:, kc * D + hf * 512:kc * D + (hf + 1) * 512],
                            ("cwo", blk % 2, 2 * kc + hf), 512))
                for kc in range(KC):
                    for q in range(8):
                        conv_queue.append((
                            blk,
                            w1_w.ap()[blk, kc * 128:(kc + 1) * 128, q * 512:(q + 1) * 512],
                            scr_w1c.ap()[blk % 2][:, kc * FF + q * 512:kc * FF + (q + 1) * 512],
                            ("cw1", blk % 2, kc * 8 + q), 512))
                for kh in range(FFC):
                    for hf in range(2):
                        conv_queue.append((
                            blk,
                            w2_w.ap()[blk, kh * 128:(kh + 1) * 128, hf * 512:(hf + 1) * 512],
                            scr_w2c.ap()[blk % 2][:, kh * D + hf * 512:kh * D + (hf + 1) * 512],
                            ("cw2", blk % 2, 2 * kh + hf), 512))

            def refill_qkv(which, blk, mc, slot):
                wi = WQI[which]
                t.dma(wcb, wcb_bf[slot][:], scr_qkvc.ap()[blk, wi, mc],
                      reads=[("cqkv", blk, wi, mc, hf) for hf in range(2)],
                      writes=[("wcb", slot, hf) for hf in range(4)])

            def refill_wo(blk):
                t.dma(wso, wo_bf[:], scr_woc.ap()[blk % 2],
                      reads=[("cwo", blk % 2, i) for i in range(16)],
                      writes=[("wo", kc, hf) for kc in range(KC) for hf in range(2)])

            def refill_w1(blk):
                t.dma(ws1, w1_bf[:], scr_w1c.ap()[blk % 2],
                      reads=[("cw1", blk % 2, i) for i in range(64)],
                      writes=[("w1", kc, q) for kc in range(KC) for q in range(8)])

            def refill_w2(blk):
                t.dma(ws2, w2_bf[:], scr_w2c.ap()[blk % 2],
                      reads=[("cw2", blk % 2, i) for i in range(64)],
                      writes=[("w2", kh, hf) for kh in range(FFC) for hf in range(2)])
                refilled_w[0] = max(refilled_w[0], blk)

            for b_ in range(NB):
                enq_conv_block(b_)
            for r_ in range(NEXP):
                apu = up_t.ap()[r_].rearrange("(kc p) e -> p kc e", p=128)
                for q in range(8):
                    conv_queue.append((None, apu[:, q:q + 1, :],
                                       scr_wupc.ap()[r_][:, q * 512:(q + 1) * 512],
                                       ("cwup", r_, q), 512))
                apd = down_t.ap()[r_].rearrange("(kc p) h -> p kc h", p=128)
                for q in range(8):
                    conv_queue.append((
                        None,
                        apd[:, q // 2:q // 2 + 1, (q % 2) * 512:(q % 2) * 512 + 512],
                        scr_wdnc.ap()[r_][:, q * 512:(q + 1) * 512],
                        ("cwdn", r_, q), 512))
            for kc in range(KC):
                ap = wqkvb_t.ap()[kc * 128:(kc + 1) * 128, :]
                for q in range(7):
                    w = min(512, QKVB_O - q * 512)
                    conv_queue.append((
                        None,
                        ap[:, q * 512:q * 512 + w],
                        scr_qkvbc.ap()[:, kc * QKVB_O + q * 512:kc * QKVB_O + q * 512 + w],
                        ("cqkvb", kc, q), w))
            for kc in range(KC):
                ap = wob_t.ap()[kc * 128:(kc + 1) * 128, :]
                for hf in range(2):
                    conv_queue.append((
                        None,
                        ap[:, hf * 512:(hf + 1) * 512],
                        scr_wobc.ap()[:, kc * D + hf * 512:kc * D + (hf + 1) * 512],
                        ("cwob", kc, hf), 512))

            load_cab(0)
            conv_pump(64)

            for blk in range(nblocks):
                cn = lambda name, j=0: cab[blk % 2][:, LOCAL[name] + j:LOCAL[name] + j + 1]
                ck = ("cab", blk % 2)

                for ti in range(NTA):
                    sl = slice(ti * TTA, (ti + 1) * TTA)
                    t.dma(lx, xa_bf[0][:].rearrange(
                        "p (kc c) -> p kc c", kc=KC), strm_v[:, :, sl],
                        reads=[("strm", c, 2 * ti + h)
                               for c in range(KC) for h in range(2)],
                        writes=[("xa", 0)])
                    seq = []
                    for c in range(KC):
                        seq += [("k", c), ("v", c)]
                    seq += [("q", mc) for mc in range(KC)]
                    refill_qkv(seq[0][0], blk, seq[0][1], 0)
                    refill_qkv(seq[1][0], blk, seq[1][1], 1)
                    for c in range(KC):
                        iK, iV = 2 * c, 2 * c + 1
                        bK = pb[c % 2]
                        for kc in range(KC):
                            t.op("tensor", lambda te, kc=kc, c=c, bK=bK: te.matmul(
                                bK[:], wcb_bf[0][:, kc * 128:(kc + 1) * 128],
                                xa_bf[0][:, kc * TTA:(kc + 1) * TTA],
                                start=(kc == 0), stop=(kc == KC - 1)),
                                reads=[("wcb", 0, kc // 2), ("xa", 0)],
                                writes=[pbk[c % 2]], inc=(kc == KC - 1))
                        t.op("vector", lambda v, bK=bK, c=c: v.tensor_scalar(
                            pool[0][:], bK[:], cn("bk", c), 0.0,
                            op0=Op.add, op1=Op.min),
                            reads=[pbk[c % 2], ck], writes=k0full)
                        t.op("scalar", lambda s: s.activation(
                            pool[0][:], pool[0][:], AF.Exp),
                            reads=k0full, writes=k0full)
                        t.op("vector", lambda v, bK=bK, c=c: v.tensor_scalar(
                            pool[1][:], bK[:], cn("bk", c), 0.0,
                            op0=Op.add, op1=Op.max),
                            reads=[pbk[c % 2], ck], writes=k1full)
                        t.op("vector", lambda v, c=c, ti=ti: v.scalar_tensor_tensor(
                            k_sc[:], pool[1][:], 0.0, pool[0][:],
                            op0=Op.bypass, op1=Op.add,
                            accum_out=ks_acc[:, c * 4 + ti:c * 4 + ti + 1]),
                            reads=k0full + k1full,
                            writes=[("k_sc",), ("ksacc", c, ti)])
                        conv_pump(1)
                        if iK + 2 < len(seq):
                            refill_qkv(seq[iK + 2][0], blk, seq[iK + 2][1], 0)
                        bV = pb[2 + c % 2]
                        for kc in range(KC):
                            t.op("tensor", lambda te, kc=kc, c=c, bV=bV: te.matmul(
                                bV[:], wcb_bf[1][:, kc * 128:(kc + 1) * 128],
                                xa_bf[0][:, kc * TTA:(kc + 1) * TTA],
                                start=(kc == 0), stop=(kc == KC - 1)),
                                reads=[("wcb", 1, kc // 2), ("xa", 0)],
                                writes=[pbk[2 + c % 2]], inc=(kc == KC - 1))
                        t.op("vector", lambda v, bV=bV, c=c, ti=ti: v.scalar_tensor_tensor(
                            k_sc[:], bV[:], cn("bv", c), k_sc[:],
                            op0=Op.add, op1=Op.mult,
                            accum_out=kv_acc[:, c * 4 + ti:c * 4 + ti + 1]),
                            reads=[pbk[2 + c % 2], ck, ("k_sc",)],
                            writes=[("k_sc",), ("kvacc", c, ti)])
                        conv_pump(1)
                        if iV + 2 < len(seq):
                            refill_qkv(seq[iV + 2][0], blk, seq[iV + 2][1], 1)
                    if ti == NTA - 1:
                        accr = [("kvacc", c, tt) for c in range(KC)
                                for tt in range(NTA)] + \
                               [("ksacc", c, tt) for c in range(KC)
                                for tt in range(NTA)]
                        t.op("vector", lambda v: v.tensor_tensor(
                            kvks[:, 0:32:4], kv_acc[:, 0:32:4], kv_acc[:, 1:32:4],
                            op=Op.add), reads=accr, writes=[("kvks",)], inc=False)
                        t.op("vector", lambda v: v.tensor_tensor(
                            kvks[:, 1:32:4], kv_acc[:, 2:32:4], kv_acc[:, 3:32:4],
                            op=Op.add), reads=accr, writes=[("kvks",)], inc=False)
                        t.op("vector", lambda v: v.tensor_tensor(
                            kvks[:, 2:32:4], ks_acc[:, 0:32:4], ks_acc[:, 1:32:4],
                            op=Op.add), reads=accr, writes=[("kvks",)], inc=False)
                        t.op("vector", lambda v: v.tensor_tensor(
                            kvks[:, 3:32:4], ks_acc[:, 2:32:4], ks_acc[:, 3:32:4],
                            op=Op.add), reads=accr, writes=[("kvks",)])
                        t.dma(sx, cc_in.ap(), kvks[:], reads=[("kvks",)],
                              writes=[("ccin",)])
                        if no_cc:
                            t.dma(sx, cc_out.ap(), cc_in.ap(),
                                  reads=[("ccin",)], writes=[("ccout",)])
                        else:
                            t.cc(ccs, "AllReduce", Op.add, [list(range(NCORE))],
                                 [cc_in.ap().opt()], [cc_out.ap().opt()],
                                 reads=[("ccin",)], writes=[("ccout",)])
                        t.dma(lx, kvred[:], cc_out.ap(), reads=[("ccout",)],
                              writes=[("kvred",)])
                        t.op("vector", lambda v: v.reciprocal(
                            ksr[:, 0:16:2], kvred[:, 2:32:4]),
                            reads=[("kvred",)], writes=[("ksr",)], inc=False)
                        t.op("vector", lambda v: v.reciprocal(
                            ksr[:, 1:16:2], kvred[:, 3:32:4]),
                            reads=[("kvred",)], writes=[("ksr",)])
                        t.op("vector", lambda v: v.tensor_tensor(
                            kvdks[:, 0:16:2], kvred[:, 0:32:4], ksr[:, 0:16:2],
                            op=Op.mult),
                            reads=[("kvred",), ("ksr",)], writes=[("kvdks",)],
                            inc=False)
                        t.op("vector", lambda v: v.tensor_tensor(
                            kvdks[:, 1:16:2], kvred[:, 1:32:4], ksr[:, 1:16:2],
                            op=Op.mult),
                            reads=[("kvred",), ("ksr",)], writes=[("kvdks",)])
                    for mc in range(KC):
                        i_q = 16 + mc
                        bQ = pb[4 + mc % 2]
                        for kc in range(KC):
                            t.op("tensor", lambda te, kc=kc, mc=mc, bQ=bQ: te.matmul(
                                bQ[:], wcb_bf[mc % 2][:, kc * 128:(kc + 1) * 128],
                                xa_bf[0][:, kc * TTA:(kc + 1) * TTA],
                                start=(kc == 0), stop=(kc == KC - 1)),
                                reads=[("wcb", mc % 2, kc // 2), ("xa", 0)],
                                writes=[pbk[4 + mc % 2]], inc=(kc == KC - 1))
                        t.op("vector", lambda v, bQ=bQ, mc=mc: v.tensor_scalar(
                            pool[0][:], bQ[:], cn("bq", mc), 0.0,
                            op0=Op.add, op1=Op.min),
                            reads=[pbk[4 + mc % 2], ck], writes=k0full)
                        t.op("scalar", lambda s: s.activation(
                            pool[0][:], pool[0][:], AF.Exp),
                            reads=k0full, writes=k0full)
                        t.op("vector", lambda v, bQ=bQ, mc=mc: v.tensor_scalar(
                            pool[1][:], bQ[:], cn("bq", mc), 0.0,
                            op0=Op.add, op1=Op.max),
                            reads=[pbk[4 + mc % 2], ck], writes=k1full)
                        t.op("vector", lambda v: v.scalar_tensor_tensor(
                            bfs[:], pool[1][:], 0.0, pool[0][:],
                            op0=Op.bypass, op1=Op.add),
                            reads=k0full + k1full, writes=[("bfs",)])
                        t.dma(sx, q_v[:, mc, sl], bfs[:],
                              reads=[("bfs",)],
                              writes=[("scrq", mc, 2 * ti), ("scrq", mc, 2 * ti + 1)])
                        conv_pump(2)
                        if i_q + 2 < len(seq):
                            refill_qkv(seq[i_q + 2][0], blk, seq[i_q + 2][1], mc % 2)


                if blk == 0:
                    refill_wo(0)
                    refill_w1(0)
                    refill_w2(0)
                vw = y1[:]

                def load_qb(j):
                    slj = slice(j * TTB, (j + 1) * TTB)
                    t.dma(lx, qb_bf[:].rearrange("p (kc c) -> p kc c", kc=KC),
                          q_v[:, :, slj],
                          reads=[("scrq", mc, j) for mc in range(KC)],
                          writes=[("qb", c) for c in range(KC)])

                def load_xb(j):
                    slj = slice(j * TTB, (j + 1) * TTB)
                    t.dma(lx, xb_bf[:].rearrange("p (kc c) -> p kc c", kc=KC),
                          strm_v[:, :, slj],
                          reads=[("strm", c, j) for c in range(KC)],
                          writes=[("xb", c) for c in range(KC)])

                def v_stage(j):
                    n = j // 4
                    for c in range(KC):
                        t.op("vector", lambda v, c=c, n=n: v.tensor_scalar(
                            vw[:, c * TTB:(c + 1) * TTB],
                            qb_bf[:, c * TTB:(c + 1) * TTB],
                            kvred[:, c * 4 + 2 + n:c * 4 + 3 + n], None, op0=Op.mult),
                            reads=[("qb", c), ("kvred",)], writes=[("y1", c)])
                    ally = [("y1", c) for c in range(KC)]
                    t.op("scalar", lambda s: s.activation(vw, vw, AF.Ln),
                         reads=ally, writes=ally)
                    t.op("scalar", lambda s: s.activation(
                        vw, vw, AF.Sigmoid, bias=NEG_LN_EPS_ATTN),
                        reads=ally + [("fconst",)], writes=ally)
                    for c in range(KC):
                        t.op("vector", lambda v, c=c, n=n: v.tensor_scalar(
                            qb_bf[:, c * TTB:(c + 1) * TTB],
                            vw[:, c * TTB:(c + 1) * TTB],
                            kvdks[:, c * 2 + n:c * 2 + n + 1], None, op0=Op.mult),
                            reads=[("y1", c), ("kvdks",)], writes=[("qb", c)])

                load_qb(0)
                load_xb(0)
                v_stage(0)
                for j in range(NTB):
                    slj = slice(j * TTB, (j + 1) * TTB)
                    if False:
                        pass
                    for mc in range(KC):
                        bO = pb[mc % 2]
                        for kc in range(KC):
                            t.op("tensor", lambda te, mc=mc, kc=kc, bO=bO: te.matmul(
                                bO[:, 0:TTB],
                                wo_bf[:, kc * D + mc * 128:kc * D + mc * 128 + 128],
                                qb_bf[:, kc * TTB:(kc + 1) * TTB],
                                start=(kc == 0), stop=(kc == KC - 1)),
                                reads=[("wo", kc, mc // 4), ("qb", kc)],
                                writes=[pbk[mc % 2]], inc=(kc == KC - 1))
                        t.op("scalar", lambda s, mc=mc, bO=bO: s.activation(
                            y1[:, mc * TTB:(mc + 1) * TTB], bO[:, 0:TTB], GELU_T,
                            bias=cn("bo", mc)),
                            reads=[pbk[mc % 2], ck], writes=[("y1", mc)])
                    t.op("vector", lambda v: v.scalar_tensor_tensor(
                        y1[:], y1[:], 0.0, xb_bf[:],
                        op0=Op.bypass, op1=Op.add),
                        reads=[("y1", m) for m in range(KC)] +
                              [("xb", c) for c in range(KC)],
                        writes=[("y1", m) for m in range(KC)])
                    y_aps = [y1[:, c * TTB:(c + 1) * TTB] for c in range(KC)]
                    ykeys = [("y1", c) for c in range(KC)]
                    ln_stats(y_aps, ykeys, TTB, EPS_LN_A, pb[4:8], pbk[4:8],
                             srow, srk, pD, kD)
                    mub = pb[6][:, 0:TTB].rearrange(
                        "p (r c) -> p r c", r=1).broadcast_to((128, KC, TTB))
                    t.op("vector", lambda v: v.scalar_tensor_tensor(
                        y1[:].rearrange("p (r c) -> p r c", r=KC),
                        y1[:].rearrange("p (r c) -> p r c", r=KC),
                        0.0, mub, op0=Op.bypass, op1=Op.subtract),
                        reads=ykeys + [pbk[6]], writes=ykeys)
                    for c in range(KC):
                        t.op("vector", lambda v, c=c: v.scalar_tensor_tensor(
                            pB, y_aps[c], cn("ln1s", c),
                            pb[7][:, 0:TTB], op0=Op.mult, op1=Op.mult),
                            reads=[ykeys[c], pbk[7], ck], writes=[kB])
                        t.op("vector", lambda v, c=c: v.tensor_scalar(
                            xb_bf[:, c * TTB:(c + 1) * TTB], pB,
                            cn("ln1b", c), None, op0=Op.add),
                            reads=[kB, ck], writes=[("xb", c)])
                    if j + 1 < NTB and HOIST_V:
                        load_qb(j + 1)
                        v_stage(j + 1)
                    for mh in range(FFC):
                        bH = pb[2 + mh % 2]
                        for kc in range(KC):
                            t.op("tensor", lambda te, mh=mh, kc=kc, bH=bH: te.matmul(
                                bH[:, 0:TTB],
                                w1_bf[:, kc * FF + mh * 128:kc * FF + mh * 128 + 128],
                                xb_bf[:, kc * TTB:(kc + 1) * TTB],
                                start=(kc == 0), stop=(kc == KC - 1)),
                                reads=[("w1", kc, mh // 4), ("xb", kc)],
                                writes=[pbk[2 + mh % 2]], inc=(kc == KC - 1))
                        t.op("scalar", lambda s, mh=mh, bH=bH: s.activation(
                            h_bf[:, mh * TTB:(mh + 1) * TTB], bH[:, 0:TTB], GELU_T,
                            bias=cn("bff1", mh)),
                            reads=[pbk[2 + mh % 2], ck], writes=[("h", mh)])
                    conv_pump(26)
                    for mc in range(KC):
                        bF = pb[mc % 2]
                        for kh in range(FFC):
                            t.op("tensor", lambda te, mc=mc, kh=kh, bF=bF: te.matmul(
                                bF[:, 0:TTB],
                                w2_bf[:, kh * D + mc * 128:kh * D + mc * 128 + 128],
                                h_bf[:, kh * TTB:(kh + 1) * TTB],
                                start=(kh == 0), stop=(kh == FFC - 1)),
                                reads=[("w2", kh, mc // 4), ("h", kh)],
                                writes=[pbk[mc % 2]], inc=(kh == FFC - 1))
                        t.op("scalar", lambda s, mc=mc, bF=bF: s.activation(
                            pC, bF[:, 0:TTB], GELU_T, bias=cn("bff2", mc)),
                            reads=[pbk[mc % 2], ck], writes=[kC])
                        t.op("vector", lambda v, mc=mc: v.scalar_tensor_tensor(
                            y1[:, mc * TTB:(mc + 1) * TTB], pC, 0.0,
                            xb_bf[:, mc * TTB:(mc + 1) * TTB],
                            op0=Op.bypass, op1=Op.add),
                            reads=[kC, ("xb", mc)], writes=[("y1", mc)])
                    ln_stats(y_aps, ykeys, TTB, EPS_LN_A, pb[4:8], pbk[4:8],
                             srow, srk, pD, kD)
                    mub = pb[6][:, 0:TTB].rearrange(
                        "p (r c) -> p r c", r=1).broadcast_to((128, KC, TTB))
                    t.op("vector", lambda v: v.scalar_tensor_tensor(
                        y1[:].rearrange("p (r c) -> p r c", r=KC),
                        y1[:].rearrange("p (r c) -> p r c", r=KC),
                        0.0, mub, op0=Op.bypass, op1=Op.subtract),
                        reads=ykeys + [pbk[6]], writes=ykeys)
                    for c in range(KC):
                        t.op("vector", lambda v, c=c: v.scalar_tensor_tensor(
                            pB, y_aps[c], cn("ln2s", c),
                            pb[7][:, 0:TTB], op0=Op.mult, op1=Op.mult),
                            reads=[kA, pbk[7], ck], writes=[kB])
                        if blk < NB - 1:
                            t.op("vector", lambda v, c=c: v.tensor_scalar(
                                xb_bf[:, c * TTB:(c + 1) * TTB], pB, cn("ln2b", c),
                                None, op0=Op.add),
                                reads=[kB, ck], writes=[("xb", c)])
                        else:
                            t.op("vector", lambda v, c=c: v.tensor_scalar(
                                y1[:, c * TTB:(c + 1) * TTB], pB, cn("ln2b", c),
                                None, op0=Op.add),
                                reads=[kB, ck], writes=[("y1", c)])
                            t.dma(sx, af32_v[:, c, slj],
                                  y1[:, c * TTB:(c + 1) * TTB],
                                  reads=[("y1", c)], writes=[("af32", c, j)])
                    if blk < NB - 1:
                        t.dma(sx, strm_v[:, :, slj],
                              xb_bf[:].rearrange("p (kc c) -> p kc c", kc=KC),
                              reads=[("xb", c) for c in range(KC)],
                              writes=[("strm", c, j) for c in range(KC)])
                    if j + 1 < NTB:
                        load_xb(j + 1)
                        if not HOIST_V:
                            load_qb(j + 1)
                            v_stage(j + 1)
                if blk < NB - 1:
                    load_cab(blk + 1)
                    refill_wo(blk + 1)
                    refill_w1(blk + 1)
                    refill_w2(blk + 1)
            conv_pump(10 ** 9)
            t.barrier(STREAMS)

        stop_now = stop_now or (upto == "aa")
        if not stop_now:
            ln_junction("j1", x0_v, None, af32_v, "af32", x1_v, "x1", n1_v, "n1", EPS_MOD)
        stop_now = stop_now or (upto == "j1")

        def moe_call(tag, xn_v, xn_pref, xp_v, xp_pref, xout_v, xout_pref):
            with contextlib.ExitStack() as seg:
                ssb = lambda name, shape, dt: seg.enter_context(
                    nc.sbuf_tensor(name, list(shape), dt))
                rtr_bf = ssb(f"rtr_bf_{tag}", (128, KC * 16), BF16)
                lgf = ssb(f"lgf_{tag}", (16, TTB), F32)
                lg_tm = ssb(f"lg_tm_{tag}", (128, 16), F32)
                e_tm = ssb(f"e_tm_{tag}", (128, 16), F32)
                est = ssb(f"est_{tag}", (128, 4), F32)
                cnt_tm = ssb(f"cnt_tm_{tag}", (128, 16), F32)
                gate_tm = ssb(f"gate_tm_{tag}", (128, 16), F32)
                gatef = ssb(f"gatef_{tag}", (16, T), F32)
                sel_all = ssb(f"sel_all_{tag}", (16, 16 * 128), F32)
                identF = ssb(f"identF_{tag}", (128, 128), F32)
                xn_sb = ssb(f"mxn_{tag}", (128, KC * TTB), BF16)
                wup_bf = [ssb(f"wup_{tag}{i}", (128, KC * 512), BF16) for i in range(8)]
                wdn_bf = [ssb(f"wdn_{tag}{i}", (128, 4 * D), BF16) for i in range(8)]
                act_bf = ssb(f"act_bf_{tag}", (128, 16 * TTB), BF16)
                acc_bf = ssb(f"acc_bf_{tag}", (128, KC * T), BF16)
                pool = [ssb(f"mpool_{tag}{i}", (128, TTB), F32) for i in range(5)]
                pbm = [seg.enter_context(nc.psum_tensor(f"mps_{tag}{i}", [128, 512], F32))
                       for i in range(8)]
                pmk = [("mps", i) for i in range(8)]

                t.op("gpsimd", lambda g: g.memset(identF[:], 0.0), writes=[("iF",)])
                t.op("gpsimd", lambda g: g.affine_select(
                    out=identF[:], in_=identF[:], compare_op=Op.not_equal,
                    fill=1.0, base=0, pattern=[[-1, 128]], channel_multiplier=1),
                    reads=[("iF",)], writes=[("iF",)])
                t.op("gpsimd", lambda g: g.memset(sel_all[:], 1.0), writes=[("sel",)])
                t.op("gpsimd", lambda g: g.affine_select(
                    out=sel_all[:], in_=sel_all[:], compare_op=Op.is_ge,
                    fill=0.0, base=0, pattern=[[1, 16 * 128]],
                    channel_multiplier=-128),
                    reads=[("sel",)], writes=[("sel",)])
                t.op("gpsimd", lambda g: g.affine_select(
                    out=sel_all[:], in_=sel_all[:], compare_op=Op.is_ge,
                    fill=0.0, base=127, pattern=[[-1, 16 * 128]],
                    channel_multiplier=128),
                    reads=[("sel",)], writes=[("sel",)])
                ap = router_t.ap().rearrange("(kc p) e -> p kc e", p=128)
                load_cast2(wmo, ap[:, :, :], rtr_bf[:], ("rtr",), 0, KC * 16)

                for j in range(NTB):
                    slj = slice(j * TTB, (j + 1) * TTB)
                    t.dma(lx, xn_sb[:].rearrange("p (kc c) -> p kc c", kc=KC),
                          xn_v[:, :, slj],
                          reads=[(xn_pref, c, j) for c in range(KC)],
                          writes=[("mxn",)])
                    bL = pbm[0]
                    for kc in range(KC):
                        t.op("tensor", lambda te, kc=kc, bL=bL: te.matmul(
                            bL[0:16, 0:TTB], rtr_bf[:, kc * 16:(kc + 1) * 16],
                            xn_sb[:, kc * TTB:(kc + 1) * TTB],
                            start=(kc == 0), stop=(kc == KC - 1)),
                            reads=[("rtr",), ("mxn",)], writes=[pmk[0]],
                            inc=(kc == KC - 1))
                    t.op("vector", lambda v, bL=bL: v.tensor_copy(
                        lgf[:, 0:TTB], bL[0:16, 0:TTB]),
                        reads=[pmk[0]], writes=[("lgf",)])
                    for sub in range(2):
                        ss = slice(sub * 128, (sub + 1) * 128)
                        bT = pbm[1]
                        t.op("tensor", lambda te, ss=ss, bT=bT: te.transpose(
                            bT[:, 0:16], lgf[0:16, ss], identF[0:16, 0:16]),
                            reads=[("lgf",), ("iF",)], writes=[pmk[1]])
                        t.op("vector", lambda v, bT=bT: v.tensor_copy(
                            lg_tm[:], bT[:, 0:16]),
                            reads=[pmk[1]], writes=[("lgtm",)])
                        t.op("vector", lambda v: v.tensor_reduce(
                            est[:, 0:1], lg_tm[:], axis=mybir.AxisListType.X,
                            op=Op.max),
                            reads=[("lgtm",)], writes=[("est", 0)])
                        t.op("vector", lambda v: v.tensor_scalar(
                            est[:, 1:2], est[:, 0:1], -1.0, None, op0=Op.mult),
                            reads=[("est", 0)], writes=[("est", 1)])
                        t.op("scalar", lambda s: s.activation(
                            e_tm[:], lg_tm[:], AF.Exp, bias=est[:, 1:2],
                            accum_out=est[:, 2:3]),
                            reads=[("lgtm",), ("est", 1)],
                            writes=[("etm",), ("est", 2)])
                        t.op("vector", lambda v: v.reciprocal(
                            est[:, 3:4], est[:, 2:3]),
                            reads=[("est", 2)], writes=[("est", 3)])
                        for k in range(16):
                            if k == 0:
                                t.op("vector", lambda v, k=k: v.tensor_scalar(
                                    cnt_tm[:], e_tm[:], e_tm[:, k:k + 1], None,
                                    op0=Op.is_lt),
                                    reads=[("etm",)], writes=[("cnt",)])
                            else:
                                t.op("vector", lambda v, k=k: v.scalar_tensor_tensor(
                                    cnt_tm[:], e_tm[:], e_tm[:, k:k + 1], cnt_tm[:],
                                    op0=Op.is_lt, op1=Op.add),
                                    reads=[("etm",), ("cnt",)], writes=[("cnt",)])
                        t.op("vector", lambda v: v.tensor_scalar(
                            cnt_tm[:], cnt_tm[:], 3.0, None, op0=Op.is_le),
                            reads=[("cnt",)], writes=[("cnt",)])
                        t.op("vector", lambda v: v.scalar_tensor_tensor(
                            gate_tm[:], e_tm[:], est[:, 3:4], cnt_tm[:],
                            op0=Op.mult, op1=Op.mult),
                            reads=[("etm",), ("est", 3), ("cnt",)], writes=[("gtm",)])
                        bT2 = pbm[2]
                        t.op("tensor", lambda te, bT2=bT2: te.transpose(
                            bT2[0:16, 0:128], gate_tm[:], identF[:]),
                            reads=[("gtm",), ("iF",)], writes=[pmk[2]])
                        t.op("vector", lambda v, j=j, sub=sub, bT2=bT2: v.tensor_copy(
                            gatef[0:16, j * TTB + sub * 128:j * TTB + (sub + 1) * 128],
                            bT2[0:16, 0:128]),
                            reads=[pmk[2]], writes=[("gatef", j, sub)])


                def load_group(gi):
                    grp = EG[gi]
                    reg = gi % 2
                    for ri, r in enumerate(grp):
                        t.dma(wmo, wup_bf[reg * 4 + ri][:], scr_wupc.ap()[r],
                              reads=[("cwup", r, q) for q in range(8)],
                              writes=[("wup", reg * 4 + ri, q) for q in range(8)])
                        t.dma(wmo, wdn_bf[reg * 4 + ri][:], scr_wdnc.ap()[r],
                              reads=[("cwdn", r, q) for q in range(8)],
                              writes=[("wdn", reg * 4 + ri, q) for q in range(8)])

                load_group(0)
                for gi, grp in enumerate(EG):
                    reg = gi % 2
                    for j in range(NTB):
                        if j == 1 and gi + 1 < len(EG):
                            load_group(gi + 1)
                        slj = slice(j * TTB, (j + 1) * TTB)
                        t.dma(lx, xn_sb[:].rearrange("p (kc c) -> p kc c", kc=KC),
                              xn_v[:, :, slj],
                              reads=[(xn_pref, c, j) for c in range(KC)],
                              writes=[("mxn",)])
                        for ri, r in enumerate(grp):
                            shared = (r >= 16)
                            wkeys = [("wup", reg * 4 + ri, q) for q in range(4)]
                            if not shared:
                                bG = pbm[2]
                                t.op("tensor", lambda te, r=r, j=j, bG=bG: te.matmul(
                                    bG[:, 0:TTB],
                                    sel_all[0:16, r * 128:(r + 1) * 128],
                                    gatef[0:16, j * TTB:(j + 1) * TTB],
                                    start=True, stop=True),
                                    reads=[("sel",)] + [("gatef", j, su) for su in range(2)],
                                    writes=[pmk[2]])
                            for ec in range(4):
                                bU = pbm[ec % 2]
                                for kc in range(KC):
                                    t.op("tensor", lambda te, ri=ri, ec=ec, kc=kc, bU=bU, reg=reg: te.matmul(
                                        bU[:, 0:TTB],
                                        wup_bf[reg * 4 + ri][:, kc * 512 + ec * 128:kc * 512 + ec * 128 + 128],
                                        xn_sb[:, kc * TTB:(kc + 1) * TTB],
                                        start=(kc == 0), stop=(kc == KC - 1)),
                                        reads=[("wup", reg * 4 + ri, kc), ("mxn",)],
                                        writes=[pmk[ec % 2]], inc=(kc == KC - 1))
                                if shared:
                                    t.op("scalar", lambda s, ri=ri, ec=ec, bU=bU: s.activation(
                                        act_bf[:, (ri * 4 + ec) * TTB:(ri * 4 + ec + 1) * TTB],
                                        bU[:, 0:TTB], GELU_E),
                                        reads=[pmk[ec % 2]], writes=[("act", ri, ec)])
                                else:
                                    t.op("scalar", lambda s, ec=ec, bU=bU: s.activation(
                                        pool[4][:], bU[:, 0:TTB], GELU_E),
                                        reads=[pmk[ec % 2]], writes=[("mpool", 4)])
                                    t.op("vector", lambda v, ri=ri, ec=ec: v.tensor_tensor(
                                        act_bf[:, (ri * 4 + ec) * TTB:(ri * 4 + ec + 1) * TTB],
                                        pool[4][:], pbm[2][:, 0:TTB], op=Op.mult),
                                        reads=[("mpool", 4), pmk[2]],
                                        writes=[("act", ri, ec)])
                        for mc in range(KC):
                            bD = pbm[3 + mc % 2]
                            nmm = len(grp) * 4
                            i = 0
                            for ri, r in enumerate(grp):

                                for ec in range(4):
                                    i += 1
                                    t.op("tensor", lambda te, ri=ri, ec=ec, mc=mc, bD=bD, reg=reg, i=i, nmm=nmm: te.matmul(
                                        bD[:, 0:TTB],
                                        wdn_bf[reg * 4 + ri][:, ec * D + mc * 128:ec * D + mc * 128 + 128],
                                        act_bf[:, (ri * 4 + ec) * TTB:(ri * 4 + ec + 1) * TTB],
                                        start=(i == 1), stop=(i == nmm)),
                                        reads=[("wdn", reg * 4 + ri, 2 * ec + mc // 4),
                                               ("act", ri, ec)],
                                        writes=[pmk[3 + mc % 2]], inc=(i == nmm))
                            asl = acc_bf[:, mc * T + j * TTB:mc * T + (j + 1) * TTB]
                            if gi == 0:
                                t.op("vector", lambda v, bD=bD, asl=asl: v.tensor_copy(
                                    asl, bD[:, 0:TTB]),
                                    reads=[pmk[3 + mc % 2]], writes=[("macc", mc, j)])
                            elif gi < len(EG) - 1:
                                t.op("vector", lambda v, bD=bD, asl=asl: v.scalar_tensor_tensor(
                                    asl, bD[:, 0:TTB], 0.0, asl,
                                    op0=Op.bypass, op1=Op.add),
                                    reads=[pmk[3 + mc % 2], ("macc", mc, j)],
                                    writes=[("macc", mc, j)])
                            else:
                                t.op("vector", lambda v, bD=bD, asl=asl, mc=mc: v.scalar_tensor_tensor(
                                    pool[mc % 2][:], bD[:, 0:TTB], 0.0, asl,
                                    op0=Op.bypass, op1=Op.add),
                                    reads=[pmk[3 + mc % 2], ("macc", mc, j)],
                                    writes=[("mpool", mc % 2)])
                                t.dma(lx, pool[2 + mc % 2][:], xp_v[:, mc, slj],
                                      reads=[(xp_pref, mc, j)],
                                      writes=[("mpool", 2 + mc % 2)])
                                t.op("vector", lambda v, mc=mc: v.tensor_tensor(
                                    pool[mc % 2][:], pool[mc % 2][:],
                                    pool[2 + mc % 2][:], op=Op.add),
                                    reads=[("mpool", mc % 2), ("mpool", 2 + mc % 2)],
                                    writes=[("mpool", mc % 2)])
                                t.dma(sx, xout_v[:, mc, slj], pool[mc % 2][:],
                                      reads=[("mpool", mc % 2)],
                                      writes=[(xout_pref, mc, j)])
                t.barrier(STREAMS)

        if not stop_now:
            moe_call("m1", n1_v, "n1", x1_v, "x1", x2_v, "x2")
        stop_now = stop_now or (upto == "m1")
        if not stop_now:
            ln_junction("j2", x2_v, "x2", None, None, None, None, n2_v, "n2", EPS_MOD)
        stop_now = stop_now or (upto == "j2")

        if not stop_now:
         with contextlib.ExitStack() as seg:
            ssb = lambda name, shape, dt: seg.enter_context(
                nc.sbuf_tensor(name, list(shape), dt))
            wqkvb_bf = ssb("wqkvb_bf", (128, KC * QKVB_O), BF16)
            wob_bf = ssb("wob_bf", (128, KC * D), BF16)
            identB = ssb("identB", (128, 128), BF16)
            identF2 = ssb("identF2", (128, 128), F32)
            xst = [ssb(f"bxst{i}", (128, KC * TTA), BF16) for i in range(2)]
            qt = [ssb(f"bqt{i}", (128, QKVB_O), F32) for i in range(2)]
            HH = NHEAD // 2
            ue = [ssb(f"bue{i}", (128, HH * 130), F32) for i in range(4)]
            pe = ssb("bpe", (128, HH * 130), F32)
            rsb = ssb("brs", (128, HH * 128), F32)
            prb = ssb("bpr", (128, HH * 128), F32)
            coef = ssb("bcoef", (128, 32 * 16), F32)
            lay = [ssb(f"blay{i}", (128, D), BF16) for i in range(2)]
            layfm = [ssb(f"blayfm{i}", (128, KC * TTA), BF16) for i in range(2)]
            jbB = ssb("bjb", (128, KC * TTA), F32)
            pool = [ssb(f"bpool{i}", (128, TTA), F32) for i in range(4)]
            srow = ssb("bsrow", (1, 2048), F32)
            pbb = [seg.enter_context(nc.psum_tensor(f"bps{i}", [128, 512], F32))
                   for i in range(7)]
            pbT = seg.enter_context(nc.psum_tensor("bpsT", [128, 1024], BF16))
            pbbk = [("bps", i) for i in range(7)]
            srkB = [("bsrow", i) for i in range(4)]

            cgb = ssb("cgb", (128, 16), F32)
            base_gb = CINDEX[("bg",)][0]
            t.dma(wmo, cgb[:], consts_in.ap()[:, base_gb:base_gb + 16],
                  writes=[("cgb",)])
            t.op("gpsimd", lambda g: g.memset(identB[:], 0.0), writes=[("iB",)])
            t.op("gpsimd", lambda g: g.affine_select(
                out=identB[:], in_=identB[:], compare_op=Op.not_equal,
                fill=1.0, base=0, pattern=[[-1, 128]], channel_multiplier=1),
                reads=[("iB",)], writes=[("iB",)])
            t.op("gpsimd", lambda g: g.memset(identF2[:], 0.0), writes=[("iF2",)])
            t.op("gpsimd", lambda g: g.affine_select(
                out=identF2[:], in_=identF2[:], compare_op=Op.not_equal,
                fill=1.0, base=0, pattern=[[-1, 128]], channel_multiplier=1),
                reads=[("iF2",)], writes=[("iF2",)])
            t.dma(wmo, wqkvb_bf[:], scr_qkvbc.ap()[:, :],
                  reads=[("cqkvb", kc, q) for kc in range(KC) for q in range(7)],
                  writes=[("wqkvb", kc, q) for kc in range(KC) for q in range(7)])
            t.dma(wmo, wob_bf[:], scr_wobc.ap()[:, :],
                  reads=[("cwob", kc, hf) for kc in range(KC) for hf in range(2)],
                  writes=[("wob", kc, hf) for kc in range(KC) for hf in range(2)])

            def hsl(buf, hb, off, w):
                return buf[:].rearrange("p (h o) -> p h o", h=NHEAD)[
                    :, hb * HH:(hb + 1) * HH, off:off + w]

            def uex(buf, off, w):
                return buf[:].rearrange("p (h o) -> p h o", h=HH)[:, :, off:off + w]

            CO = {}
            conames = ["sk0", "sk1", "sq0", "sq1", "ekk", "ek0q0", "ek0q1",
                       "ek1q1", "rsk0", "rsk1", "rsq0", "rsq1", "kd", "rkd",
                       "den0", "rden0", "den1", "rden1", "c00", "c10", "c11",
                       "ck", "sg0", "sg1", "b1p", "g0", "g1a", "g1b", "gt",
                       "tmp0", "tmp1", "tmp2"]
            for i, nm in enumerate(conames):
                CO[nm] = coef[:, i * 16:(i + 1) * 16]
            COK = {nm: ("coef", nm) for nm in conames}

            def co_op(dst, emit, reads, inc=True):
                t.op("vector", emit, reads=reads, writes=[COK[dst]], inc=inc)

            for p in range(2):
                s_tiles = (p, p + 2)
                for si, ti in enumerate(s_tiles):
                    sl = slice(ti * TTA, (ti + 1) * TTA)
                    t.dma(lx, xst[si][:].rearrange("p (kc c) -> p kc c", kc=KC),
                          n2_v[:, :, sl],
                          reads=[("n2", c, 2 * ti + h) for c in range(KC)
                                 for h in range(2)],
                          writes=[("bxst", si)])
                for sub in range(4):
                    for si in range(2):
                        for nb_ in range(7):
                            w = min(512, QKVB_O - nb_ * 512)
                            bQ = pbb[nb_]
                            for kc in range(KC):
                                t.op("tensor", lambda te, si=si, nb_=nb_, kc=kc, w=w, sub=sub, bQ=bQ: te.matmul(
                                    bQ[:, 0:w],
                                    xst[si][:, kc * TTA + sub * 128:kc * TTA + sub * 128 + 128],
                                    wqkvb_bf[:, kc * QKVB_O + nb_ * 512:kc * QKVB_O + nb_ * 512 + w],
                                    start=(kc == 0), stop=(kc == KC - 1)),
                                    reads=[("bxst", si), ("wqkvb", kc, nb_)],
                                    writes=[pbbk[nb_]],
                                    inc=(kc == KC - 1))
                        for nb_ in range(7):
                            w = min(512, QKVB_O - nb_ * 512)
                            bQ = pbb[nb_]
                            t.op("vector", lambda v, si=si, nb_=nb_, w=w, bQ=bQ: v.tensor_copy(
                                qt[si][:, nb_ * 512:nb_ * 512 + w], bQ[:, 0:w]),
                                reads=[pbbk[nb_]],
                                writes=[("qt", si)])
                    co_op("sg0", lambda v: v.tensor_copy(
                        CO["sg0"][:, 0:HH], hsl(qt[0], 0, 192, 1)), reads=[("qt", 0)],
                        inc=False)
                    co_op("sg0", lambda v: v.tensor_copy(
                        CO["sg0"][:, HH:NHEAD], hsl(qt[0], 1, 192, 1)),
                        reads=[("qt", 0)])
                    t.op("scalar", lambda s: s.activation(CO["sg0"], CO["sg0"], AF.Sigmoid),
                         reads=[COK["sg0"]], writes=[COK["sg0"]])
                    co_op("sg1", lambda v: v.tensor_copy(
                        CO["sg1"][:, 0:HH], hsl(qt[1], 0, 192, 1)), reads=[("qt", 1)],
                        inc=False)
                    co_op("sg1", lambda v: v.tensor_copy(
                        CO["sg1"][:, HH:NHEAD], hsl(qt[1], 1, 192, 1)),
                        reads=[("qt", 1)])
                    t.op("scalar", lambda s: s.activation(CO["sg1"], CO["sg1"], AF.Sigmoid),
                         reads=[COK["sg1"]], writes=[COK["sg1"]])
                    for hb in range(2):
                        hsl8 = slice(hb * HH, (hb + 1) * HH)
                        srcs = [(0, 0), (1, 0), (0, 64), (1, 64)]
                        for i, (bi, off) in enumerate(srcs):
                            t.op("vector", lambda v, i=i, bi=bi, off=off, hb=hb: v.tensor_scalar(
                                uex(ue[i], 2, 64), hsl(qt[bi], hb, off, 64), 0.0, None,
                                op0=Op.max),
                                reads=[("qt", bi)], writes=[("ue", i)], inc=False)
                            t.op("vector", lambda v, i=i, bi=bi, off=off, hb=hb: v.tensor_scalar(
                                uex(ue[i], 66, 64), hsl(qt[bi], hb, off, 64), -1.0, 0.0,
                                op0=Op.mult, op1=Op.max),
                                reads=[("qt", bi)], writes=[("ue", i)], inc=False)
                            t.op("vector", lambda v, i=i: v.tensor_copy(
                                uex(ue[i], 0, 2), uex(ue[i], 128, 2)),
                                reads=[("ue", i)], writes=[("ue", i)])
                        self_map = [("sq0", 0), ("sq1", 1), ("sk0", 2), ("sk1", 3)]
                        for nm, i in self_map:
                            t.op("vector", lambda v, i=i: v.tensor_tensor(
                                rsb[:].rearrange("p (h o) -> p h o", h=HH),
                                uex(ue[i], 1, 128), uex(ue[i], 0, 128), op=Op.add),
                                reads=[("ue", i)], writes=[("rsb",)])
                            t.op("vector", lambda v, i=i: v.tensor_tensor(
                                prb[:].rearrange("p (h o) -> p h o", h=HH),
                                uex(ue[i], 2, 128),
                                rsb[:].rearrange("p (h o) -> p h o", h=HH), op=Op.mult),
                                reads=[("ue", i), ("rsb",)], writes=[("prb",)])
                            t.op("vector", lambda v, nm=nm, hsl8=hsl8: v.tensor_reduce(
                                CO[nm][:, hsl8],
                                prb[:].rearrange("p (h o) -> p h o", h=HH),
                                axis=mybir.AxisListType.X, op=Op.add),
                                reads=[("prb",)], writes=[COK[nm]])
                        e_map = [("ekk", 2, 3), ("ek0q0", 2, 0), ("ek0q1", 2, 1),
                                 ("ek1q1", 3, 1)]
                        for nm, ia, ib in e_map:
                            t.op("vector", lambda v, ia=ia, ib=ib: v.tensor_tensor(
                                pe[:], ue[ia][:], ue[ib][:], op=Op.mult),
                                reads=[("ue", ia), ("ue", ib)], writes=[("pe",)])
                            t.op("vector", lambda v: v.tensor_tensor(
                                rsb[:].rearrange("p (h o) -> p h o", h=HH),
                                uex(pe, 1, 128), uex(pe, 0, 128), op=Op.add),
                                reads=[("pe",)], writes=[("rsb",)])
                            t.op("vector", lambda v: v.tensor_tensor(
                                prb[:].rearrange("p (h o) -> p h o", h=HH),
                                uex(pe, 2, 128),
                                rsb[:].rearrange("p (h o) -> p h o", h=HH), op=Op.mult),
                                reads=[("pe",), ("rsb",)], writes=[("prb",)])
                            t.op("vector", lambda v, nm=nm, hsl8=hsl8: v.tensor_reduce(
                                CO[nm][:, hsl8],
                                prb[:].rearrange("p (h o) -> p h o", h=HH),
                                axis=mybir.AxisListType.X, op=Op.add),
                                reads=[("prb",)], writes=[COK[nm]])
                    for nm in ("sk0", "sk1", "sq0", "sq1"):
                        co_op("r" + nm, lambda v, nm=nm: v.reciprocal(
                            CO["r" + nm], CO[nm]), reads=[COK[nm]])
                    co_op("tmp0", lambda v: v.tensor_tensor(
                        CO["tmp0"], CO["rsk0"], CO["rsk1"], op=Op.mult),
                        reads=[COK["rsk0"], COK["rsk1"]])
                    co_op("kd", lambda v: v.tensor_tensor(
                        CO["kd"], CO["ekk"], CO["tmp0"], op=Op.mult),
                        reads=[COK["ekk"], COK["tmp0"]])
                    co_op("tmp1", lambda v: v.tensor_scalar(
                        CO["tmp1"], CO["kd"], EPS_B, None, op0=Op.add),
                        reads=[COK["kd"]])
                    co_op("rkd", lambda v: v.reciprocal(CO["rkd"], CO["tmp1"]),
                          reads=[COK["tmp1"]])
                    co_op("tmp0", lambda v: v.tensor_tensor(
                        CO["tmp0"], CO["rsk0"], CO["rsq0"], op=Op.mult),
                        reads=[COK["rsk0"], COK["rsq0"]])
                    co_op("den0", lambda v: v.tensor_tensor(
                        CO["den0"], CO["ek0q0"], CO["tmp0"], op=Op.mult),
                        reads=[COK["ek0q0"], COK["tmp0"]])
                    co_op("c00", lambda v: v.tensor_scalar(
                        CO["c00"], CO["den0"], REC1, None, op0=Op.mult),
                        reads=[COK["den0"]])
                    co_op("tmp0", lambda v: v.tensor_scalar(
                        CO["tmp0"], CO["den0"], EPS_B, None, op0=Op.add),
                        reads=[COK["den0"]])
                    co_op("rden0", lambda v: v.reciprocal(CO["rden0"], CO["tmp0"]),
                          reads=[COK["tmp0"]])
                    co_op("tmp0", lambda v: v.tensor_tensor(
                        CO["tmp0"], CO["rsk0"], CO["rsq1"], op=Op.mult),
                        reads=[COK["rsk0"], COK["rsq1"]])
                    co_op("tmp0", lambda v: v.tensor_tensor(
                        CO["tmp0"], CO["ek0q1"], CO["tmp0"], op=Op.mult),
                        reads=[COK["ek0q1"], COK["tmp0"]])
                    co_op("tmp1", lambda v: v.tensor_tensor(
                        CO["tmp1"], CO["rsk1"], CO["rsq1"], op=Op.mult),
                        reads=[COK["rsk1"], COK["rsq1"]])
                    co_op("tmp1", lambda v: v.tensor_tensor(
                        CO["tmp1"], CO["ek1q1"], CO["tmp1"], op=Op.mult),
                        reads=[COK["ek1q1"], COK["tmp1"]])
                    co_op("den1", lambda v: v.tensor_tensor(
                        CO["den1"], CO["tmp0"], CO["tmp1"], op=Op.add),
                        reads=[COK["tmp0"], COK["tmp1"]])
                    co_op("c10", lambda v: v.tensor_scalar(
                        CO["c10"], CO["tmp0"], REC1, None, op0=Op.mult),
                        reads=[COK["tmp0"]])
                    co_op("c11", lambda v: v.tensor_tensor(
                        CO["c11"], CO["tmp1"], CO["rkd"], op=Op.mult),
                        reads=[COK["tmp1"], COK["rkd"]])
                    co_op("tmp2", lambda v: v.tensor_scalar(
                        CO["tmp2"], CO["den1"], EPS_B, None, op0=Op.add),
                        reads=[COK["den1"]])
                    co_op("rden1", lambda v: v.reciprocal(CO["rden1"], CO["tmp2"]),
                          reads=[COK["tmp2"]])
                    co_op("ck", lambda v: v.tensor_tensor(
                        CO["ck"], CO["kd"], CO["rkd"], op=Op.mult),
                        reads=[COK["kd"], COK["rkd"]])
                    co_op("ck", lambda v: v.tensor_scalar(
                        CO["ck"], CO["ck"], REC1, None, op0=Op.mult),
                        reads=[COK["ck"]])
                    co_op("b1p", lambda v: v.tensor_tensor(
                        CO["b1p"], CO["sg1"], CO["kd"], op=Op.mult),
                        reads=[COK["sg1"], COK["kd"]])
                    co_op("g0", lambda v: v.tensor_tensor(
                        CO["g0"], CO["c00"], CO["sg0"], op=Op.mult),
                        reads=[COK["c00"], COK["sg0"]])
                    co_op("g0", lambda v: v.tensor_tensor(
                        CO["g0"], CO["g0"], CO["rden0"], op=Op.mult),
                        reads=[COK["g0"], COK["rden0"]])
                    co_op("g0", lambda v: v.tensor_scalar(
                        CO["g0"], CO["g0"], SCALE_B, None, op0=Op.mult),
                        reads=[COK["g0"]])
                    co_op("g1a", lambda v: v.tensor_tensor(
                        CO["g1a"], CO["c10"], CO["sg0"], op=Op.mult),
                        reads=[COK["c10"], COK["sg0"]])
                    co_op("g1a", lambda v: v.tensor_tensor(
                        CO["g1a"], CO["g1a"], CO["rden1"], op=Op.mult),
                        reads=[COK["g1a"], COK["rden1"]])
                    co_op("g1a", lambda v: v.tensor_scalar(
                        CO["g1a"], CO["g1a"], SCALE_B, None, op0=Op.mult),
                        reads=[COK["g1a"]])
                    co_op("g1b", lambda v: v.tensor_tensor(
                        CO["g1b"], CO["c11"], CO["b1p"], op=Op.mult),
                        reads=[COK["c11"], COK["b1p"]])
                    co_op("g1b", lambda v: v.tensor_tensor(
                        CO["g1b"], CO["g1b"], CO["rden1"], op=Op.mult),
                        reads=[COK["g1b"], COK["rden1"]])
                    co_op("g1b", lambda v: v.tensor_scalar(
                        CO["g1b"], CO["g1b"], SCALE_B, None, op0=Op.mult),
                        reads=[COK["g1b"]])
                    co_op("gt", lambda v: v.tensor_tensor(
                        CO["gt"], CO["ck"], CO["sg0"], op=Op.mult),
                        reads=[COK["ck"], COK["sg0"]])
                    co_op("gt", lambda v: v.tensor_scalar(
                        CO["gt"], CO["gt"], -1.0, None, op0=Op.mult),
                        reads=[COK["gt"]])
                    for hh in range(NHEAD):
                        v0 = qt[0][:, hh * 193 + 128:hh * 193 + 192]
                        v1 = qt[1][:, hh * 193 + 128:hh * 193 + 192]
                        t.op("vector", lambda v, hh=hh, v0=v0: v.tensor_scalar(
                            lay[0][:, hh * 64:(hh + 1) * 64], v0,
                            CO["g0"][:, hh:hh + 1], None, op0=Op.mult),
                            reads=[("qt", 0), COK["g0"]], writes=[("lay", 0)],
                            inc=(hh == NHEAD - 1))
                        t.op("vector", lambda v, hh=hh, v0=v0, v1=v1: v.scalar_tensor_tensor(
                            pool[0][:, 0:64], v0, CO["gt"][:, hh:hh + 1], v1,
                            op0=Op.mult, op1=Op.add),
                            reads=[("qt", 0), ("qt", 1), COK["gt"]],
                            writes=[("bpool", 0)])
                        t.op("vector", lambda v, hh=hh, v0=v0: v.tensor_scalar(
                            pool[1][:, 0:64], v0, CO["g1a"][:, hh:hh + 1], None,
                            op0=Op.mult),
                            reads=[("qt", 0), COK["g1a"]], writes=[("bpool", 1)])
                        t.op("vector", lambda v, hh=hh: v.scalar_tensor_tensor(
                            lay[1][:, hh * 64:(hh + 1) * 64], pool[0][:, 0:64],
                            CO["g1b"][:, hh:hh + 1], pool[1][:, 0:64],
                            op0=Op.mult, op1=Op.add),
                            reads=[("bpool", 0), ("bpool", 1), COK["g1b"]],
                            writes=[("lay", 1)], inc=(hh == NHEAD - 1))
                    for si in range(2):
                        for ch in range(KC):
                            t.op("tensor", lambda te, si=si, ch=ch: te.transpose(
                                pbT[:, 0:128], lay[si][:, ch * 128:(ch + 1) * 128],
                                identB[:]),
                                reads=[("lay", si), ("iB",)], writes=[("bpsT",)])
                            t.op("vector", lambda v, si=si, ch=ch, sub=sub: v.tensor_copy(
                                layfm[si][:, ch * TTA + sub * 128:ch * TTA + sub * 128 + 128],
                                pbT[:, 0:128]),
                                reads=[("bpsT",)], writes=[("layfm", si, ch, sub)])
                for si, ti in enumerate(s_tiles):
                    sl = slice(ti * TTA, (ti + 1) * TTA)
                    for mc in range(KC):
                        bO = pbb[mc % 2]
                        for kc in range(KC):
                            t.op("tensor", lambda te, si=si, mc=mc, kc=kc, bO=bO: te.matmul(
                                bO[:], wob_bf[:, kc * D + mc * 128:kc * D + mc * 128 + 128],
                                layfm[si][:, kc * TTA:(kc + 1) * TTA],
                                start=(kc == 0), stop=(kc == KC - 1)),
                                reads=[("wob", kc, mc // 4)] + [("layfm", si, kc, su)
                                                                for su in range(4)],
                                writes=[pbbk[mc % 2]], inc=(kc == KC - 1))
                        t.op("vector", lambda v, si=si, mc=mc, bO=bO: v.scalar_tensor_tensor(
                            jbB[:, mc * TTA:(mc + 1) * TTA], bO[:], 0.0,
                            xst[si][:, mc * TTA:(mc + 1) * TTA],
                            op0=Op.bypass, op1=Op.add),
                            reads=[pbbk[mc % 2], ("bxst", si)], writes=[("bjb", mc)])
                    y_aps = [jbB[:, c * TTA:(c + 1) * TTA] for c in range(KC)]
                    ykeys = [("bjb", c) for c in range(KC)]
                    ln_stats(y_aps, ykeys, TTA, LNEPS_B, pbb[2:6], pbbk[2:6],
                             srow, srkB, pool[3][:], ("bpool", 3))
                    for c in range(KC):
                        t.op("vector", lambda v, c=c: v.scalar_tensor_tensor(
                            pool[0][:], y_aps[c], 0.0, pbb[4][:, 0:TTA],
                            op0=Op.bypass, op1=Op.subtract),
                            reads=[ykeys[c], pbbk[4]], writes=[("bpool", 0)])
                        t.op("vector", lambda v, c=c: v.scalar_tensor_tensor(
                            pool[1][:], pool[0][:], cgb[:, c:c + 1],
                            pbb[5][:, 0:TTA], op0=Op.mult, op1=Op.mult),
                            reads=[("bpool", 0), pbbk[5], ("cgb",)],
                            writes=[("bpool", 1)])
                        t.op("vector", lambda v, c=c: v.tensor_scalar(
                            pool[2][:], pool[1][:], cgb[:, 8 + c:8 + c + 1], None,
                            op0=Op.add),
                            reads=[("bpool", 1), ("cgb",)], writes=[("bpool", 2)])
                        t.dma(lx, pool[0][:], x2_v[:, c, sl],
                              reads=[("x2", c, 2 * ti), ("x2", c, 2 * ti + 1)],
                              writes=[("bpool", 0)])
                        t.op("vector", lambda v, c=c: v.tensor_tensor(
                            pool[2][:], pool[2][:], pool[0][:], op=Op.add),
                            reads=[("bpool", 2), ("bpool", 0)], writes=[("bpool", 2)])
                        t.dma(sx, x3_v[:, c, sl], pool[2][:],
                              reads=[("bpool", 2)],
                              writes=[("x3", c, 2 * ti), ("x3", c, 2 * ti + 1)])
            t.barrier(STREAMS)

        stop_now = stop_now or (upto == "ab")
        if not stop_now:
            ln_junction("j3", x3_v, "x3", None, None, None, None, n3_v, "n3", EPS_MOD)
        stop_now = stop_now or (upto == "j3")
        if not stop_now:
            moe_call("m2", n3_v, "n3", x3_v, "x3", out_v, "outk")

        t.barrier(STREAMS)

    es.close()
    nc.finalize()
    return nc



def _prep(inputs):
    f32 = np.float32
    g = lambda k: np.ascontiguousarray(np.asarray(inputs[k], dtype=f32))
    Wq, Wk, Wv = g("L0_attn_a_Wq"), g("L0_attn_a_Wk"), g("L0_attn_a_Wv")

    def cb(W):
        return np.ascontiguousarray(
            W.reshape(NB, D, KC, 128).transpose(0, 2, 1, 3))

    cp = ConstPack()
    for i in range(NB):
        cp.add(("bq", i), inputs["L0_attn_a_bq"][i])
        cp.add(("bk", i), inputs["L0_attn_a_bk"][i])
        cp.add(("bv", i), inputs["L0_attn_a_bv"][i])
        cp.add(("bo", i), inputs["L0_attn_a_bo"][i])
        cp.add(("bff2", i), inputs["L0_attn_a_b2"][i])
        cp.add(("ln1s", i), inputs["L0_attn_a_ln1_s"][i])
        cp.add(("ln1b", i), inputs["L0_attn_a_ln1_b"][i])
        cp.add(("ln2s", i), inputs["L0_attn_a_ln2_s"][i])
        cp.add(("ln2b", i), inputs["L0_attn_a_ln2_b"][i])
        cp.add(("bff1", i), inputs["L0_attn_a_b1"][i])
    cp.add(("bg",), inputs["L0_attn_b_ln_g"])
    cp.add(("bb",), inputs["L0_attn_b_ln_b"])
    consts = cp.array()
    assert cp.index == CINDEX, "const layout mismatch"
    assert consts.shape[1] == NCONST

    up_all = np.concatenate([
        np.asarray(inputs["L0_ffn_routed_up"], dtype=f32),
        np.asarray(inputs["L0_ffn_shared_up"], dtype=f32)], axis=0)
    down_all = np.concatenate([
        np.asarray(inputs["L0_ffn_routed_down"], dtype=f32),
        np.asarray(inputs["L0_ffn_shared_down"], dtype=f32)], axis=0)

    shared = {
        "wq_cb": cb(Wq), "wk_cb": cb(Wk), "wv_cb": cb(Wv),
        "wo_w": g("L0_attn_a_Wo"), "w1_w": g("L0_attn_a_W1"),
        "w2_w": g("L0_attn_a_W2"),
        "router_t": np.ascontiguousarray(g("L0_ffn_router_w").T),
        "up_t": np.ascontiguousarray(up_all.transpose(0, 2, 1)),
        "down_t": np.ascontiguousarray(down_all.transpose(0, 2, 1)),
        "wqkvb_t": np.ascontiguousarray(g("L0_attn_b_W_qkvb").T),
        "wob_t": np.ascontiguousarray(g("L0_attn_b_W_o").T),
        "consts": consts,
    }
    x = np.asarray(inputs["L0_attn_a_x"], dtype=f32)
    xs = []
    for c in range(NCORE):
        xc = x[:, c * 1024:(c + 1) * 1024, :].reshape(T, D)
        xs.append(np.ascontiguousarray(xc.T))
    return shared, xs


def kernel(**inputs):
    shared, xs = _prep(inputs)
    nc = build_program()
    in_maps = [dict(shared, x0=xs[c]) for c in range(NCORE)]
    res = run_bass_kernel_spmd(nc, in_maps, core_ids=list(range(NCORE)))
    out = np.empty((2, 8192, D), np.float32)
    for c in range(NCORE):
        oc = np.asarray(res.results[c]["out"])
        out[:, c * 1024:(c + 1) * 1024, :] = oc.T.reshape(2, 1024, D)
    return out
